# version 01
# baseline (speedup 1.0000x reference)
import numpy as np
import concourse.bass as bass
import concourse.mybir as mybir
from concourse.bass_utils import run_bass_kernel_spmd
from concourse.bass_core_rust import IndirectOffsetOnAxis
from concourse.library_overlay import lower_extended_insts

F32 = mybir.dt.float32
BF16 = mybir.dt.bfloat16
U32 = mybir.dt.uint32
AF = mybir.ActivationFunctionType
ALU = mybir.AluOpType
AX = mybir.AxisListType.X

B, S, DM = 4, 4096, 2048
H, KVH, D = 16, 4, 128
SKN, TK = 640, 2048
NH, NKV = 8, 2
TC = S // 128
NF = DM // 128
NQ = NH * D
NKVD = NKV * D
CT = TK + SKN
NCT = CT // 128
NTK = TK // 128
HTOK = S // 2
HTC = HTOK // 128
LOGSHIFT = float(np.log(S / SKN))
ISQD = float(1.0 / np.sqrt(D))
EPS1, EPS2 = 1e-5, 1e-6
BISECT = 34
BIGPOS = 60000.0

_SZ = {F32: 4, BF16: 2, U32: 4}


class Arena:
    def __init__(self, nc):
        self.nc = nc
        self.off = (nc.sbuf_base + 63) // 64 * 64
        self.top = nc.sbuf_top
        self.marks = {}

    def alloc(self, name, shape, dtype, align=64):
        nbytes = int(np.prod(shape[1:])) * _SZ[dtype]
        self.off = (self.off + align - 1) // align * align
        t = self.nc.alloc_sbuf_tensor_at(name, list(shape), dtype, offset=self.off)
        self.off += nbytes
        assert self.off <= self.top, f"SBUF overflow at {name}: {self.off}"
        return t.ap()

    def mark(self, name):
        self.marks[name] = self.off

    def reset_to(self, name):
        self.off = self.marks[name]


def build_program(dbg=(), phases=99):
    dbg = set(dbg)
    nc = bass.Bass("TRN2", target_bir_lowering=False)

    def din(name, shape, dtype=F32):
        return nc.dram_tensor(name, list(shape), dtype, kind="ExternalInput").ap()

    xT = din("xT", [DM, S])
    xTh = din("xTh", [DM, HTOK])
    cos_tm = din("cos_tm", [S, D])
    sin_tm = din("sin_tm", [S, D])
    wq_c = din("wq_c", [DM, NQ])
    wk_c = din("wk_c", [DM, NKVD])
    wv_c = din("wv_c", [DM, NKVD])
    imp_w1 = din("imp_w1", [DM, 64])
    imp_w2r = din("imp_w2r", [1, 64])
    imp_b1r = din("imp_b1r", [1, 64])
    imp_b2 = din("imp_b2", [1, 1])
    qnw_col = din("qnw_col", [128, 1])
    knw_col = din("knw_col", [128, 1])
    qnw_row = din("qnw_row", [1, 128])
    knw_row = din("knw_row", [1, 128])
    kbT = din("kbT", [256, 32])
    kb_n = din("kb_n", [32, 256])
    ka_n = din("ka_n", [20, 128])
    kaT_row = din("kaT_row", [1, 320])
    sela = din("sela", [32, 5 * 128])
    selc = din("selc", [20, 5 * 128])
    wo_c = din("wo_c", [NQ, DM])
    wbT = din("wbT", [DM, DM])
    b_base_cm = din("b_base_cm", [128, NF])
    rwT = din("rwT", [DM, 8])
    aT = din("aT", [DM, 128])
    bw_er = din("bw_er", [128, DM])
    esel = din("esel", [8, 128])
    ident = din("ident", [128, 128])
    iota_cm = din("iota_cm", [128, TC], U32)
    ss_in = din("ss_in", [1, 1])

    out_fm = nc.dram_tensor("out_fm", [DM, HTOK], F32, kind="ExternalOutput").ap()
    qd_dram = nc.dram_tensor("qd_dram", [TK, NQ], BF16).ap()
    kd_dram = nc.dram_tensor("kd_dram", [TK, NKVD], BF16).ap()
    vd_dram = nc.dram_tensor("vd_dram", [TK, NKVD], BF16).ap()
    cosd_dram = nc.dram_tensor("cosd_dram", [TK, D], BF16).ap()
    sind_dram = nc.dram_tensor("sind_dram", [TK, D], BF16).ap()
    outdet_dram = nc.dram_tensor("outdet_dram", [TK, NQ], BF16).ap()
    full_dram = nc.dram_tensor("full_dram", [S, NQ], BF16).ap()
    ccin = nc.dram_tensor("ccin", [DM, S], BF16).ap()
    ccout = nc.dram_tensor("ccout", [DM, S], BF16).ap()

    dbg_outs = {}

    def dout(name, shape, dtype=F32):
        ap = nc.dram_tensor("dbg_" + name, list(shape), dtype, kind="ExternalOutput").ap()
        dbg_outs[name] = "dbg_" + name
        return ap

    ar = Arena(nc)
    ident_f = ar.alloc("ident_f", [128, 128], F32)
    ident_b = ar.alloc("ident_b", [128, 128], BF16)
    ones_col_b = ar.alloc("ones_col_b", [128, 1], BF16)
    ones_col_f = ar.alloc("ones_col_f", [128, 1], F32)
    ones_row_b = ar.alloc("ones_row_b", [1, 128], BF16)
    ones_row_f = ar.alloc("ones_row_f", [1, 128], F32)
    small_f = ar.alloc("small_f", [1, 128], F32)
    w2bc = ar.alloc("w2bc", [128, 64], F32)
    b2bias = ar.alloc("b2bias", [128, 1], F32)
    b1row_b = ar.alloc("b1row_b", [1, 64], BF16)
    qw_col = ar.alloc("qw_col", [128, 1], F32)
    kw_col = ar.alloc("kw_col", [128, 1], F32)
    qw_rows = ar.alloc("qw_rows", [1, 256], F32)
    kw_rows = ar.alloc("kw_rows", [1, 256], F32)
    ssbc = ar.alloc("ssbc", [128, 1], F32)
    eps2_col = ar.alloc("eps2_col", [128, 1], F32)
    eps1_row = ar.alloc("eps1_row", [1, 1], F32)
    qwss = ar.alloc("qwss", [128, 1], F32)
    kwss = ar.alloc("kwss", [128, 1], F32)
    kabc = ar.alloc("kabc", [128, 320], BF16)
    kbT_sb = ar.alloc("kbT_sb", [128, 64], BF16)
    logits_cm = ar.alloc("logits_cm", [128, TC], F32)
    weights_cm = ar.alloc("weights_cm", [128, TC], F32)
    restw_cm = ar.alloc("restw_cm", [128, TC], F32)
    sel_cm = ar.alloc("sel_cm", [128, TC], F32)
    pos_cm = ar.alloc("pos_cm", [128, TC], U32)
    selu_cm = ar.alloc("selu_cm", [128, TC], U32)
    ar.mark("persist")

    r1b = ar.alloc("r1b", [1, S], BF16)
    nmr1b = ar.alloc("nmr1b", [1, S], BF16)
    wbar = ar.alloc("wbar", [1, 1600], BF16)
    wrhs = [ar.alloc(f"wrhs{f}", [128, 1600], BF16) for f in range(NF)]
    rmsq = [ar.alloc(f"rmsq{i}", [128, 8], F32) for i in range(2)]
    rmsk = [ar.alloc(f"rmsk{i}", [128, 8], F32) for i in range(2)]
    tanh_sb = [ar.alloc(f"tanh{i}", [128, 64], F32) for i in range(2)]
    scr_act = ar.alloc("scr_act", [128, 20 * 128], BF16)
    scr_ttr = ar.alloc("scr_ttr", [128, 2 * 64], F32)
    ar.mark("p1ab")
    vecmu = ar.alloc("vecmu", [1, 1024], F32)
    vec1 = ar.alloc("vec1", [1, 1024], F32)
    vec2 = ar.alloc("vec2", [1, 1024], F32)
    xstat = [ar.alloc(f"xstat{i}", [128, 1024], BF16) for i in range(2)]
    sq_b = [ar.alloc(f"sq_b{i}", [128, 1024], BF16) for i in range(2)]
    ar.reset_to("p1ab")
    xw = [ar.alloc(f"xw{f}", [128, 512], BF16) for f in range(NF)]
    rbc_b = ar.alloc("rbc_b", [128, 512], BF16)
    ar.mark("p1end")
    qhat = [ar.alloc(f"qhat{t}", [128, NQ], BF16) for t in range(TC)]
    khat = [ar.alloc(f"khat{t}", [128, NKVD], BF16) for t in range(TC)]
    vhat = [ar.alloc(f"vhat{t}", [128, NKVD], BF16) for t in range(TC)]
    ar.mark("qkv_end")

    pbank = [nc.alloc_psum_tensor(f"pb{i}", [128, 512], F32).ap() for i in range(8)]

    sem_cnt, sem_h = {}, {}

    def sem(name):
        if name not in sem_h:
            sem_h[name] = nc.alloc_semaphore(name)
            sem_cnt[name] = 0
        return sem_h[name]

    def inc(instr, name, k=1):
        instr.then_inc(sem(name), k)
        sem_cnt[name] += k
        return instr

    def cnt(name):
        sem(name)
        return sem_cnt[name]

    def wchain(e, name):
        e.wait_ge(sem(name), cnt(name))



    def fence(eng, instr, tag):
        inc(instr, tag)
        eng.wait_ge(sem(tag), cnt(tag))

    with nc.Block() as block:
        @block.gpsimd
        def _(g):
            inc(g.dma_start(out=ident_b[:], in_=ident[:]), "ld_g", 16)
            inc(g.dma_start(out=kbT_sb[:, 0:32], in_=kbT[0:128, :]), "ld_g", 16)
            inc(g.dma_start(out=kbT_sb[:, 32:64], in_=kbT[128:256, :]), "ld_g", 16)
            inc(g.dma_start(out=b1row_b[:], in_=imp_b1r[:]), "ld_g", 16)
            for f in range(NF):
                inc(g.dma_start(out=wrhs[f][:, 0:1024], in_=wq_c[f * 128:(f + 1) * 128, :]), "ld_g", 16)
                inc(g.dma_start(out=wrhs[f][:, 1024:1280], in_=wk_c[f * 128:(f + 1) * 128, :]), "ld_g", 16)
                inc(g.dma_start(out=wrhs[f][:, 1280:1536], in_=wv_c[f * 128:(f + 1) * 128, :]), "ld_g", 16)
                inc(g.dma_start(out=wrhs[f][:, 1536:1600], in_=imp_w1[f * 128:(f + 1) * 128, :]), "ld_g", 16)
        ld_g_p0 = cnt("ld_g")

        @block.sync
        def _(sy):
            inc(sy.dma_start(out=ident_f[:], in_=ident[:]), "ld_s", 16)
            inc(sy.dma_start(out=qw_col[:], in_=qnw_col[:]), "ld_s", 16)
            inc(sy.dma_start(out=kw_col[:], in_=knw_col[:]), "ld_s", 16)
            inc(sy.dma_start(out=qw_rows[0:1, 0:128], in_=qnw_row[:]), "ld_s", 16)
            inc(sy.dma_start(out=kw_rows[0:1, 0:128], in_=knw_row[:]), "ld_s", 16)
            inc(sy.dma_start(out=small_f[0:1, 0:64], in_=imp_w2r[:]), "ld_s", 16)
            inc(sy.dma_start(out=small_f[0:1, 64:65], in_=imp_b2[:]), "ld_s", 16)
            inc(sy.dma_start(out=small_f[0:1, 65:66], in_=ss_in[:]), "ld_s", 16)
        ld_s_p0 = cnt("ld_s")

        @block.vector
        def _(v):
            v.memset(eps2_col[:], EPS2)
            v.memset(eps1_row[:], EPS1)
            v.memset(ones_col_b[:], 1.0)
            v.memset(ones_col_f[:], 1.0)
            v.memset(ones_row_b[:], 1.0)
            v.memset(ones_row_f[:], 1.0)
            v.wait_ge(sem("ld_s"), ld_s_p0)
            v.tensor_copy(qw_rows[0:1, 128:192], qw_rows[0:1, 64:128])
            v.tensor_copy(qw_rows[0:1, 192:256], qw_rows[0:1, 0:64])
            v.tensor_copy(kw_rows[0:1, 128:192], kw_rows[0:1, 64:128])
            inc(v.tensor_copy(kw_rows[0:1, 192:256], kw_rows[0:1, 0:64]), "c0v")
        c0v1 = cnt("c0v")

        @block.tensor
        def _(t):
            t.wait_ge(sem("c0v"), c0v1)
            t.matmul(pbank[5][:, 0:64], ones_row_f[:], small_f[0:1, 0:64], start=True, stop=True)
            inc(t.matmul(pbank[5][:, 64:66], ones_row_f[:], small_f[0:1, 64:66], start=True, stop=True),
                "c0t")
        c0t1 = cnt("c0t")

        @block.vector
        def _(v):
            v.wait_ge(sem("c0t"), c0t1)
            v.tensor_copy(w2bc[:], pbank[5][:, 0:64])
            v.tensor_scalar(b2bias[:], pbank[5][:, 64:65], -LOGSHIFT, None, op0=ALU.add)
            fence(v, v.tensor_copy(ssbc[:], pbank[5][:, 65:66]), "fl_v")
            v.tensor_mul(qwss[:], qw_col[:], ssbc[:])
            inc(v.tensor_mul(kwss[:], kw_col[:], ssbc[:]), "c0v")
        c0v2 = cnt("c0v")

        psum_sum = [pbank[i][0:1, :] for i in range(2)]
        psum_sq = [pbank[2 + i][0:1, :] for i in range(2)]

        @block.gpsimd
        def _(g):
            for k in range(4 * NF):
                qtr, f = divmod(k, NF)
                i = k % 2
                if k >= 2:
                    g.wait_ge(sem("stat_used"), k - 1)
                inc(g.dma_start(out=xstat[i][:],
                                in_=xT[f * 128:(f + 1) * 128, qtr * 1024:(qtr + 1) * 1024]),
                    f"ld_x{i}", 16)

        @block.scalar
        def _(s):
            for k in range(4 * NF):
                i = k % 2
                s.wait_ge(sem(f"ld_x{i}"), (k // 2 + 1) * 16)
                if k >= 2:
                    s.wait_ge(sem("stat_used"), k - 1)
                inc(s.activation(sq_b[i][:], xstat[i][:], AF.Square), "sq1")
                if (k + 1) % NF == 0:
                    qtr = (k + 1) // NF - 1
                    s.wait_ge(sem("stats_evac"), qtr + 1)
                    fence(s, s.activation(vec2[:], vec2[:], AF.Ln, bias=eps2_col[0:1, 0:1] if False else eps1_row[:]), "fl_s")
                    inc(s.activation(vec2[:], vec2[:], AF.Exp, scale=-0.5), "rstd1")

        @block.tensor
        def _(t):
            t.wait_ge(sem("c0v"), c0v2)
            for k in range(4 * NF):
                qtr, f = divmod(k, NF)
                i = k % 2
                t.wait_ge(sem("sq1"), k + 1)
                t.wait_ge(sem(f"ld_x{i}"), (k // 2 + 1) * 16)
                if f == 0 and qtr >= 1:
                    t.wait_ge(sem("stats_evac"), qtr)
                fi, la = f == 0, f == NF - 1
                for w in range(2):
                    t.matmul(psum_sum[w], ones_col_b[:], xstat[i][:, w * 512:(w + 1) * 512],
                             start=fi, stop=la)
                    mmi = t.matmul(psum_sq[w], ones_col_b[:], sq_b[i][:, w * 512:(w + 1) * 512],
                                   start=fi, stop=la)
                inc(mmi, "stat_used")

        @block.vector
        def _(v):
            for qtr in range(4):
                v.wait_ge(sem("stat_used"), NF * (qtr + 1))
                sl = slice(qtr * 1024, (qtr + 1) * 1024)
                for w in range(2):
                    v.tensor_scalar(vecmu[0:1, w * 512:(w + 1) * 512],
                                    psum_sum[w], 1.0 / DM, None, op0=ALU.mult)
                    ii = v.tensor_scalar(vec1[0:1, w * 512:(w + 1) * 512],
                                         psum_sq[w], 1.0 / DM, None, op0=ALU.mult)
                fence(v, ii, "fl_v")
                fence(v, v.tensor_mul(vec2[:], vecmu[:], vecmu[:]), "fl_v")
                inc(v.scalar_tensor_tensor(vec2[:], vec2[:], -1.0, vec1[:],
                                            op0=ALU.mult, op1=ALU.add), "stats_evac")
                v.wait_ge(sem("rstd1"), qtr + 1)
                inc(v.scalar_tensor_tensor(nmr1b[0:1, sl], vecmu[:], -1.0, vec2[:],
                                           op0=ALU.mult, op1=ALU.mult), "ln1_rdy")
                fence(v, v.tensor_copy(r1b[0:1, sl], vec2[:]), "fl_v")

        fl_v_after_stats = cnt("fl_v")

        @block.tensor
        def _(t):
            t.wait_ge(sem("ld_g"), ld_g_p0)
            t.wait_ge(sem("stats_evac"), 4)
            for f in range(NF):
                fi, la = f == 0, f == NF - 1
                for w in range(3):
                    t.matmul(pbank[w][0:1, 0:512], ones_col_b[:], wrhs[f][:, w * 512:(w + 1) * 512],
                             start=fi, stop=la)
                mmi = t.matmul(pbank[3][0:1, 0:64], ones_col_b[:], wrhs[f][:, 1536:1600],
                               start=fi, stop=la)
            inc(mmi, "tks")

        @block.vector
        def _(v):
            wchain(v, "tks")
            for w in range(3):
                v.tensor_copy(wbar[0:1, w * 512:(w + 1) * 512], pbank[w][0:1, 0:512])
            inc(v.tensor_copy(wbar[0:1, 1536:1600], pbank[3][0:1, 0:64]), "tks")

        for wdw in range(8):
            tok0 = wdw * 512

            @block.gpsimd
            def _(g):
                if wdw == 0:
                    g.wait_ge(sem("stat_used"), 4 * NF)
                    g.wait_ge(sem("fl_v"), fl_v_after_stats)
                else:
                    g.wait_ge(sem("proj_mm"), 4 * wdw)
                for f in range(NF):
                    inc(g.dma_start(out=xw[f][:],
                                    in_=xT[f * 128:(f + 1) * 128, tok0:tok0 + 512]),
                        "ld_xw", 16)
            ld_xw_w = cnt("ld_xw")

            @block.tensor
            def _(t):
                if wdw == 0:
                    wchain(t, "tks")
                else:
                    t.wait_ge(sem("proj_evac"), 4 * wdw - 1)
                t.wait_ge(sem("ln1_rdy"), wdw // 2 + 1)
                inc(t.matmul(pbank[0][:, 0:512], ones_row_b[:],
                             r1b[0:1, tok0:tok0 + 512], start=True, stop=True), "rbc_mm")

            @block.vector
            def _(v):
                v.wait_ge(sem("rbc_mm"), wdw + 1)
                fence(v, v.tensor_copy(rbc_b[:], pbank[0][:, 0:512]), "fl_v")
                v.wait_ge(sem("ld_xw"), ld_xw_w)
                for f in range(NF):
                    ii = v.tensor_mul(xw[f][:], xw[f][:], rbc_b[:])
                inc(ii, "xr_rdy")
            xr_w = cnt("xr_rdy")

            @block.tensor
            def _(t):
                t.wait_ge(sem("xr_rdy"), xr_w)
                for tci in range(4):
                    tc = wdw * 4 + tci
                    c = slice(tci * 128, (tci + 1) * 128)
                    par = tc % 2
                    qp0, qp1, kvp = pbank[4 * par], pbank[4 * par + 1], pbank[4 * par + 2]
                    imp = pbank[4 * par + 3][:, 0:64]
                    if tc >= 2:
                        t.wait_ge(sem("proj_evac"), tc - 1)
                    for f in range(NF):
                        fi = f == 0
                        t.matmul(qp0[:, 0:512], xw[f][:, c], wrhs[f][:, 0:512], start=fi, stop=False)
                        t.matmul(qp1[:, 0:512], xw[f][:, c], wrhs[f][:, 512:1024], start=fi, stop=False)
                        t.matmul(kvp[:, 0:512], xw[f][:, c], wrhs[f][:, 1024:1536], start=fi, stop=False)
                        t.matmul(imp, xw[f][:, c], wrhs[f][:, 1536:1600], start=fi, stop=False)
                    nm = nmr1b[0:1, tok0 + tci * 128:tok0 + (tci + 1) * 128]
                    t.matmul(qp0[:, 0:512], nm, wbar[0:1, 0:512], start=False, stop=True)
                    t.matmul(qp1[:, 0:512], nm, wbar[0:1, 512:1024], start=False, stop=True)
                    t.matmul(kvp[:, 0:512], nm, wbar[0:1, 1024:1536], start=False, stop=True)
                    t.matmul(imp, nm, wbar[0:1, 1536:1600], start=False, stop=False)
                    inc(t.matmul(imp, ones_row_b[:], b1row_b[:], start=False, stop=True), "proj_mm")

            @block.scalar
            def _(s):
                for tci in range(4):
                    tc = wdw * 4 + tci
                    par = tc % 2
                    qp0, qp1, kvp = pbank[4 * par], pbank[4 * par + 1], pbank[4 * par + 2]
                    imp = pbank[4 * par + 3][:, 0:64]
                    s.wait_ge(sem("proj_mm"), tc + 1)
                    if tc >= 2:
                        s.wait_ge(sem("proj_evac"), tc - 1)
                    for h in range(4):
                        sa = scr_act[:, (par * 10 + h) * 128:(par * 10 + h + 1) * 128]
                        s.activation(sa, qp0[:, h * 128:(h + 1) * 128], AF.Square,
                                     accum_out=rmsq[par][:, h:h + 1])
                    for h in range(4):
                        sa = scr_act[:, (par * 10 + 4 + h) * 128:(par * 10 + 5 + h) * 128]
                        s.activation(sa, qp1[:, h * 128:(h + 1) * 128], AF.Square,
                                     accum_out=rmsq[par][:, 4 + h:5 + h])
                    for g2 in range(2):
                        sa = scr_act[:, (par * 10 + 8 + g2) * 128:(par * 10 + 9 + g2) * 128]
                        ii = s.activation(sa, kvp[:, g2 * 128:(g2 + 1) * 128], AF.Square,
                                          accum_out=rmsk[par][:, g2:g2 + 1])
                    fence(s, ii, "fl_s")
                    s.activation(rmsq[par][:], rmsq[par][:], AF.Ln, scale=1.0 / D, bias=eps2_col[:])
                    fence(s, s.activation(rmsk[par][:, 0:2], rmsk[par][:, 0:2], AF.Ln,
                                          scale=1.0 / D, bias=eps2_col[:]), "fl_s")
                    s.activation(rmsq[par][:], rmsq[par][:], AF.Exp, scale=-0.5)
                    s.activation(rmsk[par][:, 0:2], rmsk[par][:, 0:2], AF.Exp, scale=-0.5)
                    inc(s.activation(tanh_sb[par][:], imp, AF.Tanh), "act_tc")

            @block.vector
            def _(v):
                for tci in range(4):
                    tc = wdw * 4 + tci
                    par = tc % 2
                    qp0, qp1, kvp = pbank[4 * par], pbank[4 * par + 1], pbank[4 * par + 2]
                    v.wait_ge(sem("act_tc"), tc + 1)
                    for h in range(8):
                        qp = qp0 if h < 4 else qp1
                        v.tensor_scalar(qhat[tc][:, h * 128:(h + 1) * 128],
                                        qp[:, (h % 4) * 128:(h % 4 + 1) * 128],
                                        rmsq[par][:, h:h + 1], None, op0=ALU.mult)
                    for g2 in range(2):
                        v.tensor_scalar(khat[tc][:, g2 * 128:(g2 + 1) * 128],
                                        kvp[:, g2 * 128:(g2 + 1) * 128],
                                        rmsk[par][:, g2:g2 + 1], None, op0=ALU.mult)
                    v.tensor_copy(vhat[tc][:], kvp[:, 256:512])
                    inc(v.tensor_tensor_reduce(scr_ttr[:, par * 64:(par + 1) * 64],
                                               tanh_sb[par][:], w2bc[:], 1.0, 0.0,
                                               op0=ALU.mult, op1=ALU.add,
                                               accum_out=logits_cm[:, tc:tc + 1]), "proj_evac")
        proj_done = cnt("proj_evac")

        if dbg and phases < 2:
            @block.sync
            def _(sy):
                sy.wait_ge(sem("proj_evac"), proj_done)
                if "logits" in dbg:
                    inc(sy.dma_start(out=dout("logits", [128, TC]), in_=logits_cm[:]), "dbg_st", 16)
                if "r1" in dbg:
                    inc(sy.dma_start(out=dout("r1", [1, S], BF16), in_=r1b[:]), "dbg_st", 16)
                if "qhat" in dbg:
                    o = dout("qhat", [S, NQ], BF16)
                    for tc in range(TC):
                        inc(sy.dma_start(out=o[tc * 128:(tc + 1) * 128, :], in_=qhat[tc][:]), "dbg_st", 16)
                if "khat" in dbg:
                    o = dout("khat", [S, NKVD], BF16)
                    for tc in range(TC):
                        inc(sy.dma_start(out=o[tc * 128:(tc + 1) * 128, :], in_=khat[tc][:]), "dbg_st", 16)
                if "vhat" in dbg:
                    o = dout("vhat", [S, NKVD], BF16)
                    for tc in range(TC):
                        inc(sy.dma_start(out=o[tc * 128:(tc + 1) * 128, :], in_=vhat[tc][:]), "dbg_st", 16)
                sy.wait_ge(sem("dbg_st"), cnt("dbg_st"))

        if phases < 2:
            lower_extended_insts(nc)
            return nc, dbg_outs

        ar.reset_to("persist")
        qf = [ar.alloc(f"qf{h}", [128, CT], BF16) for h in range(NH)]
        kf = [ar.alloc(f"kf{g}", [128, CT], BF16) for g in range(NKV)]
        vtm = [[ar.alloc(f"vtm{g}_{j}", [128, 132], BF16) for j in range(NCT)] for g in range(NKV)]
        assert ar.off <= ar.marks["p1end"], "phase2 regionA overflow"
        ar.reset_to("qkv_end")
        kaT_sb = ar.alloc("kaT_sb", [1, 320], BF16)
        cs_tile = [ar.alloc(f"cs_tile{i}", [128, 2 * D], BF16) for i in range(2)]
        logitsT = ar.alloc("logitsT", [32, 128], F32)
        tk_lo = ar.alloc("tk_lo", [32, 1], F32)
        tk_hi = ar.alloc("tk_hi", [32, 1], F32)
        tk_mid = ar.alloc("tk_mid", [32, 1], F32)
        tk_gt = ar.alloc("tk_gt", [32, 128], F32)
        tk_gt2 = ar.alloc("tk_gt2", [32, 128], F32)
        tk_pc = ar.alloc("tk_pc", [32, 1], F32)
        tk_sc = ar.alloc("tk_sc", [1, 1], F32)
        tk_cond = ar.alloc("tk_cond", [32, 1], U32)
        tk_condl = ar.alloc("tk_condl", [32, 1], U32)
        tk_z = ar.alloc("tk_z", [32, 128], F32)
        tk_scan = ar.alloc("tk_scan", [32, 128], F32)
        tk_row = ar.alloc("tk_row", [1, 64], F32)
        tk_off = ar.alloc("tk_off", [32, 1], F32)
        tk_d32 = ar.alloc("tk_d32", [32, 1], F32)
        tk_selu = ar.alloc("tk_selu", [32, 128], U32)
        tk_le = ar.alloc("tk_le", [32, 128], F32)
        tk_big = ar.alloc("tk_big", [32, 128], F32)
        tk_pos = ar.alloc("tk_pos", [32, 128], F32)
        skw_b = [ar.alloc(f"skw{i}", [128, 320], BF16) for i in range(2)]
        vsf = [ar.alloc(f"vsf{g}", [128, 640], BF16) for g in range(NKV)]
        ar.mark("p2end")

        @block.gpsimd
        def _(g):
            inc(g.dma_start(out=kaT_sb[:], in_=kaT_row[:]), "ld_g", 16)
        ld_g_p2 = cnt("ld_g")

        @block.tensor
        def _(t):
            t.wait_ge(sem("ld_g"), ld_g_p2)
            t.wait_ge(sem("proj_evac"), proj_done)
            wchain(t, "tks")
            inc(t.matmul(pbank[0][:, 0:320], ones_row_b[:], kaT_sb[:], start=True, stop=True),
                "tks")

        @block.vector
        def _(v):
            wchain(v, "tks")
            inc(v.tensor_copy(kabc[:], pbank[0][:, 0:320]), "tks")

        def skw_build(v, par, tc, shalf, wvec):
            a_sl = kbT_sb[:, (tc % 2) * 32 + shalf * 16:(tc % 2) * 32 + shalf * 16 + 16]
            in0 = a_sl.rearrange("p (a one) -> p a one", one=1).to_broadcast([128, 16, 20])
            jb = tc // 2
            in1 = kabc[0:128, jb * 20:(jb + 1) * 20].rearrange("p (one c) -> p one c", one=1)\
                .to_broadcast([128, 16, 20])
            o = skw_b[par].rearrange("p (a c) -> p a c", a=16)
            return v.scalar_tensor_tensor(o, in0, wvec[:, tc:tc + 1], in1,
                                          op0=ALU.mult, op1=ALU.mult)

        @block.scalar
        def _(s):
            s.wait_ge(sem("proj_evac"), proj_done)
            inc(s.activation(weights_cm[:], logits_cm[:], AF.Sigmoid, bias=b2bias[:]), "w_rdy")

        @block.tensor
        def _(t):
            t.wait_ge(sem("proj_evac"), proj_done)
            inc(t.transpose(pbank[1][0:32, 0:128], logits_cm[:], ident_f[:]), "tks")

        @block.vector
        def _(v):
            wchain(v, "tks")
            v.tensor_copy(logitsT[:], pbank[1][0:32, 0:128])
            fence(v, v.tensor_reduce(tk_pc[:, 0:1], w2bc[0:32, :], axis=AX,
                                     op=ALU.add, apply_absolute_value=True), "fl_v")
            fence(v, v.tensor_scalar(tk_hi[:], tk_pc[:, 0:1], 1.0, 1.0, op0=ALU.mult, op1=ALU.add), "fl_v")
            v.memset(tk_z[:], 0.0)
            inc(v.scalar_tensor_tensor(tk_lo[:], tk_hi[:], -1.0, tk_hi[:], op0=ALU.mult,
                                       op1=ALU.min), "tks")

        for it in range(BISECT):
            @block.vector
            def _(v):
                wchain(v, "tks")
                if it > 0:
                    v.tensor_scalar(tk_cond[:], pbank[3][0:32, 0:1], 2048.0, None, op0=ALU.is_gt)
                    fence(v, v.tensor_scalar(tk_condl[:], pbank[3][0:32, 0:1], 2048.0, None,
                                             op0=ALU.is_le), "fl_v")
                    v.copy_predicated(tk_lo[:], tk_cond[:], tk_mid[:])
                    fence(v, v.copy_predicated(tk_hi[:], tk_condl[:], tk_mid[:]), "fl_v")
                fence(v, v.tensor_add(tk_mid[:], tk_lo[:], tk_hi[:]), "fl_v")
                fence(v, v.tensor_scalar(tk_mid[:], tk_mid[:], 0.5, None, op0=ALU.mult), "fl_v")
                fence(v, v.tensor_scalar(tk_gt[:], logitsT[:], tk_mid[:, 0:1], None, op0=ALU.is_gt), "fl_v")
                inc(v.reduce_sum(tk_pc[:], tk_gt[:], axis=AX), "tks")

            @block.tensor
            def _(t):
                wchain(t, "tks")
                inc(t.matmul(pbank[2][0:1, 0:1], ones_col_f[0:32, :], tk_pc[:], start=True, stop=True),
                    "tks")

            @block.vector
            def _(v):
                wchain(v, "tks")
                inc(v.tensor_copy(tk_sc[:], pbank[2][0:1, 0:1]), "tks")

            @block.tensor
            def _(t):
                wchain(t, "tks")
                inc(t.matmul(pbank[3][0:32, 0:1], ones_row_f[0:1, 0:32], tk_sc[:], start=True, stop=True),
                    "tks")

        @block.vector
        def _(v):
            wchain(v, "tks")
            v.tensor_scalar(tk_cond[:], pbank[3][0:32, 0:1], 2048.0, None, op0=ALU.is_gt)
            fence(v, v.tensor_scalar(tk_condl[:], pbank[3][0:32, 0:1], 2048.0, None, op0=ALU.is_le), "fl_v")
            v.copy_predicated(tk_lo[:], tk_cond[:], tk_mid[:])
            fence(v, v.copy_predicated(tk_hi[:], tk_condl[:], tk_mid[:]), "fl_v")
            v.tensor_scalar(tk_gt[:], logitsT[:], tk_hi[:, 0:1], None, op0=ALU.is_gt)
            fence(v, v.tensor_scalar(tk_gt2[:], logitsT[:], tk_lo[:, 0:1], None, op0=ALU.is_gt), "fl_v")
            fence(v, v.tensor_sub(tk_gt2[:], tk_gt2[:], tk_gt[:]), "fl_v")
            inc(v.reduce_sum(tk_pc[:], tk_gt[:], axis=AX), "tks")

        @block.tensor
        def _(t):
            wchain(t, "tks")
            inc(t.matmul(pbank[2][0:1, 0:1], ones_col_f[0:32, :], tk_pc[:], start=True, stop=True),
                "tks")

        @block.vector
        def _(v):
            wchain(v, "tks")
            inc(v.tensor_scalar(tk_sc[:], pbank[2][0:1, 0:1], -1.0, 2048.0, op0=ALU.mult,
                                op1=ALU.add), "tks")

        @block.tensor
        def _(t):
            wchain(t, "tks")
            inc(t.matmul(pbank[3][0:32, 0:1], ones_row_f[0:1, 0:32], tk_sc[:], start=True, stop=True),
                "tks")

        def prefix_pass(v, t_sems, src, dst, tag):
            fence(v, v.tensor_tensor_scan(dst, src, tk_z[:], 0.0, op0=ALU.add, op1=ALU.add), "fl_v")
            inc(v.tensor_copy(tk_pc[:], dst[0:32, 127:128]), "tks")

        @block.vector
        def _(v):
            wchain(v, "tks")
            fence(v, v.tensor_copy(tk_d32[:], pbank[3][0:32, 0:1]), "fl_v")
            prefix_pass(v, None, tk_gt2[:], tk_scan[:], "pfb")

        @block.tensor
        def _(t):
            wchain(t, "tks")
            inc(t.transpose(pbank[4][0:1, 0:32], tk_pc[0:32, 0:1], ident_f[0:32, 0:32]), "tks")

        @block.vector
        def _(v):
            wchain(v, "tks")
            fence(v, v.tensor_copy(tk_row[0:1, 0:32], pbank[4][0:1, 0:32]), "fl_v")
            fence(v, v.memset(tk_row[0:1, 32:64], 0.0), "fl_v")
            fence(v, v.tensor_tensor_scan(tk_row[0:1, 32:64], tk_row[0:1, 0:32], tk_row[0:1, 32:64],
                                          0.0, op0=ALU.add, op1=ALU.add), "fl_v")
            v.memset(tk_row[0:1, 0:1], 0.0)
            inc(v.tensor_copy(tk_row[0:1, 1:32], tk_row[0:1, 32:63]), "tks")

        @block.tensor
        def _(t):
            wchain(t, "tks")
            inc(t.transpose(pbank[4][0:32, 0:1], tk_row[0:1, 0:32], ident_f[0:1, 0:1]), "tks")

        @block.vector
        def _(v):
            wchain(v, "tks")
            fence(v, v.tensor_copy(tk_off[:], pbank[4][0:32, 0:1]), "fl_v")
            fence(v, v.tensor_scalar(tk_scan[:], tk_scan[:], tk_off[:, 0:1], None, op0=ALU.add), "fl_v")
            fence(v, v.tensor_scalar(tk_le[:], tk_scan[:], tk_d32[:, 0:1], None, op0=ALU.is_le), "fl_v")
            fence(v, v.tensor_mul(tk_gt2[:], tk_gt2[:], tk_le[:]), "fl_v")
            fence(v, v.tensor_add(tk_gt[:], tk_gt[:], tk_gt2[:]), "fl_v")
            prefix_pass(v, None, tk_gt[:], tk_scan[:], "pfs")

        @block.tensor
        def _(t):
            wchain(t, "tks")
            inc(t.transpose(pbank[4][0:1, 0:32], tk_pc[0:32, 0:1], ident_f[0:32, 0:32]), "tks")

        @block.vector
        def _(v):
            wchain(v, "tks")
            fence(v, v.tensor_copy(tk_row[0:1, 0:32], pbank[4][0:1, 0:32]), "fl_v")
            fence(v, v.memset(tk_row[0:1, 32:64], 0.0), "fl_v")
            fence(v, v.tensor_tensor_scan(tk_row[0:1, 32:64], tk_row[0:1, 0:32], tk_row[0:1, 32:64],
                                          0.0, op0=ALU.add, op1=ALU.add), "fl_v")
            v.memset(tk_row[0:1, 0:1], 0.0)
            inc(v.tensor_copy(tk_row[0:1, 1:32], tk_row[0:1, 32:63]), "tks")

        @block.tensor
        def _(t):
            wchain(t, "tks")
            inc(t.transpose(pbank[4][0:32, 0:1], tk_row[0:1, 0:32], ident_f[0:1, 0:1]), "tks")

        @block.vector
        def _(v):
            wchain(v, "tks")
            fence(v, v.tensor_copy(tk_off[:], pbank[4][0:32, 0:1]), "fl_v")
            fence(v, v.tensor_scalar(tk_scan[:], tk_scan[:], tk_off[:, 0:1], None, op0=ALU.add), "fl_v")
            v.tensor_scalar(tk_pos[:], tk_scan[:], -1.0, None, op0=ALU.add)
            v.tensor_scalar(tk_selu[:], tk_gt[:], 0.5, None, op0=ALU.is_lt)
            fence(v, v.memset(tk_big[:], BIGPOS), "fl_v")
            inc(v.copy_predicated(tk_pos[:], tk_selu[:], tk_big[:]), "tks")

        @block.tensor
        def _(t):
            wchain(t, "tks")
            inc(t.transpose(pbank[5][0:128, 0:32], tk_pos[0:32, 0:128], ident_f[0:32, 0:32]), "tks")
            inc(t.transpose(pbank[6][0:128, 0:32], tk_gt[0:32, 0:128], ident_f[0:32, 0:32]), "tks")

        @block.vector
        def _(v):
            wchain(v, "tks")
            inc(v.tensor_copy(sel_cm[:], pbank[6][0:128, 0:32]), "tks")
            v.tensor_copy(selu_cm[:], pbank[6][0:128, 0:32])
            fence(v, v.tensor_copy(pos_cm[:], pbank[5][0:128, 0:32]), "fl_v")
            v.wait_ge(sem("w_rdy"), 1)
            fence(v, v.tensor_scalar(restw_cm[:], sel_cm[:], -1.0, 1.0, op0=ALU.mult, op1=ALU.add), "fl_v")
            inc(v.tensor_mul(restw_cm[:], weights_cm[:], restw_cm[:]), "rw_rdy")

        pos_fl = cnt("fl_v")

        @block.gpsimd
        def _(g):
            g.wait_ge(sem("fl_v"), pos_fl)
            btk = g.to_reg(TK - 1)
            for tc in range(TC):
                inc(g.indirect_dma_start(
                    out=qd_dram[:], out_offset=IndirectOffsetOnAxis(pos_cm[:, tc:tc + 1], 0),
                    in_=qhat[tc][:], in_offset=None,
                    bounds_check=btk, oob_is_err=False), "sc_q", 16)
                inc(g.indirect_dma_start(
                    out=kd_dram[:], out_offset=IndirectOffsetOnAxis(pos_cm[:, tc:tc + 1], 0),
                    in_=khat[tc][:], in_offset=None,
                    bounds_check=btk, oob_is_err=False), "sc_k", 16)
                inc(g.indirect_dma_start(
                    out=vd_dram[:], out_offset=IndirectOffsetOnAxis(pos_cm[:, tc:tc + 1], 0),
                    in_=vhat[tc][:], in_offset=None,
                    bounds_check=btk, oob_is_err=False), "sc_v", 16)
        sc_q_all, sc_k_all = cnt("sc_q"), cnt("sc_k")
        sc_v_all = cnt("sc_v")

        @block.gpsimd
        def _(g):
            g.wait_ge(sem("fl_v"), pos_fl)
            btk2 = g.to_reg(TK - 1)
            for tc in range(TC):
                i = tc % 2
                if tc >= 2:
                    g.wait_ge(sem(f"sc_i{i}"), 32 * (tc // 2))
                inc(g.dma_start(out=cs_tile[i][:, 0:D],
                                in_=cos_tm[tc * 128:(tc + 1) * 128, :]), f"cs_ld{i}", 16)
                inc(g.dma_start(out=cs_tile[i][:, D:2 * D],
                                in_=sin_tm[tc * 128:(tc + 1) * 128, :]), f"cs_ld{i}", 16)
                g.wait_ge(sem(f"cs_ld{i}"), 32 * (tc // 2 + 1))
                inc(g.indirect_dma_start(
                    out=cosd_dram[:], out_offset=IndirectOffsetOnAxis(pos_cm[:, tc:tc + 1], 0),
                    in_=cs_tile[i][:, 0:D], in_offset=None,
                    bounds_check=btk2, oob_is_err=False), f"sc_i{i}", 16)
                inc(g.indirect_dma_start(
                    out=sind_dram[:], out_offset=IndirectOffsetOnAxis(pos_cm[:, tc:tc + 1], 0),
                    in_=cs_tile[i][:, D:2 * D], in_offset=None,
                    bounds_check=btk2, oob_is_err=False), f"sc_i{i}", 16)
        sc_i_all = (cnt("sc_i0"), cnt("sc_i1"))

        for shalf in range(2):
            @block.vector
            def _(v):
                for tc in range(TC):
                    if shalf == 0 and tc == 0:
                        wchain(v, "tks")
                        v.wait_ge(sem("w_rdy"), 1)
                    if shalf * TC + tc >= 2:
                        v.wait_ge(sem("skw_used"), shalf * TC + tc - 1)
                    inc(skw_build(v, tc % 2, tc, shalf, weights_cm), "skw_rdy")

            @block.tensor
            def _(t):
                for tc in range(TC):
                    t.wait_ge(sem("skw_rdy"), shalf * TC + tc + 1)
                    if shalf == 1 and tc == 0:
                        t.wait_ge(sem("qs_evac"), 8)
                    fi, la = tc == 0, tc == TC - 1
                    for h in range(NH):
                        mmi = t.matmul(pbank[h][:, 0:320], qhat[tc][:, h * 128:(h + 1) * 128],
                                       skw_b[tc % 2][:], start=fi, stop=la)
                    inc(mmi, "skw_used")

            @block.vector
            def _(v):
                v.wait_ge(sem("skw_used"), (shalf + 1) * TC)
                for h in range(NH):
                    inc(v.tensor_scalar(qf[h][:, TK + shalf * 320:TK + shalf * 320 + 320],
                                        pbank[h][:, 0:320], qwss[:, 0:1], None, op0=ALU.mult),
                        "qs_evac")

        for shalf in range(2):
            @block.vector
            def _(v):
                for tc in range(TC):
                    kk = (2 + shalf) * TC + tc
                    if shalf == 0 and tc == 0:
                        v.wait_ge(sem("rw_rdy"), 1)
                    if kk >= 2:
                        v.wait_ge(sem("skw_used"), kk - 1)
                    inc(skw_build(v, tc % 2, tc, shalf, restw_cm), "skw_rdy")

            @block.tensor
            def _(t):
                for tc in range(TC):
                    kk = (2 + shalf) * TC + tc
                    t.wait_ge(sem("skw_rdy"), kk + 1)
                    if tc == 0 and shalf == 0:
                        t.wait_ge(sem("qs_evac"), 16)
                    if tc == 0 and shalf == 1:
                        t.wait_ge(sem("kvs_evac"), 4)
                    fi, la = tc == 0, tc == TC - 1
                    for g2 in range(NKV):
                        t.matmul(pbank[g2][:, 0:320], khat[tc][:, g2 * 128:(g2 + 1) * 128],
                                 skw_b[tc % 2][:], start=fi, stop=la)
                        mmi = t.matmul(pbank[2 + g2][:, 0:320], vhat[tc][:, g2 * 128:(g2 + 1) * 128],
                                       skw_b[tc % 2][:], start=fi, stop=la)
                    inc(mmi, "skw_used")

            @block.vector
            def _(v):
                v.wait_ge(sem("skw_used"), (3 + shalf) * TC)
                for g2 in range(NKV):
                    inc(v.tensor_scalar(kf[g2][:, TK + shalf * 320:TK + shalf * 320 + 320],
                                        pbank[g2][:, 0:320], kwss[:, 0:1], None, op0=ALU.mult),
                        "kvs_evac")
                    inc(v.tensor_scalar(vsf[g2][:, shalf * 320:shalf * 320 + 320],
                                        pbank[2 + g2][:, 0:320], ssbc[:, 0:1], None, op0=ALU.mult),
                        "kvs_evac")
        kvs_all = cnt("kvs_evac")

        @block.vector
        def _(v):
            v.wait_ge(sem("proj_evac"), proj_done)
            for g2 in range(NKV):
                for j in range(NCT):
                    v.memset(vtm[g2][j][:, 129:132], 0.0)
                    ii = v.memset(vtm[g2][j][:, 128:129], 1.0)
            inc(ii, "vtm_ones")

        for g2 in range(NKV):
            for j in range(5):
                jj = g2 * 5 + j

                @block.tensor
                def _(t):
                    if jj == 0:
                        t.wait_ge(sem("kvs_evac"), kvs_all)
                    if jj >= 2:
                        t.wait_ge(sem("vs_ev"), jj - 1)
                    inc(t.transpose(pbank[4 + jj % 2].bitcast(BF16)[:, 0:128],
                                    vsf[g2][:, j * 128:(j + 1) * 128], ident_b[:]), "vs_tr")

                @block.vector
                def _(v):
                    v.wait_ge(sem("vs_tr"), jj + 1)
                    inc(v.tensor_copy(vtm[g2][16 + j][:, 0:128],
                                      pbank[4 + jj % 2].bitcast(BF16)[:, 0:128]), "vs_ev")
        vs_all = cnt("vs_ev")

        if dbg and phases < 3:
            @block.sync
            def _(sy):
                sy.wait_ge(sem("vs_ev"), vs_all)
                wchain(sy, "tks")
                sy.wait_ge(sem("rw_rdy"), 1)
                sy.wait_ge(sem("qs_evac"), 16)
                sy.wait_ge(sem("sc_q"), sc_q_all)
                sy.wait_ge(sem("sc_k"), sc_k_all)
                sy.wait_ge(sem("sc_v"), sc_v_all)
                sy.wait_ge(sem("sc_i0"), sc_i_all[0])
                sy.wait_ge(sem("sc_i1"), sc_i_all[1])
                if "pos" in dbg:
                    inc(sy.dma_start(out=dout("pos", [128, TC], U32), in_=pos_cm[:]), "dbg_st", 16)
                if "sel" in dbg:
                    inc(sy.dma_start(out=dout("sel", [128, TC]), in_=sel_cm[:]), "dbg_st", 16)
                if "weights" in dbg:
                    inc(sy.dma_start(out=dout("weights", [128, TC]), in_=weights_cm[:]), "dbg_st", 16)
                if "restw" in dbg:
                    inc(sy.dma_start(out=dout("restw", [128, TC]), in_=restw_cm[:]), "dbg_st", 16)
                if "qs" in dbg:
                    o = dout("qs", [NH * 128, 640], BF16)
                    for h in range(NH):
                        inc(sy.dma_start(out=o[h * 128:(h + 1) * 128, :], in_=qf[h][:, TK:CT]), "dbg_st", 16)
                if "ks" in dbg:
                    o = dout("ks", [NKV * 128, 640], BF16)
                    for g2 in range(NKV):
                        inc(sy.dma_start(out=o[g2 * 128:(g2 + 1) * 128, :], in_=kf[g2][:, TK:CT]), "dbg_st", 16)
                if "vs" in dbg:
                    o = dout("vs", [NKV * 128, 640], BF16)
                    for g2 in range(NKV):
                        inc(sy.dma_start(out=o[g2 * 128:(g2 + 1) * 128, :], in_=vsf[g2][:]), "dbg_st", 16)
                if "cosd" in dbg:
                    inc(sy.dma_start(out=dout("cosd", [TK, D], BF16), in_=cosd_dram[:]), "dbg_st", 16)
                if "qd" in dbg:
                    inc(sy.dma_start(out=dout("qd", [TK, NQ], BF16), in_=qd_dram[:]), "dbg_st", 16)
                sy.wait_ge(sem("dbg_st"), cnt("dbg_st"))

        if phases < 3:
            lower_extended_insts(nc)
            return nc, dbg_outs

        ar.reset_to("p1end")
        qwb1024 = ar.alloc("qwb1024", [128, NQ], BF16)
        qwsw1024 = ar.alloc("qwsw1024", [128, NQ], BF16)
        kwb256 = ar.alloc("kwb256", [128, NKVD], BF16)
        kwsw256 = ar.alloc("kwsw256", [128, NKVD], BF16)
        qd_r = [ar.alloc(f"qd_r{i}", [128, NQ], BF16) for i in range(4)]
        kcs_r = [ar.alloc(f"kcs_r{i}", [128, NKVD + 2 * D], BF16) for i in range(4)]
        ropq = [ar.alloc(f"ropq{i}", [128, NQ], BF16) for i in range(2)]
        ropk = [ar.alloc(f"ropk{i}", [128, NKVD], BF16) for i in range(2)]
        cosq = [ar.alloc(f"cosq{i}", [128, NQ], BF16) for i in range(2)]
        sinq = [ar.alloc(f"sinq{i}", [128, NQ], BF16) for i in range(2)]
        cosk = [ar.alloc(f"cosk{i}", [128, NKVD], BF16) for i in range(2)]
        sink = [ar.alloc(f"sink{i}", [128, NKVD], BF16) for i in range(2)]
        rotq = [ar.alloc(f"rotq{i}", [128, NQ], BF16) for i in range(2)]
        rotk = [ar.alloc(f"rotk{i}", [128, NKVD], BF16) for i in range(2)]
        sgn = [ar.alloc(f"sgn{i}", [128, 2 * D], BF16) for i in range(2)]
        pt_sb = ar.alloc("pt_sb", [128, NCT * 512], BF16)
        att_sc = [ar.alloc(f"attsc{i}", [128, 132], BF16) for i in range(4)]
        rcp_sb = ar.alloc("rcp_sb", [128, 8], F32)
        ar.mark("p5top")
        out_sk = [ar.alloc(f"outsk{j}", [128, NQ], BF16) for j in range(5)]
        ar.mark("p4end")

        def p3_gate(e):
            e.wait_ge(sem("skw_used"), 128)
            e.wait_ge(sem("sc_q"), sc_q_all)
            e.wait_ge(sem("sc_k"), sc_k_all)
            e.wait_ge(sem("sc_v"), sc_v_all)
            e.wait_ge(sem("sc_i0"), sc_i_all[0])
            e.wait_ge(sem("sc_i1"), sc_i_all[1])
            e.wait_ge(sem("vs_ev"), vs_all)
            e.wait_ge(sem("qs_evac"), 16)
            e.wait_ge(sem("kvs_evac"), kvs_all)
            e.wait_ge(sem("vtm_ones"), 1)

        @block.tensor
        def _(t):
            p3_gate(t)
            rep8 = qw_rows[0:1, 0:128].rearrange("a (b d) -> a b d", b=1).to_broadcast([1, 8, 128])
            rep8s = qw_rows[0:1, 128:256].rearrange("a (b d) -> a b d", b=1).to_broadcast([1, 8, 128])
            rep2 = kw_rows[0:1, 0:128].rearrange("a (b d) -> a b d", b=1).to_broadcast([1, 2, 128])
            rep2s = kw_rows[0:1, 128:256].rearrange("a (b d) -> a b d", b=1).to_broadcast([1, 2, 128])
            t.matmul(pbank[6][:, 0:512], ones_row_f[:], rep8[0:1, 0:4, :], start=True, stop=True)
            t.matmul(pbank[7][:, 0:512], ones_row_f[:], rep8[0:1, 4:8, :], start=True, stop=True)
            inc(t.matmul(pbank[4][:, 0:256], ones_row_f[:], rep2[:], start=True, stop=True), "tks")

        @block.vector
        def _(v):
            p3_gate(v)
            wchain(v, "tks")
            v.tensor_copy(qwb1024[:, 0:512], pbank[6][:, 0:512])
            v.tensor_copy(qwb1024[:, 512:1024], pbank[7][:, 0:512])
            inc(v.tensor_copy(kwb256[:], pbank[4][:, 0:256]), "tks")

        @block.tensor
        def _(t):
            wchain(t, "tks")
            rep8s = qw_rows[0:1, 128:256].rearrange("a (b d) -> a b d", b=1).to_broadcast([1, 8, 128])
            rep2s = kw_rows[0:1, 128:256].rearrange("a (b d) -> a b d", b=1).to_broadcast([1, 2, 128])
            t.matmul(pbank[6][:, 0:512], ones_row_f[:], rep8s[0:1, 0:4, :], start=True, stop=True)
            t.matmul(pbank[7][:, 0:512], ones_row_f[:], rep8s[0:1, 4:8, :], start=True, stop=True)
            inc(t.matmul(pbank[4][:, 0:256], ones_row_f[:], rep2s[:], start=True, stop=True), "tks")

        @block.vector
        def _(v):
            wchain(v, "tks")
            v.tensor_copy(qwsw1024[:, 0:512], pbank[6][:, 0:512])
            v.tensor_copy(qwsw1024[:, 512:1024], pbank[7][:, 0:512])
            inc(v.tensor_copy(kwsw256[:], pbank[4][:, 0:256]), "tks")

        @block.sync
        def _(sy):
            p3_gate(sy)
            for j in range(NTK):
                i = j % 4
                if j >= 4:
                    sy.wait_ge(sem("rope_done"), j - 3)
                inc(sy.dma_start(out=qd_r[i][:], in_=qd_dram[j * 128:(j + 1) * 128, :]), f"g3_{i}", 16)
                inc(sy.dma_start(out=kcs_r[i][:, 0:NKVD], in_=kd_dram[j * 128:(j + 1) * 128, :]), f"g3_{i}", 16)
                inc(sy.dma_start(out=kcs_r[i][:, NKVD:NKVD + D], in_=cosd_dram[j * 128:(j + 1) * 128, :]), f"g3_{i}", 16)
                inc(sy.dma_start(out=kcs_r[i][:, NKVD + D:NKVD + 2 * D], in_=sind_dram[j * 128:(j + 1) * 128, :]), f"g3_{i}", 16)
                inc(sy.dma_start(out=vtm[0][j][:, 0:128], in_=vd_dram[j * 128:(j + 1) * 128, 0:128]), f"g3_{i}", 16)
                inc(sy.dma_start(out=vtm[1][j][:, 0:128], in_=vd_dram[j * 128:(j + 1) * 128, 128:256]), f"g3_{i}", 16)

        for j in range(NTK):
            i = j % 4
            par = j % 2

            @block.vector
            def _(v):
                v.wait_ge(sem(f"g3_{i}"), 96 * (j // 4 + 1))
                if j >= 2:
                    v.wait_ge(sem("q_trev"), 10 * (j - 1))
                cosd_t = kcs_r[i][:, NKVD:NKVD + D]
                sind_t = kcs_r[i][:, NKVD + D:NKVD + 2 * D]
                v.tensor_scalar(sgn[par][:, 0:64], sind_t[:, 0:64], -1.0, None, op0=ALU.mult)
                fence(v, v.tensor_copy(sgn[par][:, 64:128], sind_t[:, 64:128]), "fl_v")
                for hh in range(NH):
                    v.tensor_copy(rotq[par][:, hh * 128:hh * 128 + 64], qd_r[i][:, hh * 128 + 64:hh * 128 + 128])
                    v.tensor_copy(rotq[par][:, hh * 128 + 64:hh * 128 + 128], qd_r[i][:, hh * 128:hh * 128 + 64])
                for g2 in range(NKV):
                    v.tensor_copy(rotk[par][:, g2 * 128:g2 * 128 + 64], kcs_r[i][:, g2 * 128 + 64:g2 * 128 + 128])
                    v.tensor_copy(rotk[par][:, g2 * 128 + 64:g2 * 128 + 128], kcs_r[i][:, g2 * 128:g2 * 128 + 64])
                ce = cosd_t.rearrange("p (one d) -> p one d", one=1).to_broadcast([128, 8, 128])
                se = sgn[par][:, 0:128].rearrange("p (one d) -> p one d", one=1).to_broadcast([128, 8, 128])
                v.tensor_tensor(cosq[par].rearrange("p (r d) -> p r d", r=8), ce,
                                qwb1024.rearrange("p (r d) -> p r d", r=8), op=ALU.mult)
                v.tensor_tensor(sinq[par].rearrange("p (r d) -> p r d", r=8), se,
                                qwsw1024.rearrange("p (r d) -> p r d", r=8), op=ALU.mult)
                ce2 = cosd_t.rearrange("p (one d) -> p one d", one=1).to_broadcast([128, 2, 128])
                se2 = sgn[par][:, 0:128].rearrange("p (one d) -> p one d", one=1).to_broadcast([128, 2, 128])
                v.tensor_tensor(cosk[par].rearrange("p (r d) -> p r d", r=2), ce2,
                                kwb256.rearrange("p (r d) -> p r d", r=2), op=ALU.mult)
                fence(v, v.tensor_tensor(sink[par].rearrange("p (r d) -> p r d", r=2), se2,
                                         kwsw256.rearrange("p (r d) -> p r d", r=2), op=ALU.mult), "fl_v")
                v.tensor_mul(ropq[par][:], qd_r[i][:], cosq[par][:])
                v.tensor_mul(rotq[par][:], rotq[par][:], sinq[par][:])
                v.tensor_mul(ropk[par][:], kcs_r[i][:, 0:NKVD], cosk[par][:])
                fence(v, v.tensor_mul(rotk[par][:], rotk[par][:], sink[par][:]), "fl_v")
                v.tensor_add(ropq[par][:], ropq[par][:], rotq[par][:])
                inc(v.tensor_add(ropk[par][:], ropk[par][:], rotk[par][:]), "rope_done")

            @block.tensor
            def _(t):
                t.wait_ge(sem("rope_done"), j + 1)
                for hh in range(NH):
                    idx = j * 10 + hh
                    if idx >= 4:
                        t.wait_ge(sem("q_trev"), idx - 3)
                    inc(t.transpose(pbank[idx % 4].bitcast(BF16)[:, 0:128],
                                    ropq[par][:, hh * 128:(hh + 1) * 128], ident_b[:]), "q_tr")
                for g2 in range(NKV):
                    idx = j * 10 + 8 + g2
                    if idx >= 4:
                        t.wait_ge(sem("q_trev"), idx - 3)
                    inc(t.transpose(pbank[idx % 4].bitcast(BF16)[:, 0:128],
                                    ropk[par][:, g2 * 128:(g2 + 1) * 128], ident_b[:]), "q_tr")

            @block.vector
            def _(v):
                for hh in range(NH):
                    idx = j * 10 + hh
                    v.wait_ge(sem("q_tr"), idx + 1)
                    inc(v.tensor_copy(qf[hh][:, j * 128:(j + 1) * 128],
                                      pbank[idx % 4].bitcast(BF16)[:, 0:128]), "q_trev")
                for g2 in range(NKV):
                    idx = j * 10 + 8 + g2
                    v.wait_ge(sem("q_tr"), idx + 1)
                    inc(v.tensor_copy(kf[g2][:, j * 128:(j + 1) * 128],
                                      pbank[idx % 4].bitcast(BF16)[:, 0:128]), "q_trev")
        g3_tot = [cnt(f"g3_{i}") for i in range(4)]

        if dbg and phases < 4:
            @block.sync
            def _(sy):
                sy.wait_ge(sem("q_trev"), 10 * NTK)
                if "pos" in dbg:
                    inc(sy.dma_start(out=dout("pos", [128, TC], U32), in_=pos_cm[:]), "dbg_st", 16)
                if "sel" in dbg:
                    inc(sy.dma_start(out=dout("sel", [128, TC]), in_=sel_cm[:]), "dbg_st", 16)
                if "qfd" in dbg:
                    o = dout("qfd", [NH * 128, CT], BF16)
                    for hh in range(NH):
                        inc(sy.dma_start(out=o[hh * 128:(hh + 1) * 128, :], in_=qf[hh][:]), "dbg_st", 16)
                if "kfd" in dbg:
                    o = dout("kfd", [NKV * 128, CT], BF16)
                    for g2 in range(NKV):
                        inc(sy.dma_start(out=o[g2 * 128:(g2 + 1) * 128, :], in_=kf[g2][:]), "dbg_st", 16)
                if "vtmd" in dbg:
                    o = dout("vtmd", [NKV * NCT * 128, 132], BF16)
                    for g2 in range(NKV):
                        for jj in range(NCT):
                            inc(sy.dma_start(out=o[(g2 * NCT + jj) * 128:(g2 * NCT + jj + 1) * 128, :],
                                             in_=vtm[g2][jj][:]), "dbg_st", 16)
                sy.wait_ge(sem("dbg_st"), cnt("dbg_st"))

        if phases < 4:
            lower_extended_insts(nc)
            return nc, dbg_outs


        QCW = [512, 512, 512, 512, 512, 128]
        det_uses = [0, 0, 0, 0]
        n_sc = 0
        n_pv = 0
        n_ev = 0

        for h in range(NH):
            g2 = h // 4
            for qc in range(6):
                w = QCW[qc]
                q0 = qc * 512

                @block.tensor
                def _(t):
                    if h == 0 and qc == 0:
                        t.wait_ge(sem("q_trev"), 10 * NTK)
                    for kt in range(NCT):
                        idx = n_sc + kt
                        if idx >= 2:
                            t.wait_ge(sem("s_exp"), idx - 1)
                        inc(t.matmul(pbank[idx % 2][:, 0:w], kf[g2][:, kt * 128:(kt + 1) * 128],
                                     qf[h][:, q0:q0 + w], start=True, stop=True), "s_mm")

                @block.scalar
                def _(s):
                    s.wait_ge(sem("pv_mm"), n_pv)
                    for kt in range(NCT):
                        idx = n_sc + kt
                        s.wait_ge(sem("s_mm"), idx + 1)
                        inc(s.activation(pt_sb[:, kt * 512:kt * 512 + w], pbank[idx % 2][:, 0:w],
                                         AF.Exp, scale=ISQD), "s_exp")
                n_sc += NCT

                @block.tensor
                def _(t):
                    t.wait_ge(sem("s_exp"), n_sc)
                    for qsub in range(w // 128):
                        if n_pv + qsub >= 4:
                            t.wait_ge(sem("att_ev"), n_pv + qsub - 3)
                        for kt in range(NCT):
                            mmi = t.matmul(pbank[2 + (n_pv + qsub) % 4][:, 0:129],
                                           pt_sb[:, kt * 512 + qsub * 128:kt * 512 + (qsub + 1) * 128],
                                           vtm[g2][kt][:, 0:129],
                                           start=(kt == 0), stop=(kt == NCT - 1))
                        inc(mmi, "pv_mm")

                @block.vector
                def _(v):
                    v.wait_ge(sem("pv_mm"), n_pv + w // 128)
                    for qsub in range(w // 128):
                        v.reciprocal(rcp_sb[:, qsub:qsub + 1],
                                     pbank[2 + (n_pv + qsub) % 4][:, 128:129])
                    fence(v, v.memset(rcp_sb[:, 7:8], 0.0), "fl_v")
                    for qsub in range(w // 128):
                        i4 = (n_pv + qsub) % 4
                        r0 = q0 + qsub * 128
                        if r0 < TK:
                            if det_uses[i4] > 0:
                                v.wait_ge(sem(f"det_st{i4}"), 16 * det_uses[i4])
                            dst = att_sc[i4][:, 0:128]
                        else:
                            sch = (r0 - TK) // 128
                            dst = out_sk[sch][:, h * 128:(h + 1) * 128]
                        inc(v.tensor_scalar(dst, pbank[2 + i4][:, 0:128],
                                            rcp_sb[:, qsub:qsub + 1], None, op0=ALU.mult), "att_ev")

                @block.sync
                def _(sy):
                    for qsub in range(w // 128):
                        i4 = (n_pv + qsub) % 4
                        r0 = q0 + qsub * 128
                        if r0 < TK:
                            sy.wait_ge(sem("att_ev"), n_pv + qsub + 1)
                            inc(sy.dma_start(out=outdet_dram[r0:r0 + 128, h * 128:(h + 1) * 128],
                                             in_=att_sc[i4][:, 0:128]), f"det_st{i4}", 16)
                            det_uses[i4] += 1
                n_pv += w // 128
        att_all = cnt("att_ev")
        det_all = [cnt(f"det_st{i}") for i in range(4)]

        if dbg and phases < 5:
            @block.sync
            def _(sy):
                sy.wait_ge(sem("att_ev"), att_all)
                for i4 in range(4):
                    sy.wait_ge(sem(f"det_st{i4}"), det_all[i4])
                if "pos" in dbg:
                    inc(sy.dma_start(out=dout("pos", [128, TC], U32), in_=pos_cm[:]), "dbg_st", 16)
                if "sel" in dbg:
                    inc(sy.dma_start(out=dout("sel", [128, TC]), in_=sel_cm[:]), "dbg_st", 16)
                if "outdet" in dbg:
                    inc(sy.dma_start(out=dout("outdet", [TK, NQ], BF16), in_=outdet_dram[:]), "dbg_st", 16)
                if "outsk" in dbg:
                    o = dout("outsk", [5 * 128, NQ], BF16)
                    for j in range(5):
                        inc(sy.dma_start(out=o[j * 128:(j + 1) * 128, :], in_=out_sk[j][:]), "dbg_st", 16)
                sy.wait_ge(sem("dbg_st"), cnt("dbg_st"))

        if phases < 5:
            lower_extended_insts(nc)
            return nc, dbg_outs

        p5top = ar.marks["p5top"]
        ar.reset_to("persist")
        kb_sb = ar.alloc("kb_sb", [32, 256], BF16)
        ka_sb = ar.alloc("ka_sb", [20, 128], BF16)
        sela_sb = ar.alloc("sela_sb", [32, 640], BF16)
        selc_sb = ar.alloc("selc_sb", [20, 640], BF16)
        kexp = [ar.alloc(f"kexp{i}", [128, 384], BF16) for i in range(2)]
        skfm = [ar.alloc(f"skfm{j}", [128, S], BF16) for j in range(5)]
        full_sb = [ar.alloc(f"full_sb{i}", [128, NQ], BF16) for i in range(2)]
        fullT = [ar.alloc(f"fullT{hd}", [128, S], BF16) for hd in range(8)]
        wo_sb = [ar.alloc(f"wo_sb{hd}", [128, DM], BF16) for hd in range(8)]
        cc_sb = [ar.alloc(f"cc_sb{i}", [128, 512], BF16) for i in range(2)]
        det_t = [ar.alloc(f"det_t{i}", [128, NQ], BF16) for i in range(2)]
        assert ar.off <= p5top, f"phase5 overflow: {ar.off} > {p5top}"

        @block.gpsimd
        def _(g):
            g.wait_ge(sem("pv_mm"), n_pv)
            g.wait_ge(sem("att_ev"), att_all)
            inc(g.dma_start(out=kb_sb[:], in_=kb_n[:]), "ld_g5", 16)
            inc(g.dma_start(out=ka_sb[:], in_=ka_n[:]), "ld_g5", 16)
            inc(g.dma_start(out=sela_sb[:], in_=sela[:]), "ld_g5", 16)
            inc(g.dma_start(out=selc_sb[:], in_=selc[:]), "ld_g5", 16)
            for hd in range(8):
                inc(g.dma_start(out=wo_sb[hd][:], in_=wo_c[hd * 128:(hd + 1) * 128, :]), "ld_g5", 16)
        ld_g5_all = cnt("ld_g5")

        for j in range(5):
            par5 = j % 2

            @block.tensor
            def _(t):
                if j == 0:
                    t.wait_ge(sem("ld_g5"), ld_g5_all)
                    t.wait_ge(sem("att_ev"), att_all)
                if j >= 2:
                    t.wait_ge(sem("skfm_rdy"), j - 1)
                t.matmul(pbank[6 + par5][:, 0:256],
                         sela_sb[:, j * 128:(j + 1) * 128], kb_sb[:], start=True, stop=True)
                inc(t.matmul(pbank[6 + par5][:, 256:384],
                             selc_sb[:, j * 128:(j + 1) * 128], ka_sb[:], start=True, stop=True),
                    "kexp_mm")

            @block.vector
            def _(v):
                v.wait_ge(sem("kexp_mm"), j + 1)
                v.tensor_copy(kexp[par5][:, 0:256], pbank[6 + par5][:, 0:256])
                fence(v, v.tensor_copy(kexp[par5][:, 256:384],
                                       pbank[6 + par5][:, 256:384]), "fl_v")
                for ts in range(TC):
                    ii = v.tensor_tensor(
                        skfm[j][:, ts * 128:(ts + 1) * 128],
                        kexp[par5][:, (ts % 2) * 128:(ts % 2) * 128 + 128],
                        kexp[par5][:, 256 + ts // 2:256 + ts // 2 + 1].to_broadcast([128, 128]),
                        op=ALU.mult)
                inc(ii, "skfm_rdy")

        @block.gpsimd
        def _(g):
            for i4 in range(4):
                g.wait_ge(sem(f"det_st{i4}"), det_all[i4])
        btk5 = [None]

        @block.gpsimd
        def _(g):
            btk5[0] = g.to_reg(TK - 1)

        for tc in range(TC):
            par5 = tc % 2

            @block.tensor
            def _(t):
                if tc == 0:
                    t.wait_ge(sem("skfm_rdy"), 5)
                if tc >= 2:
                    t.wait_ge(sem("full_ev"), tc - 1)
                for j in range(5):
                    t.matmul(pbank[par5 * 2][:, 0:512], skfm[j][:, tc * 128:(tc + 1) * 128],
                             out_sk[j][:, 0:512], start=(j == 0), stop=(j == 4))
                    mmi = t.matmul(pbank[par5 * 2 + 1][:, 0:512], skfm[j][:, tc * 128:(tc + 1) * 128],
                                   out_sk[j][:, 512:1024], start=(j == 0), stop=(j == 4))
                inc(mmi, "exp_mm")

            @block.vector
            def _(v):
                v.wait_ge(sem("exp_mm"), tc + 1)
                if tc >= 2:
                    v.wait_ge(sem(f"full_st{par5}"), 16 * (tc // 2))
                v.tensor_copy(full_sb[par5][:, 0:512], pbank[par5 * 2][:, 0:512])
                v.tensor_copy(full_sb[par5][:, 512:1024], pbank[par5 * 2 + 1][:, 0:512])
                inc(v.memset(det_t[par5][:], 0.0), "full_ev")

            @block.gpsimd
            def _(g):
                g.wait_ge(sem("full_ev"), tc + 1)
                inc(g.indirect_dma_start(
                    out=det_t[par5][:], out_offset=None,
                    in_=outdet_dram[:],
                    in_offset=IndirectOffsetOnAxis(pos_cm[:, tc:tc + 1], 0),
                    bounds_check=btk5[0], oob_is_err=False), f"patch{par5}", 16)

            @block.vector
            def _(v):
                v.wait_ge(sem(f"patch{par5}"), 16 * (tc // 2 + 1))
                inc(v.copy_predicated(full_sb[par5][:],
                                      selu_cm[:, tc:tc + 1].to_broadcast([128, NQ]),
                                      det_t[par5][:]), "blend_ev")

            @block.sync
            def _(sy):
                sy.wait_ge(sem("blend_ev"), tc + 1)
                inc(sy.dma_start(out=full_dram[tc * 128:(tc + 1) * 128, :], in_=full_sb[par5][:]),
                    f"full_st{par5}", 16)
        full_st_all = (cnt("full_st0"), cnt("full_st1"))

        @block.sync
        def _(sy):
            sy.wait_ge(sem("full_st0"), full_st_all[0])
            sy.wait_ge(sem("full_st1"), full_st_all[1])
            for hd in range(8):
                inc(sy.dma_start_transpose(fullT[hd][:], full_dram[:, hd * 128:(hd + 1) * 128]),
                    "ftr_ld", 16)
        ftr_all = cnt("ftr_ld")

        n_wo = 0
        for oc in range(NF):
            for tcc in range(8):
                @block.tensor
                def _(t):
                    if n_wo == 0:
                        t.wait_ge(sem("ftr_ld"), ftr_all)
                        t.wait_ge(sem("full_ev"), TC)
                    if n_wo >= 2:
                        t.wait_ge(sem("cc_ev"), n_wo - 1)
                    for hd in range(8):
                        mmi = t.matmul(pbank[4 + n_wo % 2][:, 0:512],
                                       wo_sb[hd][:, oc * 128:(oc + 1) * 128],
                                       fullT[hd][:, tcc * 512:(tcc + 1) * 512],
                                       start=(hd == 0), stop=(hd == 7))
                    inc(mmi, "wo_mm")

                @block.vector
                def _(v):
                    v.wait_ge(sem("wo_mm"), n_wo + 1)
                    if n_wo >= 2:
                        v.wait_ge(sem(f"cc_st{n_wo % 2}"), 16 * (n_wo // 2))
                    inc(v.tensor_copy(cc_sb[n_wo % 2][:], pbank[4 + n_wo % 2][:, 0:512]), "cc_ev")

                @block.sync
                def _(sy):
                    sy.wait_ge(sem("cc_ev"), n_wo + 1)
                    inc(sy.dma_start(out=ccin[oc * 128:(oc + 1) * 128, tcc * 512:(tcc + 1) * 512],
                                     in_=cc_sb[n_wo % 2][:]), f"cc_st{n_wo % 2}", 16)
                n_wo += 1
        cc_st_all = (cnt("cc_st0"), cnt("cc_st1"))

        if dbg and phases < 6:
            @block.sync
            def _(sy):
                sy.wait_ge(sem("cc_st0"), cc_st_all[0])
                sy.wait_ge(sem("cc_st1"), cc_st_all[1])
                if "pos" in dbg:
                    inc(sy.dma_start(out=dout("pos", [128, TC], U32), in_=pos_cm[:]), "dbg_st", 16)
                if "sel" in dbg:
                    inc(sy.dma_start(out=dout("sel", [128, TC]), in_=sel_cm[:]), "dbg_st", 16)
                if "full" in dbg:
                    inc(sy.dma_start(out=dout("full", [S, NQ], BF16), in_=full_dram[:]), "dbg_st", 16)
                if "ccin" in dbg:
                    inc(sy.dma_start(out=dout("ccin", [DM, S], BF16), in_=ccin[:]), "dbg_st", 16)
                sy.wait_ge(sem("dbg_st"), cnt("dbg_st"))

        if phases < 6:
            lower_extended_insts(nc)
            return nc, dbg_outs

        @block.gpsimd
        def _(g):
            g.wait_ge(sem("cc_st0"), cc_st_all[0])
            g.wait_ge(sem("cc_st1"), cc_st_all[1])
            inc(g.collective_compute(
                "AllReduce", ALU.add,
                replica_groups=[[0, 1], [2, 3], [4, 5], [6, 7]],
                ins=[ccin.opt()], outs=[ccout.opt()]), "cc_done")
        cc_done_n = cnt("cc_done")

        ar.reset_to("persist")
        ar.top = nc.sbuf_top
        wbT_sb = [ar.alloc(f"wbT_sb{f}", [128, DM], BF16) for f in range(NF)]
        x1full = [ar.alloc(f"x1f{f}", [128, HTOK], BF16) for f in range(NF)]
        aT_sb = [ar.alloc(f"aT_sb{f}", [128, 128], BF16) for f in range(NF)]
        rwT_sb = [ar.alloc(f"rwT_sb{f}", [128, 8], BF16) for f in range(NF)]
        bw_sb = ar.alloc("bw_sb", [128, DM], BF16)
        bb_sb = ar.alloc("bb_sb", [128, NF], F32)
        sq2_off = ar.off
        sq2 = [ar.alloc(f"sq2{i}", [128, HTOK], BF16) for i in range(2)]
        x1r = [ar.alloc(f"x1r{f}", [128, 512], BF16) for f in range(NF)]
        r2bc = ar.alloc("r2bc", [128, 512], BF16)
        vm2 = ar.alloc("vm2", [1, HTOK], F32)
        vv2 = ar.alloc("vv2", [1, HTOK], F32)
        vt2 = nc.alloc_sbuf_tensor_at("vt2", [1, HTOK], F32, offset=(sq2_off + 63) // 64 * 64).ap()
        r2b = ar.alloc("r2b", [1, HTOK], BF16)
        nmr2b = ar.alloc("nmr2b", [1, HTOK], BF16)
        wbar2 = ar.alloc("wbar2", [1, DM + 136], BF16)
        rl_tm = ar.alloc("rl_tm", [128, 8], F32)
        rl_s1 = ar.alloc("rl_s1", [128, 8], F32)
        rl_s2 = ar.alloc("rl_s2", [128, 8], F32)
        z8 = ar.alloc("z8", [128, 8], F32)
        sel8f = ar.alloc("sel8f", [128, 8], F32)
        sel8fm = ar.alloc("sel8fm", [8, 512], BF16)
        esel_sb = ar.alloc("esel_sb", [8, 128], BF16)
        h_sb = ar.alloc("h_sb", [128, 512], BF16)
        hm_sb = ar.alloc("hm_sb", [128, 512], BF16)
        out_f = [ar.alloc(f"out_f{i}", [128, 512], F32) for i in range(2)]
        assert ar.off <= ar.top, f"phase6 overflow: {ar.off}"

        @block.gpsimd
        def _(g):
            g.wait_ge(sem("cc_ev"), n_wo)
            for f in range(NF):
                inc(g.dma_start(out=wbT_sb[f][:], in_=wbT[f * 128:(f + 1) * 128, :]), "ld_g6", 16)
                inc(g.dma_start(out=aT_sb[f][:], in_=aT[f * 128:(f + 1) * 128, :]), "ld_g6", 16)
                inc(g.dma_start(out=rwT_sb[f][:], in_=rwT[f * 128:(f + 1) * 128, :]), "ld_g6", 16)
            inc(g.dma_start(out=bw_sb[:], in_=bw_er[:]), "ld_g6", 16)
            inc(g.dma_start(out=esel_sb[:], in_=esel[:]), "ld_g6", 16)
        ld_g6_all = cnt("ld_g6")

        @block.sync
        def _(sy):
            sy.wait_ge(sem("cc_ev"), n_wo)
            inc(sy.dma_start(out=bb_sb[:], in_=b_base_cm[:]), "ld_s6", 16)
        ld_s6_all = cnt("ld_s6")

        @block.vector
        def _(v):
            v.wait_ge(sem("ld_g6"), ld_g6_all)
            v.memset(z8[:], 0.0)
            inc(v.tensor_scalar(bw_sb[:], bw_sb[:], 2.0, None, op0=ALU.mult), "bw_sc")

        @block.sync
        def _(sy):
            sy.wait_ge(sem("cc_done"), cc_done_n)
            pid = nc.partition_id(engines=[mybir.EngineType.SP])
            half_off = pid % 2 * HTOK
            for f in range(NF):
                inc(sy.dma_start(out=x1full[f][:],
                                 in_=ccout[f * 128:(f + 1) * 128, bass.ds(half_off, HTOK)]),
                    "ld_cc", 16)
        ld_cc_all = cnt("ld_cc")

        @block.gpsimd
        def _(g):
            g.wait_ge(sem("cc_done"), cc_done_n)
            for f in range(NF):
                i = f % 2
                if f >= 2:
                    g.wait_ge(sem("x1_used"), f - 1)
                inc(g.dma_start(out=sq2[i][:], in_=xTh[f * 128:(f + 1) * 128, :]), f"ld_xh{i}", 16)

        @block.vector
        def _(v):
            v.wait_ge(sem("ld_cc"), ld_cc_all)
            for f in range(NF):
                i = f % 2
                v.wait_ge(sem(f"ld_xh{i}"), 16 * (f // 2 + 1))
                inc(v.tensor_add(x1full[f][:], x1full[f][:], sq2[i][:]), "x1_rdy")

        @block.scalar
        def _(s):
            for f in range(NF):
                i = f % 2
                s.wait_ge(sem("x1_rdy"), f + 1)
                inc(s.activation(sq2[i][:], x1full[f][:], AF.Square), "sq6")

        psum_sum2 = [pbank[i][0:1, :] for i in range(4)]
        psum_sq2 = [pbank[4 + i][0:1, :] for i in range(4)]

        @block.tensor
        def _(t):
            t.wait_ge(sem("cc_ev"), n_wo)
            for f in range(NF):
                i = f % 2
                t.wait_ge(sem("sq6"), f + 1)
                fi, la = f == 0, f == NF - 1
                for w in range(4):
                    t.matmul(psum_sum2[w], ones_col_b[:], x1full[f][:, w * 512:(w + 1) * 512],
                             start=fi, stop=la)
                    mmi = t.matmul(psum_sq2[w], ones_col_b[:], sq2[i][:, w * 512:(w + 1) * 512],
                                   start=fi, stop=la)
                inc(mmi, "x1_used")

        @block.vector
        def _(v):
            v.wait_ge(sem("x1_used"), NF)
            for w in range(4):
                v.tensor_scalar(vm2[0:1, w * 512:(w + 1) * 512], psum_sum2[w], 1.0 / DM,
                                None, op0=ALU.mult)
                ii = v.tensor_scalar(vv2[0:1, w * 512:(w + 1) * 512], psum_sq2[w], 1.0 / DM,
                                     None, op0=ALU.mult)
            fence(v, ii, "fl_v")
            fence(v, v.tensor_mul(vt2[:], vm2[:], vm2[:]), "fl_v")
            inc(v.tensor_sub(vv2[:], vv2[:], vt2[:]), "ln2_var")
            v.wait_ge(sem("rstd2"), 1)
            inc(v.scalar_tensor_tensor(nmr2b[:], vm2[:], -1.0, vv2[:],
                                       op0=ALU.mult, op1=ALU.mult), "ln2_rdy")
            fence(v, v.tensor_copy(r2b[:], vv2[:]), "fl_v")
        ln2_fl = cnt("fl_v")

        @block.scalar
        def _(s):
            s.wait_ge(sem("ln2_var"), 1)
            fence(s, s.activation(vv2[:], vv2[:], AF.Ln, bias=eps1_row[:]), "fl_s")
            inc(s.activation(vv2[:], vv2[:], AF.Exp, scale=-0.5), "rstd2")

        @block.tensor
        def _(t):
            t.wait_ge(sem("ln2_var"), 1)
            for f in range(NF):
                fi, la = f == 0, f == NF - 1
                for w in range(4):
                    t.matmul(pbank[w][0:1, 0:512], ones_col_b[:], wbT_sb[f][:, w * 512:(w + 1) * 512],
                             start=fi, stop=la)
                t.matmul(pbank[4][0:1, 0:128], ones_col_b[:], aT_sb[f][:], start=fi, stop=la)
                mmi = t.matmul(pbank[5][0:1, 0:8], ones_col_b[:], rwT_sb[f][:], start=fi, stop=la)
            inc(mmi, "wbar2_mm")

        @block.vector
        def _(v):
            v.wait_ge(sem("wbar2_mm"), 1)
            for w in range(4):
                v.tensor_copy(wbar2[0:1, w * 512:(w + 1) * 512], pbank[w][0:1, 0:512])
            v.tensor_copy(wbar2[0:1, DM:DM + 128], pbank[4][0:1, 0:128])
            inc(v.tensor_copy(wbar2[0:1, DM + 128:DM + 136], pbank[5][0:1, 0:8]), "wbar2_ev")

        n_fo = 0
        n_rt = 0
        for tw in range(4):
            t0 = tw * 512

            @block.tensor
            def _(t):
                if tw == 0:
                    t.wait_ge(sem("ln2_rdy"), 1)
                    t.wait_ge(sem("wbar2_ev"), 1)
                    t.wait_ge(sem("fl_v"), ln2_fl)
                else:
                    t.wait_ge(sem("r2cp"), tw)
                inc(t.matmul(pbank[5][:, 0:512], ones_row_b[:], r2b[0:1, t0:t0 + 512],
                             start=True, stop=True), "r2mm")

            @block.vector
            def _(v):
                v.wait_ge(sem("r2mm"), tw + 1)
                inc(v.tensor_copy(r2bc[:], pbank[5][:, 0:512]), "r2cp")
                fence(v, v.memset(rl_s2[0:1, 0:1], 0.0), "fl_v")

            @block.vector
            def _(v):
                if tw >= 1:
                    v.wait_ge(sem("out_st0"), 16 * (n_fo // 2))
                    v.wait_ge(sem("out_st1"), 16 * (n_fo // 2))
                for f in range(NF):
                    ii = v.tensor_mul(x1r[f][:], x1full[f][:, t0:t0 + 512], r2bc[:])
                inc(ii, "x1r_rdy")
            x1r_n = cnt("x1r_rdy")

            for tt in range(4):
                @block.tensor
                def _(t):
                    t.wait_ge(sem("x1r_rdy"), x1r_n)
                    if n_rt >= 1:
                        t.wait_ge(sem("rt_ev"), n_rt)
                    for f in range(NF):
                        t.matmul(pbank[6][:, 0:8], x1r[f][:, tt * 128:(tt + 1) * 128],
                                 rwT_sb[f][:], start=(f == 0), stop=False)
                    inc(t.matmul(pbank[6][:, 0:8], nmr2b[0:1, t0 + tt * 128:t0 + (tt + 1) * 128],
                                 wbar2[0:1, DM + 128:DM + 136], start=False, stop=True), "rt_mm")

                @block.vector
                def _(v):
                    v.wait_ge(sem("rt_mm"), n_rt + 1)
                    fence(v, v.tensor_copy(rl_tm[:], pbank[6][:, 0:8]), "fl_v")
                    fence(v, v.reduce_max(rl_s1[:, 0:1], rl_tm[:], axis=AX), "fl_v")
                    fence(v, v.tensor_scalar(rl_s2[:], rl_tm[:], rl_s1[:, 0:1], None,
                                             op0=ALU.is_ge), "fl_v")
                    fence(v, v.tensor_tensor_scan(rl_s1[:], rl_s2[:], z8[:], 0.0,
                                                  op0=ALU.add, op1=ALU.add), "fl_v")
                    fence(v, v.tensor_scalar(rl_s1[:], rl_s1[:], 1.0, None, op0=ALU.is_le), "fl_v")
                    inc(v.tensor_mul(sel8f[:], rl_s2[:], rl_s1[:]), "rt_sel")

                @block.tensor
                def _(t):
                    t.wait_ge(sem("rt_sel"), n_rt + 1)
                    inc(t.transpose(pbank[7][0:8, 0:128], sel8f[:], ident_f[:]), "rt_tr")

                @block.vector
                def _(v):
                    v.wait_ge(sem("rt_tr"), n_rt + 1)
                    inc(v.tensor_copy(sel8fm[0:8, tt * 128:(tt + 1) * 128],
                                      pbank[7][0:8, 0:128]), "rt_ev")
                n_rt += 1

            @block.tensor
            def _(t):
                t.wait_ge(sem("rt_ev"), n_rt)
                for f in range(NF):
                    t.matmul(pbank[6][:, 0:512], aT_sb[f][:], x1r[f][:], start=(f == 0), stop=False)
                inc(t.matmul(pbank[6][:, 0:512], wbar2[0:1, DM:DM + 128],
                             nmr2b[0:1, t0:t0 + 512], start=False, stop=True), "h_mm")
                inc(t.matmul(pbank[7][:, 0:512], esel_sb[:], sel8fm[:], start=True, stop=True),
                    "mask_mm")

            @block.vector
            def _(v):
                v.wait_ge(sem("h_mm"), tw + 1)
                fence(v, v.tensor_copy(h_sb[:], pbank[6][:, 0:512]), "fl_v")
                v.wait_ge(sem("mask_mm"), tw + 1)
                inc(v.tensor_mul(hm_sb[:], h_sb[:], pbank[7][:, 0:512]), "hm_rdy")

            for oc in range(NF):
                @block.tensor
                def _(t):
                    t.wait_ge(sem("hm_rdy"), tw + 1)
                    if tw == 0 and oc == 0:
                        t.wait_ge(sem("bw_sc"), 1)
                    if n_fo >= 2:
                        t.wait_ge(sem("fo_ev"), n_fo - 1)
                    for f in range(NF):
                        t.matmul(pbank[n_fo % 2][:, 0:512], wbT_sb[f][:, oc * 128:(oc + 1) * 128],
                                 x1r[f][:], start=(f == 0), stop=False)
                    t.matmul(pbank[n_fo % 2][:, 0:512], wbar2[0:1, oc * 128:(oc + 1) * 128],
                             nmr2b[0:1, t0:t0 + 512], start=False, stop=False)
                    inc(t.matmul(pbank[n_fo % 2][:, 0:512], bw_sb[:, oc * 128:(oc + 1) * 128],
                                 hm_sb[:], start=False, stop=True), "fo_mm")

                @block.vector
                def _(v):
                    v.wait_ge(sem("fo_mm"), n_fo + 1)
                    if n_fo == 0:
                        v.wait_ge(sem("ld_s6"), ld_s6_all)
                    if n_fo >= 2:
                        v.wait_ge(sem(f"out_st{n_fo % 2}"), 16 * (n_fo // 2))
                    inc(v.scalar_tensor_tensor(out_f[n_fo % 2][:], pbank[n_fo % 2][:, 0:512],
                                               bb_sb[:, oc:oc + 1], x1full[oc][:, t0:t0 + 512],
                                               op0=ALU.add, op1=ALU.add), "fo_ev")

                @block.sync
                def _(sy):
                    sy.wait_ge(sem("fo_ev"), n_fo + 1)
                    inc(sy.dma_start(out=out_fm[oc * 128:(oc + 1) * 128, t0:t0 + 512],
                                     in_=out_f[n_fo % 2][:]), f"out_st{n_fo % 2}", 16)
                n_fo += 1

        @block.sync
        def _(sy):
            sy.wait_ge(sem("out_st0"), cnt("out_st0"))
            sy.wait_ge(sem("out_st1"), cnt("out_st1"))

    lower_extended_insts(nc)
    return nc, dbg_outs


def make_in_maps(inputs):
    x = np.ascontiguousarray(inputs["L0_attn_hidden_states"], dtype=np.float32)
    cos = np.ascontiguousarray(np.asarray(inputs["L0_attn_cos"], dtype=np.float32)[0, :S, 0, :])
    sin = np.ascontiguousarray(np.asarray(inputs["L0_attn_sin"], dtype=np.float32)[0, :S, 0, :])
    wq = np.asarray(inputs["L0_attn_wq"], dtype=np.float32)
    wk = np.asarray(inputs["L0_attn_wk"], dtype=np.float32)
    wv = np.asarray(inputs["L0_attn_wv"], dtype=np.float32)
    wo = np.asarray(inputs["L0_attn_wo"], dtype=np.float32)
    qnw = np.asarray(inputs["L0_attn_q_norm_w"], dtype=np.float32).reshape(-1)
    knw = np.asarray(inputs["L0_attn_k_norm_w"], dtype=np.float32).reshape(-1)
    w1 = np.asarray(inputs["L0_attn_imp_w1"], dtype=np.float32)
    b1 = np.asarray(inputs["L0_attn_imp_b1"], dtype=np.float32).reshape(1, 64)
    w2 = np.asarray(inputs["L0_attn_imp_w2"], dtype=np.float32).reshape(1, 64)
    b2 = np.asarray(inputs["L0_attn_imp_b2"], dtype=np.float32).reshape(1, 1)
    ka = np.asarray(inputs["L0_attn_kron_a"], dtype=np.float32)
    kb = np.asarray(inputs["L0_attn_kron_b"], dtype=np.float32)
    ss = np.asarray(inputs["L0_attn_sketch_scale"], dtype=np.float32).reshape(1, 1)
    Wb = np.asarray(inputs["L0_ffn_W_base"], dtype=np.float32)
    bb = np.asarray(inputs["L0_ffn_b_base"], dtype=np.float32).reshape(-1)
    rw = np.asarray(inputs["L0_ffn_router_W"], dtype=np.float32)
    A = np.asarray(inputs["L0_ffn_A"], dtype=np.float32)
    Bw = np.asarray(inputs["L0_ffn_Bw"], dtype=np.float32)

    ident = np.eye(128, dtype=np.float32)
    iota = np.arange(S, dtype=np.uint32).reshape(TC, 128).T.copy()
    sela_m = np.zeros((32, 5 * 128), np.float32)
    selc_m = np.zeros((20, 5 * 128), np.float32)
    for j in range(5):
        for p in range(128):
            sv = 128 * j + p
            sela_m[sv // 20, j * 128 + p] = 1.0
            selc_m[sv % 20, j * 128 + p] = 1.0
    kaT_row = np.zeros((1, 320), np.float32)
    for jp in range(16):
        kaT_row[0, jp * 20:(jp + 1) * 20] = ka[:, jp]
    es = np.zeros((8, 128), np.float32)
    for e in range(8):
        es[e, e * 16:(e + 1) * 16] = 1.0

    maps = []
    for core in range(8):
        b, hf = core // 2, core % 2
        m = {
            "xT": np.ascontiguousarray(x[b].T),
            "xTh": np.ascontiguousarray(x[b].T[:, hf * HTOK:(hf + 1) * HTOK]),
            "cos_tm": cos, "sin_tm": sin,
            "wq_c": np.ascontiguousarray(wq[:, hf * NQ:(hf + 1) * NQ]),
            "wk_c": np.ascontiguousarray(wk[:, hf * NKVD:(hf + 1) * NKVD]),
            "wv_c": np.ascontiguousarray(wv[:, hf * NKVD:(hf + 1) * NKVD]),
            "imp_w1": w1, "imp_w2r": w2, "imp_b1r": b1, "imp_b2": b2,
            "qnw_col": qnw.reshape(128, 1), "knw_col": knw.reshape(128, 1),
            "qnw_row": qnw.reshape(1, 128), "knw_row": knw.reshape(1, 128),
            "kbT": np.ascontiguousarray(kb.T), "kb_n": kb, "ka_n": ka,
            "kaT_row": kaT_row, "sela": sela_m, "selc": selc_m,
            "wo_c": np.ascontiguousarray(wo[hf * NQ:(hf + 1) * NQ, :]),
            "wbT": np.ascontiguousarray(Wb.T),
            "b_base_cm": np.ascontiguousarray(bb.reshape(NF, 128).T),
            "rwT": np.ascontiguousarray(rw.T),
            "aT": np.ascontiguousarray(A.reshape(128, DM).T),
            "bw_er": np.ascontiguousarray(Bw.transpose(0, 2, 1).reshape(128, DM)),
            "esel": es, "ident": ident, "iota_cm": iota, "ss_in": ss,
        }
        maps.append({k: np.ascontiguousarray(v) for k, v in m.items()})
    return maps


_CACHE = {}


def kernel(**inputs):
    if "nc" not in _CACHE:
        _CACHE["nc"] = build_program(phases=7)[0]
    nc = _CACHE["nc"]
    maps = make_in_maps(inputs)
    res = run_bass_kernel_spmd(nc, maps, core_ids=list(range(8)))
    out = np.empty((B, S, DM), np.float32)
    for core in range(8):
        b, hf = core // 2, core % 2
        out[b, hf * HTOK:(hf + 1) * HTOK, :] = np.asarray(res.results[core]["out_fm"]).T
    return out


# version 05
# speedup vs baseline: 1.0103x; 1.0103x over previous
import numpy as np
import concourse.bass as bass
import concourse.mybir as mybir
from concourse.bass_utils import run_bass_kernel_spmd
from concourse.bass_core_rust import IndirectOffsetOnAxis
from concourse.library_overlay import lower_extended_insts

F32 = mybir.dt.float32
BF16 = mybir.dt.bfloat16
U32 = mybir.dt.uint32
AF = mybir.ActivationFunctionType
ALU = mybir.AluOpType
AX = mybir.AxisListType.X

B, S, DM = 4, 4096, 2048
H, KVH, D = 16, 4, 128
SKN, TK = 640, 2048
NH, NKV = 8, 2
TC = S // 128
NF = DM // 128
NQ = NH * D
NKVD = NKV * D
CT = TK + SKN
NCT = CT // 128
NTK = TK // 128
HTOK = S // 2
HTC = HTOK // 128
LOGSHIFT = float(np.log(S / SKN))
ISQD = float(1.0 / np.sqrt(D))
EPS1, EPS2 = 1e-5, 1e-6
BISECT = 34
BIGPOS = 60000.0

_SZ = {F32: 4, BF16: 2, U32: 4}


class Arena:
    def __init__(self, nc):
        self.nc = nc
        self.off = (nc.sbuf_base + 63) // 64 * 64
        self.top = nc.sbuf_top
        self.marks = {}

    def alloc(self, name, shape, dtype, align=64):
        nbytes = int(np.prod(shape[1:])) * _SZ[dtype]
        self.off = (self.off + align - 1) // align * align
        t = self.nc.alloc_sbuf_tensor_at(name, list(shape), dtype, offset=self.off)
        self.off += nbytes
        assert self.off <= self.top, f"SBUF overflow at {name}: {self.off}"
        return t.ap()

    def mark(self, name):
        self.marks[name] = self.off

    def reset_to(self, name):
        self.off = self.marks[name]


def build_program(dbg=(), phases=99):
    dbg = set(dbg)
    nc = bass.Bass("TRN2", target_bir_lowering=False)

    def din(name, shape, dtype=F32):
        return nc.dram_tensor(name, list(shape), dtype, kind="ExternalInput").ap()

    xT = din("xT", [DM, S])
    xTh = din("xTh", [DM, HTOK])
    cos_tm = din("cos_tm", [S, D])
    sin_tm = din("sin_tm", [S, D])
    wq_c = din("wq_c", [DM, NQ])
    wk_c = din("wk_c", [DM, NKVD])
    wv_c = din("wv_c", [DM, NKVD])
    imp_w1 = din("imp_w1", [DM, 64])
    imp_w2r = din("imp_w2r", [1, 64])
    imp_b1r = din("imp_b1r", [1, 64])
    imp_b2 = din("imp_b2", [1, 1])
    qnw_col = din("qnw_col", [128, 1])
    knw_col = din("knw_col", [128, 1])
    qnw_row = din("qnw_row", [1, 128])
    knw_row = din("knw_row", [1, 128])
    kbT = din("kbT", [256, 32])
    kb_n = din("kb_n", [32, 256])
    ka_n = din("ka_n", [20, 128])
    kaT_row = din("kaT_row", [1, 320])
    sela = din("sela", [32, 5 * 128])
    selc = din("selc", [20, 5 * 128])
    wo_c = din("wo_c", [NQ, DM])
    wbT = din("wbT", [DM, DM])
    b_base_cm = din("b_base_cm", [128, NF])
    rwT = din("rwT", [DM, 8])
    aT = din("aT", [DM, 128])
    bw_er = din("bw_er", [128, DM])
    esel = din("esel", [8, 128])
    ident = din("ident", [128, 128])
    iota_cm = din("iota_cm", [128, TC], U32)
    ss_in = din("ss_in", [1, 1])

    out_fm = nc.dram_tensor("out_fm", [DM, HTOK], F32, kind="ExternalOutput").ap()
    qd_dram = nc.dram_tensor("qd_dram", [TK, NQ], BF16).ap()
    kd_dram = nc.dram_tensor("kd_dram", [TK, NKVD], BF16).ap()
    vd_dram = nc.dram_tensor("vd_dram", [TK, NKVD], BF16).ap()
    cosd_dram = nc.dram_tensor("cosd_dram", [TK, D], BF16).ap()
    sind_dram = nc.dram_tensor("sind_dram", [TK, D], BF16).ap()
    outdet_dram = nc.dram_tensor("outdet_dram", [TK, NQ], BF16).ap()
    full_dram = nc.dram_tensor("full_dram", [S, NQ], BF16).ap()
    ccin = nc.dram_tensor("ccin", [DM, S], BF16).ap()
    ccout = nc.dram_tensor("ccout", [DM, S], BF16).ap()

    dbg_outs = {}

    def dout(name, shape, dtype=F32):
        ap = nc.dram_tensor("dbg_" + name, list(shape), dtype, kind="ExternalOutput").ap()
        dbg_outs[name] = "dbg_" + name
        return ap

    ar = Arena(nc)
    ident_f = ar.alloc("ident_f", [128, 128], F32)
    ident_b = ar.alloc("ident_b", [128, 128], BF16)
    ones_col_b = ar.alloc("ones_col_b", [128, 1], BF16)
    ones_col_f = ar.alloc("ones_col_f", [128, 1], F32)
    ones_row_b = ar.alloc("ones_row_b", [1, 128], BF16)
    ones_row_f = ar.alloc("ones_row_f", [1, 128], F32)
    small_f = ar.alloc("small_f", [1, 128], F32)
    w2bc = ar.alloc("w2bc", [128, 64], F32)
    b2bias = ar.alloc("b2bias", [128, 1], F32)
    b1row_b = ar.alloc("b1row_b", [1, 64], BF16)
    qw_col = ar.alloc("qw_col", [128, 1], F32)
    kw_col = ar.alloc("kw_col", [128, 1], F32)
    qw_rows = ar.alloc("qw_rows", [1, 256], F32)
    kw_rows = ar.alloc("kw_rows", [1, 256], F32)
    ssbc = ar.alloc("ssbc", [128, 1], F32)
    eps2_col = ar.alloc("eps2_col", [128, 1], F32)
    eps1_row = ar.alloc("eps1_row", [1, 1], F32)
    qwss = ar.alloc("qwss", [128, 1], F32)
    kwss = ar.alloc("kwss", [128, 1], F32)
    kabc = ar.alloc("kabc", [128, 320], BF16)
    kbT_sb = ar.alloc("kbT_sb", [128, 64], BF16)
    logits_cm = ar.alloc("logits_cm", [128, TC], F32)
    weights_cm = ar.alloc("weights_cm", [128, TC], F32)
    restw_cm = ar.alloc("restw_cm", [128, TC], F32)
    sel_cm = ar.alloc("sel_cm", [128, TC], F32)
    pos_cm = ar.alloc("pos_cm", [128, TC], U32)
    selu_cm = ar.alloc("selu_cm", [128, TC], U32)
    ar.mark("persist")

    r1b = ar.alloc("r1b", [1, S], BF16)
    nmr1b = ar.alloc("nmr1b", [1, S], BF16)
    wbar = ar.alloc("wbar", [1, 1600], BF16)
    wrhs = [ar.alloc(f"wrhs{f}", [128, 1600], BF16) for f in range(NF)]
    rmsq = [ar.alloc(f"rmsq{i}", [128, 8], F32) for i in range(2)]
    rmsk = [ar.alloc(f"rmsk{i}", [128, 8], F32) for i in range(2)]
    tanh_sb = [ar.alloc(f"tanh{i}", [128, 64], F32) for i in range(2)]
    scr_act = ar.alloc("scr_act", [128, 20 * 128], BF16)
    scr_ttr = ar.alloc("scr_ttr", [128, 2 * 64], F32)
    ar.mark("p1ab")
    vecmu = ar.alloc("vecmu", [1, 1024], F32)
    vec1 = ar.alloc("vec1", [1, 1024], F32)
    vec2 = ar.alloc("vec2", [1, 1024], F32)
    xstat = [ar.alloc(f"xstat{i}", [128, 1024], BF16) for i in range(2)]
    sq_b = [ar.alloc(f"sq_b{i}", [128, 1024], BF16) for i in range(2)]
    ar.reset_to("p1ab")
    xw = [ar.alloc(f"xw{f}", [128, 512], BF16) for f in range(NF)]
    rbc_b = ar.alloc("rbc_b", [128, 512], BF16)
    ar.mark("p1end")
    qhat = [ar.alloc(f"qhat{t}", [128, NQ], BF16) for t in range(TC)]
    khat = [ar.alloc(f"khat{t}", [128, NKVD], BF16) for t in range(TC)]
    vhat = [ar.alloc(f"vhat{t}", [128, NKVD], BF16) for t in range(TC)]
    ar.mark("qkv_end")

    pbank = [nc.alloc_psum_tensor(f"pb{i}", [128, 512], F32).ap() for i in range(8)]

    sem_cnt, sem_h = {}, {}

    def sem(name):
        if name not in sem_h:
            sem_h[name] = nc.alloc_semaphore(name)
            sem_cnt[name] = 0
        return sem_h[name]

    def inc(instr, name, k=1):
        instr.then_inc(sem(name), k)
        sem_cnt[name] += k
        return instr

    def cnt(name):
        sem(name)
        return sem_cnt[name]

    def wchain(e, name):
        e.wait_ge(sem(name), cnt(name))



    def fence(eng, instr, tag):
        inc(instr, tag)
        eng.wait_ge(sem(tag), cnt(tag))

    with nc.Block() as block:
        @block.gpsimd
        def _(g):
            inc(g.dma_start(out=ident_b[:], in_=ident[:]), "ld_g", 16)
            inc(g.dma_start(out=kbT_sb[:, 0:32], in_=kbT[0:128, :]), "ld_g", 16)
            inc(g.dma_start(out=kbT_sb[:, 32:64], in_=kbT[128:256, :]), "ld_g", 16)
            inc(g.dma_start(out=b1row_b[:], in_=imp_b1r[:]), "ld_g", 16)
            for f in range(NF):
                inc(g.dma_start(out=wrhs[f][:, 0:1024], in_=wq_c[f * 128:(f + 1) * 128, :]), "ld_g", 16)
                inc(g.dma_start(out=wrhs[f][:, 1024:1280], in_=wk_c[f * 128:(f + 1) * 128, :]), "ld_g", 16)
                inc(g.dma_start(out=wrhs[f][:, 1280:1536], in_=wv_c[f * 128:(f + 1) * 128, :]), "ld_g", 16)
                inc(g.dma_start(out=wrhs[f][:, 1536:1600], in_=imp_w1[f * 128:(f + 1) * 128, :]), "ld_g", 16)
        ld_g_p0 = cnt("ld_g")

        @block.sync
        def _(sy):
            inc(sy.dma_start(out=ident_f[:], in_=ident[:]), "ld_s", 16)
            inc(sy.dma_start(out=qw_col[:], in_=qnw_col[:]), "ld_s", 16)
            inc(sy.dma_start(out=kw_col[:], in_=knw_col[:]), "ld_s", 16)
            inc(sy.dma_start(out=qw_rows[0:1, 0:128], in_=qnw_row[:]), "ld_s", 16)
            inc(sy.dma_start(out=kw_rows[0:1, 0:128], in_=knw_row[:]), "ld_s", 16)
            inc(sy.dma_start(out=small_f[0:1, 0:64], in_=imp_w2r[:]), "ld_s", 16)
            inc(sy.dma_start(out=small_f[0:1, 64:65], in_=imp_b2[:]), "ld_s", 16)
            inc(sy.dma_start(out=small_f[0:1, 65:66], in_=ss_in[:]), "ld_s", 16)
        ld_s_p0 = cnt("ld_s")

        @block.vector
        def _(v):
            v.memset(eps2_col[:], EPS2)
            v.memset(eps1_row[:], EPS1)
            v.memset(ones_col_b[:], 1.0)
            v.memset(ones_col_f[:], 1.0)
            v.memset(ones_row_b[:], 1.0)
            v.memset(ones_row_f[:], 1.0)
            v.wait_ge(sem("ld_s"), ld_s_p0)
            v.tensor_copy(qw_rows[0:1, 128:192], qw_rows[0:1, 64:128])
            v.tensor_copy(qw_rows[0:1, 192:256], qw_rows[0:1, 0:64])
            v.tensor_copy(kw_rows[0:1, 128:192], kw_rows[0:1, 64:128])
            inc(v.tensor_copy(kw_rows[0:1, 192:256], kw_rows[0:1, 0:64]), "c0v")
        c0v1 = cnt("c0v")

        @block.tensor
        def _(t):
            t.wait_ge(sem("c0v"), c0v1)
            t.matmul(pbank[5][:, 0:64], ones_row_f[:], small_f[0:1, 0:64], start=True, stop=True)
            inc(t.matmul(pbank[5][:, 64:66], ones_row_f[:], small_f[0:1, 64:66], start=True, stop=True),
                "c0t")
        c0t1 = cnt("c0t")

        @block.vector
        def _(v):
            v.wait_ge(sem("c0t"), c0t1)
            v.tensor_copy(w2bc[:], pbank[5][:, 0:64])
            v.tensor_scalar(b2bias[:], pbank[5][:, 64:65], -LOGSHIFT, None, op0=ALU.add)
            fence(v, v.tensor_copy(ssbc[:], pbank[5][:, 65:66]), "fl_v")
            v.tensor_mul(qwss[:], qw_col[:], ssbc[:])
            inc(v.tensor_mul(kwss[:], kw_col[:], ssbc[:]), "c0v")
        c0v2 = cnt("c0v")

        psum_sum = [pbank[i][0:1, :] for i in range(2)]
        psum_sq = [pbank[2 + i][0:1, :] for i in range(2)]

        @block.gpsimd
        def _(g):
            for k in range(4 * NF):
                qtr, f = divmod(k, NF)
                i = k % 2
                if k >= 2:
                    g.wait_ge(sem("stat_used"), k - 1)
                inc(g.dma_start(out=xstat[i][:],
                                in_=xT[f * 128:(f + 1) * 128, qtr * 1024:(qtr + 1) * 1024]),
                    f"ld_x{i}", 16)

        @block.scalar
        def _(s):
            for k in range(4 * NF):
                i = k % 2
                s.wait_ge(sem(f"ld_x{i}"), (k // 2 + 1) * 16)
                if k >= 2:
                    s.wait_ge(sem("stat_used"), k - 1)
                inc(s.activation(sq_b[i][:], xstat[i][:], AF.Square), "sq1")
                if (k + 1) % NF == 0:
                    qtr = (k + 1) // NF - 1
                    s.wait_ge(sem("stats_evac"), qtr + 1)
                    fence(s, s.activation(vec2[:], vec2[:], AF.Ln, bias=eps2_col[0:1, 0:1] if False else eps1_row[:]), "fl_s")
                    inc(s.activation(vec2[:], vec2[:], AF.Exp, scale=-0.5), "rstd1")

        @block.tensor
        def _(t):
            t.wait_ge(sem("c0v"), c0v2)
            for k in range(4 * NF):
                qtr, f = divmod(k, NF)
                i = k % 2
                t.wait_ge(sem("sq1"), k + 1)
                t.wait_ge(sem(f"ld_x{i}"), (k // 2 + 1) * 16)
                if f == 0 and qtr >= 1:
                    t.wait_ge(sem("stats_evac"), qtr)
                fi, la = f == 0, f == NF - 1
                for w in range(2):
                    t.matmul(psum_sum[w], ones_col_b[:], xstat[i][:, w * 512:(w + 1) * 512],
                             start=fi, stop=la)
                    mmi = t.matmul(psum_sq[w], ones_col_b[:], sq_b[i][:, w * 512:(w + 1) * 512],
                                   start=fi, stop=la)
                inc(mmi, "stat_used")

        @block.vector
        def _(v):
            for qtr in range(4):
                v.wait_ge(sem("stat_used"), NF * (qtr + 1))
                sl = slice(qtr * 1024, (qtr + 1) * 1024)
                for w in range(2):
                    v.tensor_scalar(vecmu[0:1, w * 512:(w + 1) * 512],
                                    psum_sum[w], 1.0 / DM, None, op0=ALU.mult)
                    ii = v.tensor_scalar(vec1[0:1, w * 512:(w + 1) * 512],
                                         psum_sq[w], 1.0 / DM, None, op0=ALU.mult)
                fence(v, ii, "fl_v")
                fence(v, v.tensor_mul(vec2[:], vecmu[:], vecmu[:]), "fl_v")
                inc(v.scalar_tensor_tensor(vec2[:], vec2[:], -1.0, vec1[:],
                                            op0=ALU.mult, op1=ALU.add), "stats_evac")
                v.wait_ge(sem("rstd1"), qtr + 1)
                inc(v.scalar_tensor_tensor(nmr1b[0:1, sl], vecmu[:], -1.0, vec2[:],
                                           op0=ALU.mult, op1=ALU.mult), "ln1_rdy")
                fence(v, v.tensor_copy(r1b[0:1, sl], vec2[:]), "fl_v")

        fl_v_after_stats = cnt("fl_v")

        @block.tensor
        def _(t):
            t.wait_ge(sem("ld_g"), ld_g_p0)
            t.wait_ge(sem("stats_evac"), 4)
            for f in range(NF):
                fi, la = f == 0, f == NF - 1
                for w in range(3):
                    t.matmul(pbank[w][0:1, 0:512], ones_col_b[:], wrhs[f][:, w * 512:(w + 1) * 512],
                             start=fi, stop=la)
                mmi = t.matmul(pbank[3][0:1, 0:64], ones_col_b[:], wrhs[f][:, 1536:1600],
                               start=fi, stop=la)
            inc(mmi, "tks")

        @block.vector
        def _(v):
            wchain(v, "tks")
            for w in range(3):
                v.tensor_copy(wbar[0:1, w * 512:(w + 1) * 512], pbank[w][0:1, 0:512])
            inc(v.tensor_copy(wbar[0:1, 1536:1600], pbank[3][0:1, 0:64]), "tks")

        for wdw in range(8):
            tok0 = wdw * 512

            @block.gpsimd
            def _(g):
                if wdw == 0:
                    g.wait_ge(sem("stat_used"), 4 * NF)
                    g.wait_ge(sem("fl_v"), fl_v_after_stats)
                else:
                    g.wait_ge(sem("proj_mm"), 4 * wdw)
                for f in range(NF):
                    inc(g.dma_start(out=xw[f][:],
                                    in_=xT[f * 128:(f + 1) * 128, tok0:tok0 + 512]),
                        "ld_xw", 16)
            ld_xw_w = cnt("ld_xw")

            @block.tensor
            def _(t):
                if wdw == 0:
                    wchain(t, "tks")
                else:
                    t.wait_ge(sem("proj_evac"), 4 * wdw - 1)
                t.wait_ge(sem("ln1_rdy"), wdw // 2 + 1)
                inc(t.matmul(pbank[0][:, 0:512], ones_row_b[:],
                             r1b[0:1, tok0:tok0 + 512], start=True, stop=True), "rbc_mm")

            @block.vector
            def _(v):
                v.wait_ge(sem("rbc_mm"), wdw + 1)
                fence(v, v.tensor_copy(rbc_b[:], pbank[0][:, 0:512]), "fl_v")
                v.wait_ge(sem("ld_xw"), ld_xw_w)
                for f in range(NF):
                    ii = v.tensor_mul(xw[f][:], xw[f][:], rbc_b[:])
                inc(ii, "xr_rdy")
            xr_w = cnt("xr_rdy")

            @block.tensor
            def _(t):
                t.wait_ge(sem("xr_rdy"), xr_w)
                for tci in range(4):
                    tc = wdw * 4 + tci
                    c = slice(tci * 128, (tci + 1) * 128)
                    par = tc % 2
                    qp0, qp1, kvp = pbank[4 * par], pbank[4 * par + 1], pbank[4 * par + 2]
                    imp = pbank[4 * par + 3][:, 0:64]
                    if tc >= 2:
                        t.wait_ge(sem("proj_evac"), tc - 1)
                    for f in range(NF):
                        fi = f == 0
                        t.matmul(qp0[:, 0:512], xw[f][:, c], wrhs[f][:, 0:512], start=fi, stop=False)
                        t.matmul(qp1[:, 0:512], xw[f][:, c], wrhs[f][:, 512:1024], start=fi, stop=False)
                        t.matmul(kvp[:, 0:512], xw[f][:, c], wrhs[f][:, 1024:1536], start=fi, stop=False)
                        t.matmul(imp, xw[f][:, c], wrhs[f][:, 1536:1600], start=fi, stop=False)
                    nm = nmr1b[0:1, tok0 + tci * 128:tok0 + (tci + 1) * 128]
                    t.matmul(qp0[:, 0:512], nm, wbar[0:1, 0:512], start=False, stop=True)
                    t.matmul(qp1[:, 0:512], nm, wbar[0:1, 512:1024], start=False, stop=True)
                    t.matmul(kvp[:, 0:512], nm, wbar[0:1, 1024:1536], start=False, stop=True)
                    t.matmul(imp, nm, wbar[0:1, 1536:1600], start=False, stop=False)
                    inc(t.matmul(imp, ones_row_b[:], b1row_b[:], start=False, stop=True), "proj_mm")

            @block.scalar
            def _(s):
                for tci in range(4):
                    tc = wdw * 4 + tci
                    par = tc % 2
                    qp0, qp1, kvp = pbank[4 * par], pbank[4 * par + 1], pbank[4 * par + 2]
                    imp = pbank[4 * par + 3][:, 0:64]
                    s.wait_ge(sem("proj_mm"), tc + 1)
                    if tc >= 2:
                        s.wait_ge(sem("proj_evac"), tc - 1)
                    for h in range(4):
                        sa = scr_act[:, (par * 10 + h) * 128:(par * 10 + h + 1) * 128]
                        s.activation(sa, qp0[:, h * 128:(h + 1) * 128], AF.Square,
                                     accum_out=rmsq[par][:, h:h + 1])
                    for h in range(4):
                        sa = scr_act[:, (par * 10 + 4 + h) * 128:(par * 10 + 5 + h) * 128]
                        s.activation(sa, qp1[:, h * 128:(h + 1) * 128], AF.Square,
                                     accum_out=rmsq[par][:, 4 + h:5 + h])
                    for g2 in range(2):
                        sa = scr_act[:, (par * 10 + 8 + g2) * 128:(par * 10 + 9 + g2) * 128]
                        ii = s.activation(sa, kvp[:, g2 * 128:(g2 + 1) * 128], AF.Square,
                                          accum_out=rmsk[par][:, g2:g2 + 1])
                    fence(s, ii, "fl_s")
                    s.activation(rmsq[par][:], rmsq[par][:], AF.Ln, scale=1.0 / D, bias=eps2_col[:])
                    fence(s, s.activation(rmsk[par][:, 0:2], rmsk[par][:, 0:2], AF.Ln,
                                          scale=1.0 / D, bias=eps2_col[:]), "fl_s")
                    s.activation(rmsq[par][:], rmsq[par][:], AF.Exp, scale=-0.5)
                    s.activation(rmsk[par][:, 0:2], rmsk[par][:, 0:2], AF.Exp, scale=-0.5)
                    inc(s.activation(tanh_sb[par][:], imp, AF.Tanh), "act_tc")

            @block.vector
            def _(v):
                for tci in range(4):
                    tc = wdw * 4 + tci
                    par = tc % 2
                    qp0, qp1, kvp = pbank[4 * par], pbank[4 * par + 1], pbank[4 * par + 2]
                    v.wait_ge(sem("act_tc"), tc + 1)
                    for h in range(8):
                        qp = qp0 if h < 4 else qp1
                        v.tensor_scalar(qhat[tc][:, h * 128:(h + 1) * 128],
                                        qp[:, (h % 4) * 128:(h % 4 + 1) * 128],
                                        rmsq[par][:, h:h + 1], None, op0=ALU.mult)
                    for g2 in range(2):
                        v.tensor_scalar(khat[tc][:, g2 * 128:(g2 + 1) * 128],
                                        kvp[:, g2 * 128:(g2 + 1) * 128],
                                        rmsk[par][:, g2:g2 + 1], None, op0=ALU.mult)
                    v.tensor_copy(vhat[tc][:], kvp[:, 256:512])
                    inc(v.tensor_tensor_reduce(scr_ttr[:, par * 64:(par + 1) * 64],
                                               tanh_sb[par][:], w2bc[:], 1.0, 0.0,
                                               op0=ALU.mult, op1=ALU.add,
                                               accum_out=logits_cm[:, tc:tc + 1]), "proj_evac")
        proj_done = cnt("proj_evac")

        if dbg and phases < 2:
            @block.sync
            def _(sy):
                sy.wait_ge(sem("proj_evac"), proj_done)
                if "logits" in dbg:
                    inc(sy.dma_start(out=dout("logits", [128, TC]), in_=logits_cm[:]), "dbg_st", 16)
                if "r1" in dbg:
                    inc(sy.dma_start(out=dout("r1", [1, S], BF16), in_=r1b[:]), "dbg_st", 16)
                if "qhat" in dbg:
                    o = dout("qhat", [S, NQ], BF16)
                    for tc in range(TC):
                        inc(sy.dma_start(out=o[tc * 128:(tc + 1) * 128, :], in_=qhat[tc][:]), "dbg_st", 16)
                if "khat" in dbg:
                    o = dout("khat", [S, NKVD], BF16)
                    for tc in range(TC):
                        inc(sy.dma_start(out=o[tc * 128:(tc + 1) * 128, :], in_=khat[tc][:]), "dbg_st", 16)
                if "vhat" in dbg:
                    o = dout("vhat", [S, NKVD], BF16)
                    for tc in range(TC):
                        inc(sy.dma_start(out=o[tc * 128:(tc + 1) * 128, :], in_=vhat[tc][:]), "dbg_st", 16)
                sy.wait_ge(sem("dbg_st"), cnt("dbg_st"))

        if phases < 2:
            lower_extended_insts(nc)
            return nc, dbg_outs

        ar.reset_to("persist")
        qf = [ar.alloc(f"qf{h}", [128, CT], BF16) for h in range(NH)]
        kf = [ar.alloc(f"kf{g}", [128, CT], BF16) for g in range(NKV)]
        vtm = [[ar.alloc(f"vtm{g}_{j}", [128, 132], BF16) for j in range(NCT)] for g in range(NKV)]
        ar.mark("regA_end")
        assert ar.off <= ar.marks["p1end"], "phase2 regionA overflow"
        ar.reset_to("qkv_end")
        kaT_sb = ar.alloc("kaT_sb", [1, 320], BF16)
        cs_tile = [ar.alloc(f"cs_tile{i}", [128, 2 * D], BF16) for i in range(2)]
        logitsT = ar.alloc("logitsT", [32, 128], F32)
        tk_lo = ar.alloc("tk_lo", [32, 1], F32)
        tk_hi = ar.alloc("tk_hi", [32, 1], F32)
        tk_mid = ar.alloc("tk_mid", [32, 1], F32)
        tk_gt = ar.alloc("tk_gt", [32, 128], F32)
        tk_gt2 = ar.alloc("tk_gt2", [32, 128], F32)
        tk_pc = ar.alloc("tk_pc", [32, 1], F32)
        tk_sc = ar.alloc("tk_sc", [1, 1], F32)
        tk_cond = ar.alloc("tk_cond", [32, 1], U32)
        tk_condl = ar.alloc("tk_condl", [32, 1], U32)
        tk_z = ar.alloc("tk_z", [32, 128], F32)
        tk_scan = ar.alloc("tk_scan", [32, 128], F32)
        tk_row = ar.alloc("tk_row", [1, 64], F32)
        tk_off = ar.alloc("tk_off", [32, 1], F32)
        tk_d32 = ar.alloc("tk_d32", [32, 1], F32)
        tk_selu = ar.alloc("tk_selu", [32, 128], U32)
        tk_le = ar.alloc("tk_le", [32, 128], F32)
        tk_big = ar.alloc("tk_big", [32, 128], F32)
        tk_pos = ar.alloc("tk_pos", [32, 128], F32)
        skw_b = [ar.alloc(f"skw{i}", [128, 320], BF16) for i in range(2)]
        vsf = [ar.alloc(f"vsf{g}", [128, 640], BF16) for g in range(NKV)]
        ar.mark("p2end")

        @block.gpsimd
        def _(g):
            inc(g.dma_start(out=kaT_sb[:], in_=kaT_row[:]), "ld_g", 16)
        ld_g_p2 = cnt("ld_g")

        @block.tensor
        def _(t):
            t.wait_ge(sem("ld_g"), ld_g_p2)
            t.wait_ge(sem("proj_evac"), proj_done)
            wchain(t, "tks")
            inc(t.matmul(pbank[0][:, 0:320], ones_row_b[:], kaT_sb[:], start=True, stop=True),
                "tks")

        @block.vector
        def _(v):
            wchain(v, "tks")
            inc(v.tensor_copy(kabc[:], pbank[0][:, 0:320]), "tks")

        def skw_build(v, par, tc, shalf, wvec):
            a_sl = kbT_sb[:, (tc % 2) * 32 + shalf * 16:(tc % 2) * 32 + shalf * 16 + 16]
            in0 = a_sl.rearrange("p (a one) -> p a one", one=1).to_broadcast([128, 16, 20])
            jb = tc // 2
            in1 = kabc[0:128, jb * 20:(jb + 1) * 20].rearrange("p (one c) -> p one c", one=1)\
                .to_broadcast([128, 16, 20])
            o = skw_b[par].rearrange("p (a c) -> p a c", a=16)
            return v.scalar_tensor_tensor(o, in0, wvec[:, tc:tc + 1], in1,
                                          op0=ALU.mult, op1=ALU.mult)

        @block.scalar
        def _(s):
            s.wait_ge(sem("proj_evac"), proj_done)
            inc(s.activation(weights_cm[:], logits_cm[:], AF.Sigmoid, bias=b2bias[:]), "w_rdy")

        @block.tensor
        def _(t):
            t.wait_ge(sem("proj_evac"), proj_done)
            inc(t.transpose(pbank[1][0:32, 0:128], logits_cm[:], ident_f[:]), "tks")

        @block.vector
        def _(v):
            wchain(v, "tks")
            v.tensor_copy(logitsT[:], pbank[1][0:32, 0:128])
            fence(v, v.tensor_reduce(tk_pc[:, 0:1], w2bc[0:32, :], axis=AX,
                                     op=ALU.add, apply_absolute_value=True), "fl_v")
            fence(v, v.tensor_scalar(tk_hi[:], tk_pc[:, 0:1], 1.0, 1.0, op0=ALU.mult, op1=ALU.add), "fl_v")
            v.memset(tk_z[:], 0.0)
            inc(v.scalar_tensor_tensor(tk_lo[:], tk_hi[:], -1.0, tk_hi[:], op0=ALU.mult,
                                       op1=ALU.min), "tks")

        for it in range(BISECT):
            @block.vector
            def _(v):
                wchain(v, "tks")
                if it > 0:
                    v.tensor_scalar(tk_cond[:], pbank[3][0:32, 0:1], 2048.0, None, op0=ALU.is_gt)
                    fence(v, v.tensor_scalar(tk_condl[:], pbank[3][0:32, 0:1], 2048.0, None,
                                             op0=ALU.is_le), "fl_v")
                    v.copy_predicated(tk_lo[:], tk_cond[:], tk_mid[:])
                    fence(v, v.copy_predicated(tk_hi[:], tk_condl[:], tk_mid[:]), "fl_v")
                fence(v, v.tensor_add(tk_mid[:], tk_lo[:], tk_hi[:]), "fl_v")
                fence(v, v.tensor_scalar(tk_mid[:], tk_mid[:], 0.5, None, op0=ALU.mult), "fl_v")
                fence(v, v.tensor_scalar(tk_gt[:], logitsT[:], tk_mid[:, 0:1], None, op0=ALU.is_gt), "fl_v")
                inc(v.reduce_sum(tk_pc[:], tk_gt[:], axis=AX), "tks")

            @block.tensor
            def _(t):
                wchain(t, "tks")
                inc(t.matmul(pbank[2][0:1, 0:1], ones_col_f[0:32, :], tk_pc[:], start=True, stop=True),
                    "tks")

            @block.vector
            def _(v):
                wchain(v, "tks")
                inc(v.tensor_copy(tk_sc[:], pbank[2][0:1, 0:1]), "tks")

            @block.tensor
            def _(t):
                wchain(t, "tks")
                inc(t.matmul(pbank[3][0:32, 0:1], ones_row_f[0:1, 0:32], tk_sc[:], start=True, stop=True),
                    "tks")

        @block.vector
        def _(v):
            wchain(v, "tks")
            v.tensor_scalar(tk_cond[:], pbank[3][0:32, 0:1], 2048.0, None, op0=ALU.is_gt)
            fence(v, v.tensor_scalar(tk_condl[:], pbank[3][0:32, 0:1], 2048.0, None, op0=ALU.is_le), "fl_v")
            v.copy_predicated(tk_lo[:], tk_cond[:], tk_mid[:])
            fence(v, v.copy_predicated(tk_hi[:], tk_condl[:], tk_mid[:]), "fl_v")
            v.tensor_scalar(tk_gt[:], logitsT[:], tk_hi[:, 0:1], None, op0=ALU.is_gt)
            fence(v, v.tensor_scalar(tk_gt2[:], logitsT[:], tk_lo[:, 0:1], None, op0=ALU.is_gt), "fl_v")
            fence(v, v.tensor_sub(tk_gt2[:], tk_gt2[:], tk_gt[:]), "fl_v")
            inc(v.reduce_sum(tk_pc[:], tk_gt[:], axis=AX), "tks")

        @block.tensor
        def _(t):
            wchain(t, "tks")
            inc(t.matmul(pbank[2][0:1, 0:1], ones_col_f[0:32, :], tk_pc[:], start=True, stop=True),
                "tks")

        @block.vector
        def _(v):
            wchain(v, "tks")
            inc(v.tensor_scalar(tk_sc[:], pbank[2][0:1, 0:1], -1.0, 2048.0, op0=ALU.mult,
                                op1=ALU.add), "tks")

        @block.tensor
        def _(t):
            wchain(t, "tks")
            inc(t.matmul(pbank[3][0:32, 0:1], ones_row_f[0:1, 0:32], tk_sc[:], start=True, stop=True),
                "tks")

        def prefix_pass(v, t_sems, src, dst, tag):
            fence(v, v.tensor_tensor_scan(dst, src, tk_z[:], 0.0, op0=ALU.add, op1=ALU.add), "fl_v")
            inc(v.tensor_copy(tk_pc[:], dst[0:32, 127:128]), "tks")

        @block.vector
        def _(v):
            wchain(v, "tks")
            fence(v, v.tensor_copy(tk_d32[:], pbank[3][0:32, 0:1]), "fl_v")
            prefix_pass(v, None, tk_gt2[:], tk_scan[:], "pfb")

        @block.tensor
        def _(t):
            wchain(t, "tks")
            inc(t.transpose(pbank[4][0:1, 0:32], tk_pc[0:32, 0:1], ident_f[0:32, 0:32]), "tks")

        @block.vector
        def _(v):
            wchain(v, "tks")
            fence(v, v.tensor_copy(tk_row[0:1, 0:32], pbank[4][0:1, 0:32]), "fl_v")
            fence(v, v.memset(tk_row[0:1, 32:64], 0.0), "fl_v")
            fence(v, v.tensor_tensor_scan(tk_row[0:1, 32:64], tk_row[0:1, 0:32], tk_row[0:1, 32:64],
                                          0.0, op0=ALU.add, op1=ALU.add), "fl_v")
            v.memset(tk_row[0:1, 0:1], 0.0)
            inc(v.tensor_copy(tk_row[0:1, 1:32], tk_row[0:1, 32:63]), "tks")

        @block.tensor
        def _(t):
            wchain(t, "tks")
            inc(t.transpose(pbank[4][0:32, 0:1], tk_row[0:1, 0:32], ident_f[0:1, 0:1]), "tks")

        @block.vector
        def _(v):
            wchain(v, "tks")
            fence(v, v.tensor_copy(tk_off[:], pbank[4][0:32, 0:1]), "fl_v")
            fence(v, v.tensor_scalar(tk_scan[:], tk_scan[:], tk_off[:, 0:1], None, op0=ALU.add), "fl_v")
            fence(v, v.tensor_scalar(tk_le[:], tk_scan[:], tk_d32[:, 0:1], None, op0=ALU.is_le), "fl_v")
            fence(v, v.tensor_mul(tk_gt2[:], tk_gt2[:], tk_le[:]), "fl_v")
            fence(v, v.tensor_add(tk_gt[:], tk_gt[:], tk_gt2[:]), "fl_v")
            prefix_pass(v, None, tk_gt[:], tk_scan[:], "pfs")

        @block.tensor
        def _(t):
            wchain(t, "tks")
            inc(t.transpose(pbank[4][0:1, 0:32], tk_pc[0:32, 0:1], ident_f[0:32, 0:32]), "tks")

        @block.vector
        def _(v):
            wchain(v, "tks")
            fence(v, v.tensor_copy(tk_row[0:1, 0:32], pbank[4][0:1, 0:32]), "fl_v")
            fence(v, v.memset(tk_row[0:1, 32:64], 0.0), "fl_v")
            fence(v, v.tensor_tensor_scan(tk_row[0:1, 32:64], tk_row[0:1, 0:32], tk_row[0:1, 32:64],
                                          0.0, op0=ALU.add, op1=ALU.add), "fl_v")
            v.memset(tk_row[0:1, 0:1], 0.0)
            inc(v.tensor_copy(tk_row[0:1, 1:32], tk_row[0:1, 32:63]), "tks")

        @block.tensor
        def _(t):
            wchain(t, "tks")
            inc(t.transpose(pbank[4][0:32, 0:1], tk_row[0:1, 0:32], ident_f[0:1, 0:1]), "tks")

        @block.vector
        def _(v):
            wchain(v, "tks")
            fence(v, v.tensor_copy(tk_off[:], pbank[4][0:32, 0:1]), "fl_v")
            fence(v, v.tensor_scalar(tk_scan[:], tk_scan[:], tk_off[:, 0:1], None, op0=ALU.add), "fl_v")
            v.tensor_scalar(tk_pos[:], tk_scan[:], -1.0, None, op0=ALU.add)
            v.tensor_scalar(tk_selu[:], tk_gt[:], 0.5, None, op0=ALU.is_lt)
            fence(v, v.memset(tk_big[:], BIGPOS), "fl_v")
            inc(v.copy_predicated(tk_pos[:], tk_selu[:], tk_big[:]), "tks")

        @block.tensor
        def _(t):
            wchain(t, "tks")
            inc(t.transpose(pbank[5][0:128, 0:32], tk_pos[0:32, 0:128], ident_f[0:32, 0:32]), "tks")
            inc(t.transpose(pbank[6][0:128, 0:32], tk_gt[0:32, 0:128], ident_f[0:32, 0:32]), "tks")

        @block.vector
        def _(v):
            wchain(v, "tks")
            inc(v.tensor_copy(sel_cm[:], pbank[6][0:128, 0:32]), "tks")
            v.tensor_copy(selu_cm[:], pbank[6][0:128, 0:32])
            fence(v, v.tensor_copy(pos_cm[:], pbank[5][0:128, 0:32]), "fl_v")
            v.wait_ge(sem("w_rdy"), 1)
            fence(v, v.tensor_scalar(restw_cm[:], sel_cm[:], -1.0, 1.0, op0=ALU.mult, op1=ALU.add), "fl_v")
            inc(v.tensor_mul(restw_cm[:], weights_cm[:], restw_cm[:]), "rw_rdy")

        pos_fl = cnt("fl_v")

        @block.gpsimd
        def _(g):
            g.wait_ge(sem("fl_v"), pos_fl)
            btk = g.to_reg(TK - 1)
            for tc in range(TC):
                inc(g.indirect_dma_start(
                    out=qd_dram[:], out_offset=IndirectOffsetOnAxis(pos_cm[:, tc:tc + 1], 0),
                    in_=qhat[tc][:], in_offset=None,
                    bounds_check=btk, oob_is_err=False), "sc_q", 16)
                inc(g.indirect_dma_start(
                    out=kd_dram[:], out_offset=IndirectOffsetOnAxis(pos_cm[:, tc:tc + 1], 0),
                    in_=khat[tc][:], in_offset=None,
                    bounds_check=btk, oob_is_err=False), "sc_k", 16)
                inc(g.indirect_dma_start(
                    out=vd_dram[:], out_offset=IndirectOffsetOnAxis(pos_cm[:, tc:tc + 1], 0),
                    in_=vhat[tc][:], in_offset=None,
                    bounds_check=btk, oob_is_err=False), "sc_v", 16)
        sc_q_all, sc_k_all = cnt("sc_q"), cnt("sc_k")
        sc_v_all = cnt("sc_v")

        @block.gpsimd
        def _(g):
            g.wait_ge(sem("fl_v"), pos_fl)
            btk2 = g.to_reg(TK - 1)
            for tc in range(TC):
                i = tc % 2
                if tc >= 2:
                    g.wait_ge(sem(f"sc_i{i}"), 32 * (tc // 2))
                inc(g.dma_start(out=cs_tile[i][:, 0:D],
                                in_=cos_tm[tc * 128:(tc + 1) * 128, :]), f"cs_ld{i}", 16)
                inc(g.dma_start(out=cs_tile[i][:, D:2 * D],
                                in_=sin_tm[tc * 128:(tc + 1) * 128, :]), f"cs_ld{i}", 16)
                g.wait_ge(sem(f"cs_ld{i}"), 32 * (tc // 2 + 1))
                inc(g.indirect_dma_start(
                    out=cosd_dram[:], out_offset=IndirectOffsetOnAxis(pos_cm[:, tc:tc + 1], 0),
                    in_=cs_tile[i][:, 0:D], in_offset=None,
                    bounds_check=btk2, oob_is_err=False), f"sc_i{i}", 16)
                inc(g.indirect_dma_start(
                    out=sind_dram[:], out_offset=IndirectOffsetOnAxis(pos_cm[:, tc:tc + 1], 0),
                    in_=cs_tile[i][:, D:2 * D], in_offset=None,
                    bounds_check=btk2, oob_is_err=False), f"sc_i{i}", 16)
        sc_i_all = (cnt("sc_i0"), cnt("sc_i1"))

        for shalf in range(2):
            @block.vector
            def _(v):
                for tc in range(TC):
                    if shalf == 0 and tc == 0:
                        wchain(v, "tks")
                        v.wait_ge(sem("w_rdy"), 1)
                    if shalf * TC + tc >= 2:
                        v.wait_ge(sem("skw_used"), shalf * TC + tc - 1)
                    inc(skw_build(v, tc % 2, tc, shalf, weights_cm), "skw_rdy")

            @block.tensor
            def _(t):
                for tc in range(TC):
                    t.wait_ge(sem("skw_rdy"), shalf * TC + tc + 1)
                    if shalf == 1 and tc == 0:
                        t.wait_ge(sem("qs_evac"), 8)
                    fi, la = tc == 0, tc == TC - 1
                    for h in range(NH):
                        mmi = t.matmul(pbank[h][:, 0:320], qhat[tc][:, h * 128:(h + 1) * 128],
                                       skw_b[tc % 2][:], start=fi, stop=la)
                    inc(mmi, "skw_used")

            @block.vector
            def _(v):
                v.wait_ge(sem("skw_used"), (shalf + 1) * TC)
                for h in range(NH):
                    inc(v.tensor_scalar(qf[h][:, TK + shalf * 320:TK + shalf * 320 + 320],
                                        pbank[h][:, 0:320], qwss[:, 0:1], None, op0=ALU.mult),
                        "qs_evac")

        for shalf in range(2):
            @block.vector
            def _(v):
                for tc in range(TC):
                    kk = (2 + shalf) * TC + tc
                    if shalf == 0 and tc == 0:
                        v.wait_ge(sem("rw_rdy"), 1)
                    if kk >= 2:
                        v.wait_ge(sem("skw_used"), kk - 1)
                    inc(skw_build(v, tc % 2, tc, shalf, restw_cm), "skw_rdy")

            @block.tensor
            def _(t):
                for tc in range(TC):
                    kk = (2 + shalf) * TC + tc
                    t.wait_ge(sem("skw_rdy"), kk + 1)
                    if tc == 0 and shalf == 0:
                        t.wait_ge(sem("qs_evac"), 16)
                    if tc == 0 and shalf == 1:
                        t.wait_ge(sem("kvs_evac"), 4)
                    fi, la = tc == 0, tc == TC - 1
                    for g2 in range(NKV):
                        t.matmul(pbank[g2][:, 0:320], khat[tc][:, g2 * 128:(g2 + 1) * 128],
                                 skw_b[tc % 2][:], start=fi, stop=la)
                        mmi = t.matmul(pbank[2 + g2][:, 0:320], vhat[tc][:, g2 * 128:(g2 + 1) * 128],
                                       skw_b[tc % 2][:], start=fi, stop=la)
                    inc(mmi, "skw_used")

            @block.vector
            def _(v):
                v.wait_ge(sem("skw_used"), (3 + shalf) * TC)
                for g2 in range(NKV):
                    inc(v.tensor_scalar(kf[g2][:, TK + shalf * 320:TK + shalf * 320 + 320],
                                        pbank[g2][:, 0:320], kwss[:, 0:1], None, op0=ALU.mult),
                        "kvs_evac")
                    inc(v.tensor_scalar(vsf[g2][:, shalf * 320:shalf * 320 + 320],
                                        pbank[2 + g2][:, 0:320], ssbc[:, 0:1], None, op0=ALU.mult),
                        "kvs_evac")
        kvs_all = cnt("kvs_evac")

        @block.vector
        def _(v):
            v.wait_ge(sem("proj_evac"), proj_done)
            for g2 in range(NKV):
                for j in range(NCT):
                    v.memset(vtm[g2][j][:, 129:132], 0.0)
                    ii = v.memset(vtm[g2][j][:, 128:129], 1.0)
            inc(ii, "vtm_ones")

        for g2 in range(NKV):
            for j in range(5):
                jj = g2 * 5 + j

                @block.tensor
                def _(t):
                    if jj == 0:
                        t.wait_ge(sem("kvs_evac"), kvs_all)
                    if jj >= 2:
                        t.wait_ge(sem("vs_ev"), jj - 1)
                    inc(t.transpose(pbank[4 + jj % 2].bitcast(BF16)[:, 0:128],
                                    vsf[g2][:, j * 128:(j + 1) * 128], ident_b[:]), "vs_tr")

                @block.vector
                def _(v):
                    v.wait_ge(sem("vs_tr"), jj + 1)
                    inc(v.tensor_copy(vtm[g2][16 + j][:, 0:128],
                                      pbank[4 + jj % 2].bitcast(BF16)[:, 0:128]), "vs_ev")
        vs_all = cnt("vs_ev")

        if dbg and phases < 3:
            @block.sync
            def _(sy):
                sy.wait_ge(sem("vs_ev"), vs_all)
                wchain(sy, "tks")
                sy.wait_ge(sem("rw_rdy"), 1)
                sy.wait_ge(sem("qs_evac"), 16)
                sy.wait_ge(sem("sc_q"), sc_q_all)
                sy.wait_ge(sem("sc_k"), sc_k_all)
                sy.wait_ge(sem("sc_v"), sc_v_all)
                sy.wait_ge(sem("sc_i0"), sc_i_all[0])
                sy.wait_ge(sem("sc_i1"), sc_i_all[1])
                if "pos" in dbg:
                    inc(sy.dma_start(out=dout("pos", [128, TC], U32), in_=pos_cm[:]), "dbg_st", 16)
                if "sel" in dbg:
                    inc(sy.dma_start(out=dout("sel", [128, TC]), in_=sel_cm[:]), "dbg_st", 16)
                if "weights" in dbg:
                    inc(sy.dma_start(out=dout("weights", [128, TC]), in_=weights_cm[:]), "dbg_st", 16)
                if "restw" in dbg:
                    inc(sy.dma_start(out=dout("restw", [128, TC]), in_=restw_cm[:]), "dbg_st", 16)
                if "qs" in dbg:
                    o = dout("qs", [NH * 128, 640], BF16)
                    for h in range(NH):
                        inc(sy.dma_start(out=o[h * 128:(h + 1) * 128, :], in_=qf[h][:, TK:CT]), "dbg_st", 16)
                if "ks" in dbg:
                    o = dout("ks", [NKV * 128, 640], BF16)
                    for g2 in range(NKV):
                        inc(sy.dma_start(out=o[g2 * 128:(g2 + 1) * 128, :], in_=kf[g2][:, TK:CT]), "dbg_st", 16)
                if "vs" in dbg:
                    o = dout("vs", [NKV * 128, 640], BF16)
                    for g2 in range(NKV):
                        inc(sy.dma_start(out=o[g2 * 128:(g2 + 1) * 128, :], in_=vsf[g2][:]), "dbg_st", 16)
                if "cosd" in dbg:
                    inc(sy.dma_start(out=dout("cosd", [TK, D], BF16), in_=cosd_dram[:]), "dbg_st", 16)
                if "qd" in dbg:
                    inc(sy.dma_start(out=dout("qd", [TK, NQ], BF16), in_=qd_dram[:]), "dbg_st", 16)
                sy.wait_ge(sem("dbg_st"), cnt("dbg_st"))

        if phases < 3:
            lower_extended_insts(nc)
            return nc, dbg_outs

        ar.reset_to("p1end")
        qwb1024 = ar.alloc("qwb1024", [128, NQ], BF16)
        qwsw1024 = ar.alloc("qwsw1024", [128, NQ], BF16)
        kwb256 = ar.alloc("kwb256", [128, NKVD], BF16)
        kwsw256 = ar.alloc("kwsw256", [128, NKVD], BF16)
        qd_r = [ar.alloc(f"qd_r{i}", [128, NQ], BF16) for i in range(4)]
        kcs_r = [ar.alloc(f"kcs_r{i}", [128, NKVD + 2 * D], BF16) for i in range(4)]
        ropq = [ar.alloc(f"ropq{i}", [128, NQ], BF16) for i in range(2)]
        ropk = [ar.alloc(f"ropk{i}", [128, NKVD], BF16) for i in range(2)]
        cosq = [ar.alloc(f"cosq{i}", [128, NQ], BF16) for i in range(2)]
        sinq = [ar.alloc(f"sinq{i}", [128, NQ], BF16) for i in range(2)]
        cosk = [ar.alloc(f"cosk{i}", [128, NKVD], BF16) for i in range(2)]
        sink = [ar.alloc(f"sink{i}", [128, NKVD], BF16) for i in range(2)]
        rotq = [ar.alloc(f"rotq{i}", [128, NQ], BF16) for i in range(2)]
        rotk = [ar.alloc(f"rotk{i}", [128, NKVD], BF16) for i in range(2)]
        sgn = [ar.alloc(f"sgn{i}", [128, 2 * D], BF16) for i in range(2)]
        pt_sb = ar.alloc("pt_sb", [128, NCT * 512], BF16)
        att_sc = [ar.alloc(f"attsc{i}", [128, 132], BF16) for i in range(4)]
        rcp_sb = ar.alloc("rcp_sb", [128, 8], F32)
        ar.mark("p5top")
        out_sk = [ar.alloc(f"outsk{j}", [128, NQ], BF16) for j in range(5)]
        ar.mark("p4end")

        def p3_gate(e):
            e.wait_ge(sem("skw_used"), 128)
            e.wait_ge(sem("sc_q"), sc_q_all)
            e.wait_ge(sem("sc_k"), sc_k_all)
            e.wait_ge(sem("sc_v"), sc_v_all)
            e.wait_ge(sem("sc_i0"), sc_i_all[0])
            e.wait_ge(sem("sc_i1"), sc_i_all[1])
            e.wait_ge(sem("vs_ev"), vs_all)
            e.wait_ge(sem("qs_evac"), 16)
            e.wait_ge(sem("kvs_evac"), kvs_all)
            e.wait_ge(sem("vtm_ones"), 1)

        @block.tensor
        def _(t):
            p3_gate(t)
            rep8 = qw_rows[0:1, 0:128].rearrange("a (b d) -> a b d", b=1).to_broadcast([1, 8, 128])
            rep8s = qw_rows[0:1, 128:256].rearrange("a (b d) -> a b d", b=1).to_broadcast([1, 8, 128])
            rep2 = kw_rows[0:1, 0:128].rearrange("a (b d) -> a b d", b=1).to_broadcast([1, 2, 128])
            rep2s = kw_rows[0:1, 128:256].rearrange("a (b d) -> a b d", b=1).to_broadcast([1, 2, 128])
            t.matmul(pbank[6][:, 0:512], ones_row_f[:], rep8[0:1, 0:4, :], start=True, stop=True)
            t.matmul(pbank[7][:, 0:512], ones_row_f[:], rep8[0:1, 4:8, :], start=True, stop=True)
            inc(t.matmul(pbank[4][:, 0:256], ones_row_f[:], rep2[:], start=True, stop=True), "tks")

        @block.vector
        def _(v):
            p3_gate(v)
            wchain(v, "tks")
            v.tensor_copy(qwb1024[:, 0:512], pbank[6][:, 0:512])
            v.tensor_copy(qwb1024[:, 512:1024], pbank[7][:, 0:512])
            inc(v.tensor_copy(kwb256[:], pbank[4][:, 0:256]), "tks")

        @block.tensor
        def _(t):
            wchain(t, "tks")
            rep8s = qw_rows[0:1, 128:256].rearrange("a (b d) -> a b d", b=1).to_broadcast([1, 8, 128])
            rep2s = kw_rows[0:1, 128:256].rearrange("a (b d) -> a b d", b=1).to_broadcast([1, 2, 128])
            t.matmul(pbank[6][:, 0:512], ones_row_f[:], rep8s[0:1, 0:4, :], start=True, stop=True)
            t.matmul(pbank[7][:, 0:512], ones_row_f[:], rep8s[0:1, 4:8, :], start=True, stop=True)
            inc(t.matmul(pbank[4][:, 0:256], ones_row_f[:], rep2s[:], start=True, stop=True), "tks")

        @block.vector
        def _(v):
            wchain(v, "tks")
            v.tensor_copy(qwsw1024[:, 0:512], pbank[6][:, 0:512])
            v.tensor_copy(qwsw1024[:, 512:1024], pbank[7][:, 0:512])
            inc(v.tensor_copy(kwsw256[:], pbank[4][:, 0:256]), "tks")

        @block.sync
        def _(sy):
            p3_gate(sy)
            for j in range(NTK):
                i = j % 4
                if j >= 4:
                    sy.wait_ge(sem("rope_done"), j - 3)
                inc(sy.dma_start(out=qd_r[i][:], in_=qd_dram[j * 128:(j + 1) * 128, :]), f"g3_{i}", 16)
                inc(sy.dma_start(out=kcs_r[i][:, 0:NKVD], in_=kd_dram[j * 128:(j + 1) * 128, :]), f"g3_{i}", 16)
                inc(sy.dma_start(out=kcs_r[i][:, NKVD:NKVD + D], in_=cosd_dram[j * 128:(j + 1) * 128, :]), f"g3_{i}", 16)
                inc(sy.dma_start(out=kcs_r[i][:, NKVD + D:NKVD + 2 * D], in_=sind_dram[j * 128:(j + 1) * 128, :]), f"g3_{i}", 16)
                inc(sy.dma_start(out=vtm[0][j][:, 0:128], in_=vd_dram[j * 128:(j + 1) * 128, 0:128]), f"g3_{i}", 16)
                inc(sy.dma_start(out=vtm[1][j][:, 0:128], in_=vd_dram[j * 128:(j + 1) * 128, 128:256]), f"g3_{i}", 16)

        for j in range(NTK):
            i = j % 4
            par = j % 2

            @block.vector
            def _(v):
                v.wait_ge(sem(f"g3_{i}"), 96 * (j // 4 + 1))
                if j >= 2:
                    v.wait_ge(sem("q_trev"), 10 * (j - 1))
                cosd_t = kcs_r[i][:, NKVD:NKVD + D]
                sind_t = kcs_r[i][:, NKVD + D:NKVD + 2 * D]
                v.tensor_scalar(sgn[par][:, 0:64], sind_t[:, 0:64], -1.0, None, op0=ALU.mult)
                fence(v, v.tensor_copy(sgn[par][:, 64:128], sind_t[:, 64:128]), "fl_v")
                for hh in range(NH):
                    v.tensor_copy(rotq[par][:, hh * 128:hh * 128 + 64], qd_r[i][:, hh * 128 + 64:hh * 128 + 128])
                    v.tensor_copy(rotq[par][:, hh * 128 + 64:hh * 128 + 128], qd_r[i][:, hh * 128:hh * 128 + 64])
                for g2 in range(NKV):
                    v.tensor_copy(rotk[par][:, g2 * 128:g2 * 128 + 64], kcs_r[i][:, g2 * 128 + 64:g2 * 128 + 128])
                    v.tensor_copy(rotk[par][:, g2 * 128 + 64:g2 * 128 + 128], kcs_r[i][:, g2 * 128:g2 * 128 + 64])
                ce = cosd_t.rearrange("p (one d) -> p one d", one=1).to_broadcast([128, 8, 128])
                se = sgn[par][:, 0:128].rearrange("p (one d) -> p one d", one=1).to_broadcast([128, 8, 128])
                v.tensor_tensor(cosq[par].rearrange("p (r d) -> p r d", r=8), ce,
                                qwb1024.rearrange("p (r d) -> p r d", r=8), op=ALU.mult)
                v.tensor_tensor(sinq[par].rearrange("p (r d) -> p r d", r=8), se,
                                qwsw1024.rearrange("p (r d) -> p r d", r=8), op=ALU.mult)
                ce2 = cosd_t.rearrange("p (one d) -> p one d", one=1).to_broadcast([128, 2, 128])
                se2 = sgn[par][:, 0:128].rearrange("p (one d) -> p one d", one=1).to_broadcast([128, 2, 128])
                v.tensor_tensor(cosk[par].rearrange("p (r d) -> p r d", r=2), ce2,
                                kwb256.rearrange("p (r d) -> p r d", r=2), op=ALU.mult)
                fence(v, v.tensor_tensor(sink[par].rearrange("p (r d) -> p r d", r=2), se2,
                                         kwsw256.rearrange("p (r d) -> p r d", r=2), op=ALU.mult), "fl_v")
                v.tensor_mul(ropq[par][:], qd_r[i][:], cosq[par][:])
                v.tensor_mul(rotq[par][:], rotq[par][:], sinq[par][:])
                v.tensor_mul(ropk[par][:], kcs_r[i][:, 0:NKVD], cosk[par][:])
                fence(v, v.tensor_mul(rotk[par][:], rotk[par][:], sink[par][:]), "fl_v")
                v.tensor_add(ropq[par][:], ropq[par][:], rotq[par][:])
                inc(v.tensor_add(ropk[par][:], ropk[par][:], rotk[par][:]), "rope_done")

            @block.tensor
            def _(t):
                t.wait_ge(sem("rope_done"), j + 1)
                for hh in range(NH):
                    idx = j * 10 + hh
                    if idx >= 4:
                        t.wait_ge(sem("q_trev"), idx - 3)
                    inc(t.transpose(pbank[idx % 4].bitcast(BF16)[:, 0:128],
                                    ropq[par][:, hh * 128:(hh + 1) * 128], ident_b[:]), "q_tr")
                for g2 in range(NKV):
                    idx = j * 10 + 8 + g2
                    if idx >= 4:
                        t.wait_ge(sem("q_trev"), idx - 3)
                    inc(t.transpose(pbank[idx % 4].bitcast(BF16)[:, 0:128],
                                    ropk[par][:, g2 * 128:(g2 + 1) * 128], ident_b[:]), "q_tr")

            @block.vector
            def _(v):
                for hh in range(NH):
                    idx = j * 10 + hh
                    v.wait_ge(sem("q_tr"), idx + 1)
                    inc(v.tensor_copy(qf[hh][:, j * 128:(j + 1) * 128],
                                      pbank[idx % 4].bitcast(BF16)[:, 0:128]), "q_trev")
                for g2 in range(NKV):
                    idx = j * 10 + 8 + g2
                    v.wait_ge(sem("q_tr"), idx + 1)
                    inc(v.tensor_copy(kf[g2][:, j * 128:(j + 1) * 128],
                                      pbank[idx % 4].bitcast(BF16)[:, 0:128]), "q_trev")
        g3_tot = [cnt(f"g3_{i}") for i in range(4)]

        if dbg and phases < 4:
            @block.sync
            def _(sy):
                sy.wait_ge(sem("q_trev"), 10 * NTK)
                if "pos" in dbg:
                    inc(sy.dma_start(out=dout("pos", [128, TC], U32), in_=pos_cm[:]), "dbg_st", 16)
                if "sel" in dbg:
                    inc(sy.dma_start(out=dout("sel", [128, TC]), in_=sel_cm[:]), "dbg_st", 16)
                if "qfd" in dbg:
                    o = dout("qfd", [NH * 128, CT], BF16)
                    for hh in range(NH):
                        inc(sy.dma_start(out=o[hh * 128:(hh + 1) * 128, :], in_=qf[hh][:]), "dbg_st", 16)
                if "kfd" in dbg:
                    o = dout("kfd", [NKV * 128, CT], BF16)
                    for g2 in range(NKV):
                        inc(sy.dma_start(out=o[g2 * 128:(g2 + 1) * 128, :], in_=kf[g2][:]), "dbg_st", 16)
                if "vtmd" in dbg:
                    o = dout("vtmd", [NKV * NCT * 128, 132], BF16)
                    for g2 in range(NKV):
                        for jj in range(NCT):
                            inc(sy.dma_start(out=o[(g2 * NCT + jj) * 128:(g2 * NCT + jj + 1) * 128, :],
                                             in_=vtm[g2][jj][:]), "dbg_st", 16)
                sy.wait_ge(sem("dbg_st"), cnt("dbg_st"))

        if phases < 4:
            lower_extended_insts(nc)
            return nc, dbg_outs


        QCW = [512, 512, 512, 512, 512, 128]
        det_uses = [0, 0, 0, 0]
        n_sc = 0
        n_pv = 0
        n_ev = 0

        for h in range(NH):
            g2 = h // 4
            for qc in range(6):
                w = QCW[qc]
                q0 = qc * 512

                @block.tensor
                def _(t):
                    if h == 0 and qc == 0:
                        t.wait_ge(sem("q_trev"), 10 * NTK)
                    for kt in range(NCT):
                        idx = n_sc + kt
                        if idx >= 2:
                            t.wait_ge(sem("s_exp"), idx - 1)
                        inc(t.matmul(pbank[idx % 2][:, 0:w], kf[g2][:, kt * 128:(kt + 1) * 128],
                                     qf[h][:, q0:q0 + w], start=True, stop=True), "s_mm")

                @block.scalar
                def _(s):
                    s.wait_ge(sem("pv_mm"), n_pv)
                    for kt in range(NCT):
                        idx = n_sc + kt
                        s.wait_ge(sem("s_mm"), idx + 1)
                        inc(s.activation(pt_sb[:, kt * 512:kt * 512 + w], pbank[idx % 2][:, 0:w],
                                         AF.Exp, scale=ISQD), "s_exp")
                n_sc += NCT

                @block.tensor
                def _(t):
                    t.wait_ge(sem("s_exp"), n_sc)
                    for qsub in range(w // 128):
                        if n_pv + qsub >= 4:
                            t.wait_ge(sem("att_ev"), n_pv + qsub - 3)
                        for kt in range(NCT):
                            mmi = t.matmul(pbank[2 + (n_pv + qsub) % 4][:, 0:129],
                                           pt_sb[:, kt * 512 + qsub * 128:kt * 512 + (qsub + 1) * 128],
                                           vtm[g2][kt][:, 0:129],
                                           start=(kt == 0), stop=(kt == NCT - 1))
                        inc(mmi, "pv_mm")

                @block.vector
                def _(v):
                    v.wait_ge(sem("pv_mm"), n_pv + w // 128)
                    for qsub in range(w // 128):
                        v.reciprocal(rcp_sb[:, qsub:qsub + 1],
                                     pbank[2 + (n_pv + qsub) % 4][:, 128:129])
                    fence(v, v.memset(rcp_sb[:, 7:8], 0.0), "fl_v")
                    for qsub in range(w // 128):
                        i4 = (n_pv + qsub) % 4
                        r0 = q0 + qsub * 128
                        if r0 < TK:
                            if det_uses[i4] > 0:
                                v.wait_ge(sem(f"det_st{i4}"), 16 * det_uses[i4])
                            dst = att_sc[i4][:, 0:128]
                        else:
                            sch = (r0 - TK) // 128
                            dst = out_sk[sch][:, h * 128:(h + 1) * 128]
                        inc(v.tensor_scalar(dst, pbank[2 + i4][:, 0:128],
                                            rcp_sb[:, qsub:qsub + 1], None, op0=ALU.mult), "att_ev")

                @block.sync
                def _(sy):
                    for qsub in range(w // 128):
                        i4 = (n_pv + qsub) % 4
                        r0 = q0 + qsub * 128
                        if r0 < TK:
                            sy.wait_ge(sem("att_ev"), n_pv + qsub + 1)
                            inc(sy.dma_start(out=outdet_dram[r0:r0 + 128, h * 128:(h + 1) * 128],
                                             in_=att_sc[i4][:, 0:128]), f"det_st{i4}", 16)
                            det_uses[i4] += 1
                n_pv += w // 128
        att_all = cnt("att_ev")
        det_all = [cnt(f"det_st{i}") for i in range(4)]

        if dbg and phases < 5:
            @block.sync
            def _(sy):
                sy.wait_ge(sem("att_ev"), att_all)
                for i4 in range(4):
                    sy.wait_ge(sem(f"det_st{i4}"), det_all[i4])
                if "pos" in dbg:
                    inc(sy.dma_start(out=dout("pos", [128, TC], U32), in_=pos_cm[:]), "dbg_st", 16)
                if "sel" in dbg:
                    inc(sy.dma_start(out=dout("sel", [128, TC]), in_=sel_cm[:]), "dbg_st", 16)
                if "outdet" in dbg:
                    inc(sy.dma_start(out=dout("outdet", [TK, NQ], BF16), in_=outdet_dram[:]), "dbg_st", 16)
                if "outsk" in dbg:
                    o = dout("outsk", [5 * 128, NQ], BF16)
                    for j in range(5):
                        inc(sy.dma_start(out=o[j * 128:(j + 1) * 128, :], in_=out_sk[j][:]), "dbg_st", 16)
                sy.wait_ge(sem("dbg_st"), cnt("dbg_st"))

        if phases < 5:
            lower_extended_insts(nc)
            return nc, dbg_outs

        p5top = ar.marks["p5top"]
        ar.reset_to("p4end")
        wo_sb = [ar.alloc(f"wo_sb{hd}", [128, DM], BF16) for hd in range(6)]
        kb_sb = ar.alloc("kb_sb", [32, 256], BF16)
        ka_sb = ar.alloc("ka_sb", [20, 128], BF16)
        assert ar.off <= ar.marks["qkv_end"], f"phase5 D2 overflow: {ar.off}"
        ar.reset_to("persist")
        kexp = [ar.alloc(f"kexp{i}", [128, 384], BF16) for i in range(2)]
        skfm = [ar.alloc(f"skfm{j}", [128, S], BF16) for j in range(5)]
        full_sb = [ar.alloc(f"full_sb{i}", [128, NQ], BF16) for i in range(2)]
        fullT = [ar.alloc(f"fullT{hd}", [128, S], BF16) for hd in range(8)]
        cc_sb = [ar.alloc(f"cc_sb{i}", [128, 512], BF16) for i in range(2)]
        wo_sb += [ar.alloc(f"wo_sb{hd}", [128, DM], BF16) for hd in range(6, 8)]
        sela_sb = ar.alloc("sela_sb", [32, 640], BF16)
        selc_sb = ar.alloc("selc_sb", [20, 640], BF16)
        det_t = [ar.alloc(f"det_t{i}", [128, NQ], BF16) for i in range(2)]
        assert ar.off <= p5top, f"phase5 overflow: {ar.off} > {p5top}"

        @block.gpsimd
        def _(g):
            g.wait_ge(sem("skw_used"), 128)
            g.wait_ge(sem("sc_q"), sc_q_all)
            g.wait_ge(sem("sc_k"), sc_k_all)
            g.wait_ge(sem("sc_v"), sc_v_all)
            inc(g.dma_start(out=kb_sb[:], in_=kb_n[:]), "ld_g5", 16)
            inc(g.dma_start(out=ka_sb[:], in_=ka_n[:]), "ld_g5", 16)
            for hd in range(6):
                inc(g.dma_start(out=wo_sb[hd][:], in_=wo_c[hd * 128:(hd + 1) * 128, :]), "ld_g5", 16)
            g.wait_ge(sem("pv_mm"), n_pv)
            g.wait_ge(sem("att_ev"), att_all)
            inc(g.dma_start(out=sela_sb[:], in_=sela[:]), "ld_g5", 16)
            inc(g.dma_start(out=selc_sb[:], in_=selc[:]), "ld_g5", 16)
            for hd in range(6, 8):
                inc(g.dma_start(out=wo_sb[hd][:], in_=wo_c[hd * 128:(hd + 1) * 128, :]), "ld_g5", 16)
        ld_g5_all = cnt("ld_g5")

        for j in range(5):
            par5 = j % 2

            @block.tensor
            def _(t):
                if j == 0:
                    t.wait_ge(sem("ld_g5"), ld_g5_all)
                    t.wait_ge(sem("att_ev"), att_all)
                if j >= 2:
                    t.wait_ge(sem("skfm_rdy"), j - 1)
                t.matmul(pbank[6 + par5][:, 0:256],
                         sela_sb[:, j * 128:(j + 1) * 128], kb_sb[:], start=True, stop=True)
                inc(t.matmul(pbank[6 + par5][:, 256:384],
                             selc_sb[:, j * 128:(j + 1) * 128], ka_sb[:], start=True, stop=True),
                    "kexp_mm")

            @block.vector
            def _(v):
                v.wait_ge(sem("kexp_mm"), j + 1)
                v.tensor_copy(kexp[par5][:, 0:256], pbank[6 + par5][:, 0:256])
                fence(v, v.tensor_copy(kexp[par5][:, 256:384],
                                       pbank[6 + par5][:, 256:384]), "fl_v")
                for ts in range(TC):
                    ii = v.tensor_tensor(
                        skfm[j][:, ts * 128:(ts + 1) * 128],
                        kexp[par5][:, (ts % 2) * 128:(ts % 2) * 128 + 128],
                        kexp[par5][:, 256 + ts // 2:256 + ts // 2 + 1].to_broadcast([128, 128]),
                        op=ALU.mult)
                inc(ii, "skfm_rdy")

        @block.gpsimd
        def _(g):
            for i4 in range(4):
                g.wait_ge(sem(f"det_st{i4}"), det_all[i4])
        btk5 = [None]

        @block.gpsimd
        def _(g):
            btk5[0] = g.to_reg(TK - 1)

        for tc in range(TC):
            par5 = tc % 2

            @block.tensor
            def _(t):
                if tc == 0:
                    t.wait_ge(sem("skfm_rdy"), 5)
                if tc >= 2:
                    t.wait_ge(sem("full_ev"), tc - 1)
                for j in range(5):
                    t.matmul(pbank[par5 * 2][:, 0:512], skfm[j][:, tc * 128:(tc + 1) * 128],
                             out_sk[j][:, 0:512], start=(j == 0), stop=(j == 4))
                    mmi = t.matmul(pbank[par5 * 2 + 1][:, 0:512], skfm[j][:, tc * 128:(tc + 1) * 128],
                                   out_sk[j][:, 512:1024], start=(j == 0), stop=(j == 4))
                inc(mmi, "exp_mm")

            @block.vector
            def _(v):
                v.wait_ge(sem("exp_mm"), tc + 1)
                if tc >= 2:
                    v.wait_ge(sem(f"full_st{par5}"), 16 * (tc // 2))
                v.tensor_copy(full_sb[par5][:, 0:512], pbank[par5 * 2][:, 0:512])
                v.tensor_copy(full_sb[par5][:, 512:1024], pbank[par5 * 2 + 1][:, 0:512])
                inc(v.memset(det_t[par5][:], 0.0), "full_ev")

            @block.gpsimd
            def _(g):
                g.wait_ge(sem("full_ev"), tc + 1)
                inc(g.indirect_dma_start(
                    out=det_t[par5][:], out_offset=None,
                    in_=outdet_dram[:],
                    in_offset=IndirectOffsetOnAxis(pos_cm[:, tc:tc + 1], 0),
                    bounds_check=btk5[0], oob_is_err=False), f"patch{par5}", 16)

            @block.vector
            def _(v):
                v.wait_ge(sem(f"patch{par5}"), 16 * (tc // 2 + 1))
                inc(v.copy_predicated(full_sb[par5][:],
                                      selu_cm[:, tc:tc + 1].to_broadcast([128, NQ]),
                                      det_t[par5][:]), "blend_ev")

            @block.sync
            def _(sy):
                sy.wait_ge(sem("blend_ev"), tc + 1)
                inc(sy.dma_start(out=full_dram[tc * 128:(tc + 1) * 128, :], in_=full_sb[par5][:]),
                    f"full_st{par5}", 16)
        full_st_all = (cnt("full_st0"), cnt("full_st1"))

        @block.sync
        def _(sy):
            sy.wait_ge(sem("full_st0"), full_st_all[0])
            sy.wait_ge(sem("full_st1"), full_st_all[1])
            for hd in range(8):
                inc(sy.dma_start_transpose(fullT[hd][:], full_dram[:, hd * 128:(hd + 1) * 128]),
                    "ftr_ld", 16)
        ftr_all = cnt("ftr_ld")

        n_wo = 0
        for oc in range(NF):
            for tcc in range(8):
                @block.tensor
                def _(t):
                    if n_wo == 0:
                        t.wait_ge(sem("ftr_ld"), ftr_all)
                        t.wait_ge(sem("full_ev"), TC)
                    if n_wo >= 2:
                        t.wait_ge(sem("cc_ev"), n_wo - 1)
                    for hd in range(8):
                        mmi = t.matmul(pbank[4 + n_wo % 2][:, 0:512],
                                       wo_sb[hd][:, oc * 128:(oc + 1) * 128],
                                       fullT[hd][:, tcc * 512:(tcc + 1) * 512],
                                       start=(hd == 0), stop=(hd == 7))
                    inc(mmi, "wo_mm")

                @block.vector
                def _(v):
                    v.wait_ge(sem("wo_mm"), n_wo + 1)
                    if n_wo >= 2:
                        v.wait_ge(sem(f"cc_st{n_wo % 2}"), 16 * (n_wo // 2))
                    inc(v.tensor_copy(cc_sb[n_wo % 2][:], pbank[4 + n_wo % 2][:, 0:512]), "cc_ev")

                @block.sync
                def _(sy):
                    sy.wait_ge(sem("cc_ev"), n_wo + 1)
                    inc(sy.dma_start(out=ccin[oc * 128:(oc + 1) * 128, tcc * 512:(tcc + 1) * 512],
                                     in_=cc_sb[n_wo % 2][:]), f"cc_st{n_wo % 2}", 16)
                n_wo += 1
        cc_st_all = (cnt("cc_st0"), cnt("cc_st1"))

        if dbg and phases < 6:
            @block.sync
            def _(sy):
                sy.wait_ge(sem("cc_st0"), cc_st_all[0])
                sy.wait_ge(sem("cc_st1"), cc_st_all[1])
                if "pos" in dbg:
                    inc(sy.dma_start(out=dout("pos", [128, TC], U32), in_=pos_cm[:]), "dbg_st", 16)
                if "sel" in dbg:
                    inc(sy.dma_start(out=dout("sel", [128, TC]), in_=sel_cm[:]), "dbg_st", 16)
                if "full" in dbg:
                    inc(sy.dma_start(out=dout("full", [S, NQ], BF16), in_=full_dram[:]), "dbg_st", 16)
                if "ccin" in dbg:
                    inc(sy.dma_start(out=dout("ccin", [DM, S], BF16), in_=ccin[:]), "dbg_st", 16)
                sy.wait_ge(sem("dbg_st"), cnt("dbg_st"))

        if phases < 6:
            lower_extended_insts(nc)
            return nc, dbg_outs

        @block.gpsimd
        def _(g):
            g.wait_ge(sem("cc_st0"), cc_st_all[0])
            g.wait_ge(sem("cc_st1"), cc_st_all[1])
            inc(g.collective_compute(
                "AllReduce", ALU.add,
                replica_groups=[[0, 1], [2, 3], [4, 5], [6, 7]],
                ins=[ccin.opt()], outs=[ccout.opt()]), "cc_done")
        cc_done_n = cnt("cc_done")

        ar.reset_to("persist")
        ar.top = nc.sbuf_top
        wbT_sb = [ar.alloc(f"wbT_sb{f}", [128, DM], BF16) for f in range(NF)]
        x1full = [ar.alloc(f"x1f{f}", [128, HTOK], BF16) for f in range(NF)]
        aT_sb = [ar.alloc(f"aT_sb{f}", [128, 128], BF16) for f in range(NF)]
        rwT_sb = [ar.alloc(f"rwT_sb{f}", [128, 8], BF16) for f in range(NF)]
        bw_sb = ar.alloc("bw_sb", [128, DM], BF16)
        bb_sb = ar.alloc("bb_sb", [128, NF], F32)
        sq2_off = ar.off
        sq2 = [ar.alloc(f"sq2{i}", [128, HTOK], BF16) for i in range(2)]
        x1r = [ar.alloc(f"x1r{f}", [128, 512], BF16) for f in range(NF)]
        r2bc = ar.alloc("r2bc", [128, 512], BF16)
        vm2 = ar.alloc("vm2", [1, HTOK], F32)
        vv2 = ar.alloc("vv2", [1, HTOK], F32)
        vt2 = nc.alloc_sbuf_tensor_at("vt2", [1, HTOK], F32, offset=(sq2_off + 63) // 64 * 64).ap()
        r2b = ar.alloc("r2b", [1, HTOK], BF16)
        nmr2b = ar.alloc("nmr2b", [1, HTOK], BF16)
        wbar2 = ar.alloc("wbar2", [1, DM + 136], BF16)
        rl_tm = ar.alloc("rl_tm", [128, 8], F32)
        rl_s1 = ar.alloc("rl_s1", [128, 8], F32)
        rl_s2 = ar.alloc("rl_s2", [128, 8], F32)
        z8 = ar.alloc("z8", [128, 8], F32)
        sel8f = ar.alloc("sel8f", [128, 8], F32)
        sel8fm = ar.alloc("sel8fm", [8, 512], BF16)
        esel_sb = ar.alloc("esel_sb", [8, 128], BF16)
        h_sb = ar.alloc("h_sb", [128, 512], BF16)
        hm_sb = ar.alloc("hm_sb", [128, 512], BF16)
        out_f = [ar.alloc(f"out_f{i}", [128, 512], F32) for i in range(2)]
        assert ar.off <= ar.top, f"phase6 overflow: {ar.off}"

        @block.gpsimd
        def _(g):
            g.wait_ge(sem("cc_ev"), n_wo)
            for f in range(NF):
                inc(g.dma_start(out=wbT_sb[f][:], in_=wbT[f * 128:(f + 1) * 128, :]), "ld_g6", 16)
                inc(g.dma_start(out=aT_sb[f][:], in_=aT[f * 128:(f + 1) * 128, :]), "ld_g6", 16)
                inc(g.dma_start(out=rwT_sb[f][:], in_=rwT[f * 128:(f + 1) * 128, :]), "ld_g6", 16)
            inc(g.dma_start(out=bw_sb[:], in_=bw_er[:]), "ld_g6", 16)
            inc(g.dma_start(out=esel_sb[:], in_=esel[:]), "ld_g6", 16)
        ld_g6_all = cnt("ld_g6")

        @block.sync
        def _(sy):
            sy.wait_ge(sem("cc_ev"), n_wo)
            inc(sy.dma_start(out=bb_sb[:], in_=b_base_cm[:]), "ld_s6", 16)
        ld_s6_all = cnt("ld_s6")

        @block.vector
        def _(v):
            v.wait_ge(sem("ld_g6"), ld_g6_all)
            v.memset(z8[:], 0.0)
            inc(v.tensor_scalar(bw_sb[:], bw_sb[:], 2.0, None, op0=ALU.mult), "bw_sc")

        @block.sync
        def _(sy):
            sy.wait_ge(sem("cc_done"), cc_done_n)
            pid = nc.partition_id(engines=[mybir.EngineType.SP])
            half_off = pid % 2 * HTOK
            for f in range(NF):
                inc(sy.dma_start(out=x1full[f][:],
                                 in_=ccout[f * 128:(f + 1) * 128, bass.ds(half_off, HTOK)]),
                    "ld_cc", 16)
        ld_cc_all = cnt("ld_cc")

        @block.gpsimd
        def _(g):
            g.wait_ge(sem("cc_done"), cc_done_n)
            for f in range(NF):
                i = f % 2
                if f >= 2:
                    g.wait_ge(sem("x1_used"), f - 1)
                inc(g.dma_start(out=sq2[i][:], in_=xTh[f * 128:(f + 1) * 128, :]), f"ld_xh{i}", 16)

        @block.vector
        def _(v):
            v.wait_ge(sem("ld_cc"), ld_cc_all)
            for f in range(NF):
                i = f % 2
                v.wait_ge(sem(f"ld_xh{i}"), 16 * (f // 2 + 1))
                inc(v.tensor_add(x1full[f][:], x1full[f][:], sq2[i][:]), "x1_rdy")

        @block.scalar
        def _(s):
            for f in range(NF):
                i = f % 2
                s.wait_ge(sem("x1_rdy"), f + 1)
                inc(s.activation(sq2[i][:], x1full[f][:], AF.Square), "sq6")

        psum_sum2 = [pbank[i][0:1, :] for i in range(4)]
        psum_sq2 = [pbank[4 + i][0:1, :] for i in range(4)]

        @block.tensor
        def _(t):
            t.wait_ge(sem("cc_ev"), n_wo)
            for f in range(NF):
                i = f % 2
                t.wait_ge(sem("sq6"), f + 1)
                fi, la = f == 0, f == NF - 1
                for w in range(4):
                    t.matmul(psum_sum2[w], ones_col_b[:], x1full[f][:, w * 512:(w + 1) * 512],
                             start=fi, stop=la)
                    mmi = t.matmul(psum_sq2[w], ones_col_b[:], sq2[i][:, w * 512:(w + 1) * 512],
                                   start=fi, stop=la)
                inc(mmi, "x1_used")

        @block.vector
        def _(v):
            v.wait_ge(sem("x1_used"), NF)
            for w in range(4):
                v.tensor_scalar(vm2[0:1, w * 512:(w + 1) * 512], psum_sum2[w], 1.0 / DM,
                                None, op0=ALU.mult)
                ii = v.tensor_scalar(vv2[0:1, w * 512:(w + 1) * 512], psum_sq2[w], 1.0 / DM,
                                     None, op0=ALU.mult)
            fence(v, ii, "fl_v")
            fence(v, v.tensor_mul(vt2[:], vm2[:], vm2[:]), "fl_v")
            inc(v.tensor_sub(vv2[:], vv2[:], vt2[:]), "ln2_var")
            v.wait_ge(sem("rstd2"), 1)
            inc(v.scalar_tensor_tensor(nmr2b[:], vm2[:], -1.0, vv2[:],
                                       op0=ALU.mult, op1=ALU.mult), "ln2_rdy")
            fence(v, v.tensor_copy(r2b[:], vv2[:]), "fl_v")
        ln2_fl = cnt("fl_v")

        @block.scalar
        def _(s):
            s.wait_ge(sem("ln2_var"), 1)
            fence(s, s.activation(vv2[:], vv2[:], AF.Ln, bias=eps1_row[:]), "fl_s")
            inc(s.activation(vv2[:], vv2[:], AF.Exp, scale=-0.5), "rstd2")

        @block.tensor
        def _(t):
            t.wait_ge(sem("ln2_var"), 1)
            for f in range(NF):
                fi, la = f == 0, f == NF - 1
                for w in range(4):
                    t.matmul(pbank[w][0:1, 0:512], ones_col_b[:], wbT_sb[f][:, w * 512:(w + 1) * 512],
                             start=fi, stop=la)
                t.matmul(pbank[4][0:1, 0:128], ones_col_b[:], aT_sb[f][:], start=fi, stop=la)
                mmi = t.matmul(pbank[5][0:1, 0:8], ones_col_b[:], rwT_sb[f][:], start=fi, stop=la)
            inc(mmi, "wbar2_mm")

        @block.vector
        def _(v):
            v.wait_ge(sem("wbar2_mm"), 1)
            for w in range(4):
                v.tensor_copy(wbar2[0:1, w * 512:(w + 1) * 512], pbank[w][0:1, 0:512])
            v.tensor_copy(wbar2[0:1, DM:DM + 128], pbank[4][0:1, 0:128])
            inc(v.tensor_copy(wbar2[0:1, DM + 128:DM + 136], pbank[5][0:1, 0:8]), "wbar2_ev")

        n_fo = 0
        n_rt = 0
        for tw in range(4):
            t0 = tw * 512

            @block.tensor
            def _(t):
                if tw == 0:
                    t.wait_ge(sem("ln2_rdy"), 1)
                    t.wait_ge(sem("wbar2_ev"), 1)
                    t.wait_ge(sem("fl_v"), ln2_fl)
                else:
                    t.wait_ge(sem("r2cp"), tw)
                inc(t.matmul(pbank[5][:, 0:512], ones_row_b[:], r2b[0:1, t0:t0 + 512],
                             start=True, stop=True), "r2mm")

            @block.vector
            def _(v):
                v.wait_ge(sem("r2mm"), tw + 1)
                inc(v.tensor_copy(r2bc[:], pbank[5][:, 0:512]), "r2cp")
                fence(v, v.memset(rl_s2[0:1, 0:1], 0.0), "fl_v")

            @block.vector
            def _(v):
                if tw >= 1:
                    v.wait_ge(sem("out_st0"), 16 * (n_fo // 2))
                    v.wait_ge(sem("out_st1"), 16 * (n_fo // 2))
                for f in range(NF):
                    ii = v.tensor_mul(x1r[f][:], x1full[f][:, t0:t0 + 512], r2bc[:])
                inc(ii, "x1r_rdy")
            x1r_n = cnt("x1r_rdy")

            for tt in range(4):
                @block.tensor
                def _(t):
                    t.wait_ge(sem("x1r_rdy"), x1r_n)
                    if n_rt >= 1:
                        t.wait_ge(sem("rt_ev"), n_rt)
                    for f in range(NF):
                        t.matmul(pbank[6][:, 0:8], x1r[f][:, tt * 128:(tt + 1) * 128],
                                 rwT_sb[f][:], start=(f == 0), stop=False)
                    inc(t.matmul(pbank[6][:, 0:8], nmr2b[0:1, t0 + tt * 128:t0 + (tt + 1) * 128],
                                 wbar2[0:1, DM + 128:DM + 136], start=False, stop=True), "rt_mm")

                @block.vector
                def _(v):
                    v.wait_ge(sem("rt_mm"), n_rt + 1)
                    fence(v, v.tensor_copy(rl_tm[:], pbank[6][:, 0:8]), "fl_v")
                    fence(v, v.reduce_max(rl_s1[:, 0:1], rl_tm[:], axis=AX), "fl_v")
                    fence(v, v.tensor_scalar(rl_s2[:], rl_tm[:], rl_s1[:, 0:1], None,
                                             op0=ALU.is_ge), "fl_v")
                    fence(v, v.tensor_tensor_scan(rl_s1[:], rl_s2[:], z8[:], 0.0,
                                                  op0=ALU.add, op1=ALU.add), "fl_v")
                    fence(v, v.tensor_scalar(rl_s1[:], rl_s1[:], 1.0, None, op0=ALU.is_le), "fl_v")
                    inc(v.tensor_mul(sel8f[:], rl_s2[:], rl_s1[:]), "rt_sel")

                @block.tensor
                def _(t):
                    t.wait_ge(sem("rt_sel"), n_rt + 1)
                    inc(t.transpose(pbank[7][0:8, 0:128], sel8f[:], ident_f[:]), "rt_tr")

                @block.vector
                def _(v):
                    v.wait_ge(sem("rt_tr"), n_rt + 1)
                    inc(v.tensor_copy(sel8fm[0:8, tt * 128:(tt + 1) * 128],
                                      pbank[7][0:8, 0:128]), "rt_ev")
                n_rt += 1

            @block.tensor
            def _(t):
                t.wait_ge(sem("rt_ev"), n_rt)
                for f in range(NF):
                    t.matmul(pbank[6][:, 0:512], aT_sb[f][:], x1r[f][:], start=(f == 0), stop=False)
                inc(t.matmul(pbank[6][:, 0:512], wbar2[0:1, DM:DM + 128],
                             nmr2b[0:1, t0:t0 + 512], start=False, stop=True), "h_mm")
                inc(t.matmul(pbank[7][:, 0:512], esel_sb[:], sel8fm[:], start=True, stop=True),
                    "mask_mm")

            @block.vector
            def _(v):
                v.wait_ge(sem("h_mm"), tw + 1)
                fence(v, v.tensor_copy(h_sb[:], pbank[6][:, 0:512]), "fl_v")
                v.wait_ge(sem("mask_mm"), tw + 1)
                inc(v.tensor_mul(hm_sb[:], h_sb[:], pbank[7][:, 0:512]), "hm_rdy")

            for oc in range(NF):
                @block.tensor
                def _(t):
                    t.wait_ge(sem("hm_rdy"), tw + 1)
                    if tw == 0 and oc == 0:
                        t.wait_ge(sem("bw_sc"), 1)
                    if n_fo >= 2:
                        t.wait_ge(sem("fo_ev"), n_fo - 1)
                    for f in range(NF):
                        t.matmul(pbank[n_fo % 2][:, 0:512], wbT_sb[f][:, oc * 128:(oc + 1) * 128],
                                 x1r[f][:], start=(f == 0), stop=False)
                    t.matmul(pbank[n_fo % 2][:, 0:512], wbar2[0:1, oc * 128:(oc + 1) * 128],
                             nmr2b[0:1, t0:t0 + 512], start=False, stop=False)
                    inc(t.matmul(pbank[n_fo % 2][:, 0:512], bw_sb[:, oc * 128:(oc + 1) * 128],
                                 hm_sb[:], start=False, stop=True), "fo_mm")

                @block.vector
                def _(v):
                    v.wait_ge(sem("fo_mm"), n_fo + 1)
                    if n_fo == 0:
                        v.wait_ge(sem("ld_s6"), ld_s6_all)
                    if n_fo >= 2:
                        v.wait_ge(sem(f"out_st{n_fo % 2}"), 16 * (n_fo // 2))
                    inc(v.scalar_tensor_tensor(out_f[n_fo % 2][:], pbank[n_fo % 2][:, 0:512],
                                               bb_sb[:, oc:oc + 1], x1full[oc][:, t0:t0 + 512],
                                               op0=ALU.add, op1=ALU.add), "fo_ev")

                @block.sync
                def _(sy):
                    sy.wait_ge(sem("fo_ev"), n_fo + 1)
                    inc(sy.dma_start(out=out_fm[oc * 128:(oc + 1) * 128, t0:t0 + 512],
                                     in_=out_f[n_fo % 2][:]), f"out_st{n_fo % 2}", 16)
                n_fo += 1

        @block.sync
        def _(sy):
            sy.wait_ge(sem("out_st0"), cnt("out_st0"))
            sy.wait_ge(sem("out_st1"), cnt("out_st1"))

    lower_extended_insts(nc)
    return nc, dbg_outs


def make_in_maps(inputs):
    x = np.ascontiguousarray(inputs["L0_attn_hidden_states"], dtype=np.float32)
    cos = np.ascontiguousarray(np.asarray(inputs["L0_attn_cos"], dtype=np.float32)[0, :S, 0, :])
    sin = np.ascontiguousarray(np.asarray(inputs["L0_attn_sin"], dtype=np.float32)[0, :S, 0, :])
    wq = np.asarray(inputs["L0_attn_wq"], dtype=np.float32)
    wk = np.asarray(inputs["L0_attn_wk"], dtype=np.float32)
    wv = np.asarray(inputs["L0_attn_wv"], dtype=np.float32)
    wo = np.asarray(inputs["L0_attn_wo"], dtype=np.float32)
    qnw = np.asarray(inputs["L0_attn_q_norm_w"], dtype=np.float32).reshape(-1)
    knw = np.asarray(inputs["L0_attn_k_norm_w"], dtype=np.float32).reshape(-1)
    w1 = np.asarray(inputs["L0_attn_imp_w1"], dtype=np.float32)
    b1 = np.asarray(inputs["L0_attn_imp_b1"], dtype=np.float32).reshape(1, 64)
    w2 = np.asarray(inputs["L0_attn_imp_w2"], dtype=np.float32).reshape(1, 64)
    b2 = np.asarray(inputs["L0_attn_imp_b2"], dtype=np.float32).reshape(1, 1)
    ka = np.asarray(inputs["L0_attn_kron_a"], dtype=np.float32)
    kb = np.asarray(inputs["L0_attn_kron_b"], dtype=np.float32)
    ss = np.asarray(inputs["L0_attn_sketch_scale"], dtype=np.float32).reshape(1, 1)
    Wb = np.asarray(inputs["L0_ffn_W_base"], dtype=np.float32)
    bb = np.asarray(inputs["L0_ffn_b_base"], dtype=np.float32).reshape(-1)
    rw = np.asarray(inputs["L0_ffn_router_W"], dtype=np.float32)
    A = np.asarray(inputs["L0_ffn_A"], dtype=np.float32)
    Bw = np.asarray(inputs["L0_ffn_Bw"], dtype=np.float32)

    ident = np.eye(128, dtype=np.float32)
    iota = np.arange(S, dtype=np.uint32).reshape(TC, 128).T.copy()
    sela_m = np.zeros((32, 5 * 128), np.float32)
    selc_m = np.zeros((20, 5 * 128), np.float32)
    for j in range(5):
        for p in range(128):
            sv = 128 * j + p
            sela_m[sv // 20, j * 128 + p] = 1.0
            selc_m[sv % 20, j * 128 + p] = 1.0
    kaT_row = np.zeros((1, 320), np.float32)
    for jp in range(16):
        kaT_row[0, jp * 20:(jp + 1) * 20] = ka[:, jp]
    es = np.zeros((8, 128), np.float32)
    for e in range(8):
        es[e, e * 16:(e + 1) * 16] = 1.0

    maps = []
    for core in range(8):
        b, hf = core // 2, core % 2
        m = {
            "xT": np.ascontiguousarray(x[b].T),
            "xTh": np.ascontiguousarray(x[b].T[:, hf * HTOK:(hf + 1) * HTOK]),
            "cos_tm": cos, "sin_tm": sin,
            "wq_c": np.ascontiguousarray(wq[:, hf * NQ:(hf + 1) * NQ]),
            "wk_c": np.ascontiguousarray(wk[:, hf * NKVD:(hf + 1) * NKVD]),
            "wv_c": np.ascontiguousarray(wv[:, hf * NKVD:(hf + 1) * NKVD]),
            "imp_w1": w1, "imp_w2r": w2, "imp_b1r": b1, "imp_b2": b2,
            "qnw_col": qnw.reshape(128, 1), "knw_col": knw.reshape(128, 1),
            "qnw_row": qnw.reshape(1, 128), "knw_row": knw.reshape(1, 128),
            "kbT": np.ascontiguousarray(kb.T), "kb_n": kb, "ka_n": ka,
            "kaT_row": kaT_row, "sela": sela_m, "selc": selc_m,
            "wo_c": np.ascontiguousarray(wo[hf * NQ:(hf + 1) * NQ, :]),
            "wbT": np.ascontiguousarray(Wb.T),
            "b_base_cm": np.ascontiguousarray(bb.reshape(NF, 128).T),
            "rwT": np.ascontiguousarray(rw.T),
            "aT": np.ascontiguousarray(A.reshape(128, DM).T),
            "bw_er": np.ascontiguousarray(Bw.transpose(0, 2, 1).reshape(128, DM)),
            "esel": es, "ident": ident, "iota_cm": iota, "ss_in": ss,
        }
        maps.append({k: np.ascontiguousarray(v) for k, v in m.items()})
    return maps


_CACHE = {}


def kernel(**inputs):
    if "nc" not in _CACHE:
        _CACHE["nc"] = build_program(phases=7)[0]
    nc = _CACHE["nc"]
    maps = make_in_maps(inputs)
    res = run_bass_kernel_spmd(nc, maps, core_ids=list(range(8)))
    out = np.empty((B, S, DM), np.float32)
    for core in range(8):
        b, hf = core // 2, core % 2
        out[b, hf * HTOK:(hf + 1) * HTOK, :] = np.asarray(res.results[core]["out_fm"]).T
    return out


# version 06 (final)
# speedup vs baseline: 1.0176x; 1.0072x over previous
import numpy as np
import concourse.bass as bass
import concourse.mybir as mybir
from concourse.bass_utils import run_bass_kernel_spmd
from concourse.bass_core_rust import IndirectOffsetOnAxis
from concourse.library_overlay import lower_extended_insts

F32 = mybir.dt.float32
BF16 = mybir.dt.bfloat16
U32 = mybir.dt.uint32
AF = mybir.ActivationFunctionType
ALU = mybir.AluOpType
AX = mybir.AxisListType.X

B, S, DM = 4, 4096, 2048
H, KVH, D = 16, 4, 128
SKN, TK = 640, 2048
NH, NKV = 8, 2
TC = S // 128
NF = DM // 128
NQ = NH * D
NKVD = NKV * D
CT = TK + SKN
NCT = CT // 128
NTK = TK // 128
HTOK = S // 2
HTC = HTOK // 128
LOGSHIFT = float(np.log(S / SKN))
ISQD = float(1.0 / np.sqrt(D))
EPS1, EPS2 = 1e-5, 1e-6
BISECT = 34
BIGPOS = 60000.0

_SZ = {F32: 4, BF16: 2, U32: 4}


class Arena:
    def __init__(self, nc):
        self.nc = nc
        self.off = (nc.sbuf_base + 63) // 64 * 64
        self.top = nc.sbuf_top
        self.marks = {}

    def alloc(self, name, shape, dtype, align=64):
        nbytes = int(np.prod(shape[1:])) * _SZ[dtype]
        self.off = (self.off + align - 1) // align * align
        t = self.nc.alloc_sbuf_tensor_at(name, list(shape), dtype, offset=self.off)
        self.off += nbytes
        assert self.off <= self.top, f"SBUF overflow at {name}: {self.off}"
        return t.ap()

    def mark(self, name):
        self.marks[name] = self.off

    def reset_to(self, name):
        self.off = self.marks[name]


def build_program(dbg=(), phases=99):
    dbg = set(dbg)
    nc = bass.Bass("TRN2", target_bir_lowering=False)

    def din(name, shape, dtype=F32):
        return nc.dram_tensor(name, list(shape), dtype, kind="ExternalInput").ap()

    xT = din("xT", [DM, S])
    xTh = din("xTh", [DM, HTOK])
    cos_tm = din("cos_tm", [S, D])
    sin_tm = din("sin_tm", [S, D])
    wq_c = din("wq_c", [DM, NQ])
    wk_c = din("wk_c", [DM, NKVD])
    wv_c = din("wv_c", [DM, NKVD])
    imp_w1 = din("imp_w1", [DM, 64])
    imp_w2r = din("imp_w2r", [1, 64])
    imp_b1r = din("imp_b1r", [1, 64])
    imp_b2 = din("imp_b2", [1, 1])
    qnw_col = din("qnw_col", [128, 1])
    knw_col = din("knw_col", [128, 1])
    qnw_row = din("qnw_row", [1, 128])
    knw_row = din("knw_row", [1, 128])
    kbT = din("kbT", [256, 32])
    kb_n = din("kb_n", [32, 256])
    ka_n = din("ka_n", [20, 128])
    kaT_row = din("kaT_row", [1, 320])
    sela = din("sela", [32, 5 * 128])
    selc = din("selc", [20, 5 * 128])
    wo_c = din("wo_c", [NQ, DM])
    wbT = din("wbT", [DM, DM])
    b_base_cm = din("b_base_cm", [128, NF])
    rwT = din("rwT", [DM, 8])
    aT = din("aT", [DM, 128])
    bw_er = din("bw_er", [128, DM])
    esel = din("esel", [8, 128])
    ident = din("ident", [128, 128])
    iota_cm = din("iota_cm", [128, TC], U32)
    ss_in = din("ss_in", [1, 1])

    out_fm = nc.dram_tensor("out_fm", [DM, HTOK], F32, kind="ExternalOutput").ap()
    qd_dram = nc.dram_tensor("qd_dram", [TK, NQ], BF16).ap()
    kd_dram = nc.dram_tensor("kd_dram", [TK, NKVD], BF16).ap()
    vd_dram = nc.dram_tensor("vd_dram", [TK, NKVD], BF16).ap()
    cosd_dram = nc.dram_tensor("cosd_dram", [TK, D], BF16).ap()
    sind_dram = nc.dram_tensor("sind_dram", [TK, D], BF16).ap()
    outdet_dram = nc.dram_tensor("outdet_dram", [TK, NQ], BF16).ap()
    full_dram = nc.dram_tensor("full_dram", [S, NQ], BF16).ap()
    ccin = nc.dram_tensor("ccin", [DM, S], BF16).ap()
    ccout = nc.dram_tensor("ccout", [DM, S], BF16).ap()

    dbg_outs = {}

    def dout(name, shape, dtype=F32):
        ap = nc.dram_tensor("dbg_" + name, list(shape), dtype, kind="ExternalOutput").ap()
        dbg_outs[name] = "dbg_" + name
        return ap

    ar = Arena(nc)
    ident_f = ar.alloc("ident_f", [128, 128], F32)
    ident_b = ar.alloc("ident_b", [128, 128], BF16)
    ones_col_b = ar.alloc("ones_col_b", [128, 1], BF16)
    ones_col_f = ar.alloc("ones_col_f", [128, 1], F32)
    ones_row_b = ar.alloc("ones_row_b", [1, 128], BF16)
    ones_row_f = ar.alloc("ones_row_f", [1, 128], F32)
    small_f = ar.alloc("small_f", [1, 128], F32)
    w2bc = ar.alloc("w2bc", [128, 64], F32)
    b2bias = ar.alloc("b2bias", [128, 1], F32)
    b1row_b = ar.alloc("b1row_b", [1, 64], BF16)
    qw_col = ar.alloc("qw_col", [128, 1], F32)
    kw_col = ar.alloc("kw_col", [128, 1], F32)
    qw_rows = ar.alloc("qw_rows", [1, 256], F32)
    kw_rows = ar.alloc("kw_rows", [1, 256], F32)
    ssbc = ar.alloc("ssbc", [128, 1], F32)
    eps2_col = ar.alloc("eps2_col", [128, 1], F32)
    eps1_row = ar.alloc("eps1_row", [1, 1], F32)
    qwss = ar.alloc("qwss", [128, 1], F32)
    kwss = ar.alloc("kwss", [128, 1], F32)
    kabc = ar.alloc("kabc", [128, 320], BF16)
    kbT_sb = ar.alloc("kbT_sb", [128, 64], BF16)
    logits_cm = ar.alloc("logits_cm", [128, TC], F32)
    weights_cm = ar.alloc("weights_cm", [128, TC], F32)
    restw_cm = ar.alloc("restw_cm", [128, TC], F32)
    sel_cm = ar.alloc("sel_cm", [128, TC], F32)
    pos_cm = ar.alloc("pos_cm", [128, TC], U32)
    selu_cm = ar.alloc("selu_cm", [128, TC], U32)
    ar.mark("persist")

    r1b = ar.alloc("r1b", [1, S], BF16)
    nmr1b = ar.alloc("nmr1b", [1, S], BF16)
    wbar = ar.alloc("wbar", [1, 1600], BF16)
    wrhs = [ar.alloc(f"wrhs{f}", [128, 1600], BF16) for f in range(NF)]
    rmsq = [ar.alloc(f"rmsq{i}", [128, 8], F32) for i in range(2)]
    rmsk = [ar.alloc(f"rmsk{i}", [128, 8], F32) for i in range(2)]
    tanh_sb = [ar.alloc(f"tanh{i}", [128, 64], F32) for i in range(2)]
    scr_act = ar.alloc("scr_act", [128, 20 * 128], BF16)
    scr_ttr = ar.alloc("scr_ttr", [128, 2 * 64], F32)
    ar.mark("p1ab")
    vecmu = ar.alloc("vecmu", [1, 1024], F32)
    vec1 = ar.alloc("vec1", [1, 1024], F32)
    vec2 = ar.alloc("vec2", [1, 1024], F32)
    xstat = [ar.alloc(f"xstat{i}", [128, 1024], BF16) for i in range(2)]
    sq_b = [ar.alloc(f"sq_b{i}", [128, 1024], BF16) for i in range(2)]
    ar.reset_to("p1ab")
    xw = [ar.alloc(f"xw{f}", [128, 512], BF16) for f in range(NF)]
    rbc_b = ar.alloc("rbc_b", [128, 512], BF16)
    ar.mark("p1end")
    qhat = [ar.alloc(f"qhat{t}", [128, NQ], BF16) for t in range(TC)]
    khat = [ar.alloc(f"khat{t}", [128, NKVD], BF16) for t in range(TC)]
    vhat = [ar.alloc(f"vhat{t}", [128, NKVD], BF16) for t in range(TC)]
    ar.mark("qkv_end")

    pbank = [nc.alloc_psum_tensor(f"pb{i}", [128, 512], F32).ap() for i in range(8)]

    sem_cnt, sem_h = {}, {}

    def sem(name):
        if name not in sem_h:
            sem_h[name] = nc.alloc_semaphore(name)
            sem_cnt[name] = 0
        return sem_h[name]

    def inc(instr, name, k=1):
        instr.then_inc(sem(name), k)
        sem_cnt[name] += k
        return instr

    def cnt(name):
        sem(name)
        return sem_cnt[name]

    def wchain(e, name):
        e.wait_ge(sem(name), cnt(name))



    def fence(eng, instr, tag):
        inc(instr, tag)
        eng.wait_ge(sem(tag), cnt(tag))

    with nc.Block() as block:
        @block.gpsimd
        def _(g):
            inc(g.dma_start(out=ident_b[:], in_=ident[:]), "ld_g", 16)
            inc(g.dma_start(out=kbT_sb[:, 0:32], in_=kbT[0:128, :]), "ld_g", 16)
            inc(g.dma_start(out=kbT_sb[:, 32:64], in_=kbT[128:256, :]), "ld_g", 16)
            inc(g.dma_start(out=b1row_b[:], in_=imp_b1r[:]), "ld_g", 16)
            for f in range(NF):
                inc(g.dma_start(out=wrhs[f][:, 0:1024], in_=wq_c[f * 128:(f + 1) * 128, :]), "ld_g", 16)
                inc(g.dma_start(out=wrhs[f][:, 1024:1280], in_=wk_c[f * 128:(f + 1) * 128, :]), "ld_g", 16)
                inc(g.dma_start(out=wrhs[f][:, 1280:1536], in_=wv_c[f * 128:(f + 1) * 128, :]), "ld_g", 16)
                inc(g.dma_start(out=wrhs[f][:, 1536:1600], in_=imp_w1[f * 128:(f + 1) * 128, :]), "ld_g", 16)
        ld_g_p0 = cnt("ld_g")

        @block.sync
        def _(sy):
            inc(sy.dma_start(out=ident_f[:], in_=ident[:]), "ld_s", 16)
            inc(sy.dma_start(out=qw_col[:], in_=qnw_col[:]), "ld_s", 16)
            inc(sy.dma_start(out=kw_col[:], in_=knw_col[:]), "ld_s", 16)
            inc(sy.dma_start(out=qw_rows[0:1, 0:128], in_=qnw_row[:]), "ld_s", 16)
            inc(sy.dma_start(out=kw_rows[0:1, 0:128], in_=knw_row[:]), "ld_s", 16)
            inc(sy.dma_start(out=small_f[0:1, 0:64], in_=imp_w2r[:]), "ld_s", 16)
            inc(sy.dma_start(out=small_f[0:1, 64:65], in_=imp_b2[:]), "ld_s", 16)
            inc(sy.dma_start(out=small_f[0:1, 65:66], in_=ss_in[:]), "ld_s", 16)
        ld_s_p0 = cnt("ld_s")

        @block.vector
        def _(v):
            v.memset(eps2_col[:], EPS2)
            v.memset(eps1_row[:], EPS1)
            v.memset(ones_col_b[:], 1.0)
            v.memset(ones_col_f[:], 1.0)
            v.memset(ones_row_b[:], 1.0)
            v.memset(ones_row_f[:], 1.0)
            v.wait_ge(sem("ld_s"), ld_s_p0)
            v.tensor_copy(qw_rows[0:1, 128:192], qw_rows[0:1, 64:128])
            v.tensor_copy(qw_rows[0:1, 192:256], qw_rows[0:1, 0:64])
            v.tensor_copy(kw_rows[0:1, 128:192], kw_rows[0:1, 64:128])
            inc(v.tensor_copy(kw_rows[0:1, 192:256], kw_rows[0:1, 0:64]), "c0v")
        c0v1 = cnt("c0v")

        @block.tensor
        def _(t):
            t.wait_ge(sem("c0v"), c0v1)
            t.matmul(pbank[5][:, 0:64], ones_row_f[:], small_f[0:1, 0:64], start=True, stop=True)
            inc(t.matmul(pbank[5][:, 64:66], ones_row_f[:], small_f[0:1, 64:66], start=True, stop=True),
                "c0t")
        c0t1 = cnt("c0t")

        @block.vector
        def _(v):
            v.wait_ge(sem("c0t"), c0t1)
            v.tensor_copy(w2bc[:], pbank[5][:, 0:64])
            v.tensor_scalar(b2bias[:], pbank[5][:, 64:65], -LOGSHIFT, None, op0=ALU.add)
            fence(v, v.tensor_copy(ssbc[:], pbank[5][:, 65:66]), "fl_v")
            v.tensor_mul(qwss[:], qw_col[:], ssbc[:])
            inc(v.tensor_mul(kwss[:], kw_col[:], ssbc[:]), "c0v")
        c0v2 = cnt("c0v")

        psum_sum = [pbank[i][0:1, :] for i in range(2)]
        psum_sq = [pbank[2 + i][0:1, :] for i in range(2)]

        @block.gpsimd
        def _(g):
            for k in range(4 * NF):
                qtr, f = divmod(k, NF)
                i = k % 2
                if k >= 2:
                    g.wait_ge(sem("stat_used"), k - 1)
                inc(g.dma_start(out=xstat[i][:],
                                in_=xT[f * 128:(f + 1) * 128, qtr * 1024:(qtr + 1) * 1024]),
                    f"ld_x{i}", 16)

        @block.scalar
        def _(s):
            for k in range(4 * NF):
                i = k % 2
                s.wait_ge(sem(f"ld_x{i}"), (k // 2 + 1) * 16)
                if k >= 2:
                    s.wait_ge(sem("stat_used"), k - 1)
                inc(s.activation(sq_b[i][:], xstat[i][:], AF.Square), "sq1")
                if (k + 1) % NF == 0:
                    qtr = (k + 1) // NF - 1
                    s.wait_ge(sem("stats_evac"), qtr + 1)
                    fence(s, s.activation(vec2[:], vec2[:], AF.Ln, bias=eps2_col[0:1, 0:1] if False else eps1_row[:]), "fl_s")
                    inc(s.activation(vec2[:], vec2[:], AF.Exp, scale=-0.5), "rstd1")

        @block.tensor
        def _(t):
            t.wait_ge(sem("c0v"), c0v2)
            for k in range(4 * NF):
                qtr, f = divmod(k, NF)
                i = k % 2
                t.wait_ge(sem("sq1"), k + 1)
                t.wait_ge(sem(f"ld_x{i}"), (k // 2 + 1) * 16)
                if f == 0 and qtr >= 1:
                    t.wait_ge(sem("stats_evac"), qtr)
                fi, la = f == 0, f == NF - 1
                for w in range(2):
                    t.matmul(psum_sum[w], ones_col_b[:], xstat[i][:, w * 512:(w + 1) * 512],
                             start=fi, stop=la)
                    mmi = t.matmul(psum_sq[w], ones_col_b[:], sq_b[i][:, w * 512:(w + 1) * 512],
                                   start=fi, stop=la)
                inc(mmi, "stat_used")

        @block.vector
        def _(v):
            for qtr in range(4):
                v.wait_ge(sem("stat_used"), NF * (qtr + 1))
                sl = slice(qtr * 1024, (qtr + 1) * 1024)
                for w in range(2):
                    v.tensor_scalar(vecmu[0:1, w * 512:(w + 1) * 512],
                                    psum_sum[w], 1.0 / DM, None, op0=ALU.mult)
                    ii = v.tensor_scalar(vec1[0:1, w * 512:(w + 1) * 512],
                                         psum_sq[w], 1.0 / DM, None, op0=ALU.mult)
                fence(v, ii, "fl_v")
                fence(v, v.tensor_mul(vec2[:], vecmu[:], vecmu[:]), "fl_v")
                inc(v.scalar_tensor_tensor(vec2[:], vec2[:], -1.0, vec1[:],
                                            op0=ALU.mult, op1=ALU.add), "stats_evac")
                v.wait_ge(sem("rstd1"), qtr + 1)
                inc(v.scalar_tensor_tensor(nmr1b[0:1, sl], vecmu[:], -1.0, vec2[:],
                                           op0=ALU.mult, op1=ALU.mult), "ln1_rdy")
                fence(v, v.tensor_copy(r1b[0:1, sl], vec2[:]), "fl_v")

        fl_v_after_stats = cnt("fl_v")

        @block.tensor
        def _(t):
            t.wait_ge(sem("ld_g"), ld_g_p0)
            t.wait_ge(sem("stats_evac"), 4)
            for f in range(NF):
                fi, la = f == 0, f == NF - 1
                for w in range(3):
                    t.matmul(pbank[w][0:1, 0:512], ones_col_b[:], wrhs[f][:, w * 512:(w + 1) * 512],
                             start=fi, stop=la)
                mmi = t.matmul(pbank[3][0:1, 0:64], ones_col_b[:], wrhs[f][:, 1536:1600],
                               start=fi, stop=la)
            inc(mmi, "tks")

        @block.vector
        def _(v):
            wchain(v, "tks")
            for w in range(3):
                v.tensor_copy(wbar[0:1, w * 512:(w + 1) * 512], pbank[w][0:1, 0:512])
            inc(v.tensor_copy(wbar[0:1, 1536:1600], pbank[3][0:1, 0:64]), "tks")

        for wdw in range(8):
            tok0 = wdw * 512

            @block.gpsimd
            def _(g):
                if wdw == 0:
                    g.wait_ge(sem("stat_used"), 4 * NF)
                    g.wait_ge(sem("fl_v"), fl_v_after_stats)
                else:
                    g.wait_ge(sem("proj_mm"), 4 * wdw)
                for f in range(NF):
                    inc(g.dma_start(out=xw[f][:],
                                    in_=xT[f * 128:(f + 1) * 128, tok0:tok0 + 512]),
                        "ld_xw", 16)
            ld_xw_w = cnt("ld_xw")

            @block.tensor
            def _(t):
                if wdw == 0:
                    wchain(t, "tks")
                else:
                    t.wait_ge(sem("proj_evac"), 4 * wdw - 1)
                t.wait_ge(sem("ln1_rdy"), wdw // 2 + 1)
                inc(t.matmul(pbank[0][:, 0:512], ones_row_b[:],
                             r1b[0:1, tok0:tok0 + 512], start=True, stop=True), "rbc_mm")

            @block.vector
            def _(v):
                v.wait_ge(sem("rbc_mm"), wdw + 1)
                fence(v, v.tensor_copy(rbc_b[:], pbank[0][:, 0:512]), "fl_v")
                v.wait_ge(sem("ld_xw"), ld_xw_w)
                for f in range(NF):
                    ii = v.tensor_mul(xw[f][:], xw[f][:], rbc_b[:])
                inc(ii, "xr_rdy")
            xr_w = cnt("xr_rdy")

            @block.tensor
            def _(t):
                t.wait_ge(sem("xr_rdy"), xr_w)
                for tci in range(4):
                    tc = wdw * 4 + tci
                    c = slice(tci * 128, (tci + 1) * 128)
                    par = tc % 2
                    qp0, qp1, kvp = pbank[4 * par], pbank[4 * par + 1], pbank[4 * par + 2]
                    imp = pbank[4 * par + 3][:, 0:64]
                    if tc >= 2:
                        t.wait_ge(sem("proj_evac"), tc - 1)
                    for f in range(NF):
                        fi = f == 0
                        t.matmul(qp0[:, 0:512], xw[f][:, c], wrhs[f][:, 0:512], start=fi, stop=False)
                        t.matmul(qp1[:, 0:512], xw[f][:, c], wrhs[f][:, 512:1024], start=fi, stop=False)
                        t.matmul(kvp[:, 0:512], xw[f][:, c], wrhs[f][:, 1024:1536], start=fi, stop=False)
                        t.matmul(imp, xw[f][:, c], wrhs[f][:, 1536:1600], start=fi, stop=False)
                    nm = nmr1b[0:1, tok0 + tci * 128:tok0 + (tci + 1) * 128]
                    t.matmul(qp0[:, 0:512], nm, wbar[0:1, 0:512], start=False, stop=True)
                    t.matmul(qp1[:, 0:512], nm, wbar[0:1, 512:1024], start=False, stop=True)
                    t.matmul(kvp[:, 0:512], nm, wbar[0:1, 1024:1536], start=False, stop=True)
                    t.matmul(imp, nm, wbar[0:1, 1536:1600], start=False, stop=False)
                    inc(t.matmul(imp, ones_row_b[:], b1row_b[:], start=False, stop=True), "proj_mm")

            @block.scalar
            def _(s):
                for tci in range(4):
                    tc = wdw * 4 + tci
                    par = tc % 2
                    qp0, qp1, kvp = pbank[4 * par], pbank[4 * par + 1], pbank[4 * par + 2]
                    imp = pbank[4 * par + 3][:, 0:64]
                    s.wait_ge(sem("proj_mm"), tc + 1)
                    if tc >= 2:
                        s.wait_ge(sem("proj_evac"), tc - 1)
                    for h in range(4):
                        sa = scr_act[:, (par * 10 + h) * 128:(par * 10 + h + 1) * 128]
                        s.activation(sa, qp0[:, h * 128:(h + 1) * 128], AF.Square,
                                     accum_out=rmsq[par][:, h:h + 1])
                    for h in range(4):
                        sa = scr_act[:, (par * 10 + 4 + h) * 128:(par * 10 + 5 + h) * 128]
                        s.activation(sa, qp1[:, h * 128:(h + 1) * 128], AF.Square,
                                     accum_out=rmsq[par][:, 4 + h:5 + h])
                    for g2 in range(2):
                        sa = scr_act[:, (par * 10 + 8 + g2) * 128:(par * 10 + 9 + g2) * 128]
                        ii = s.activation(sa, kvp[:, g2 * 128:(g2 + 1) * 128], AF.Square,
                                          accum_out=rmsk[par][:, g2:g2 + 1])
                    fence(s, ii, "fl_s")
                    s.activation(rmsq[par][:], rmsq[par][:], AF.Ln, scale=1.0 / D, bias=eps2_col[:])
                    fence(s, s.activation(rmsk[par][:, 0:2], rmsk[par][:, 0:2], AF.Ln,
                                          scale=1.0 / D, bias=eps2_col[:]), "fl_s")
                    s.activation(rmsq[par][:], rmsq[par][:], AF.Exp, scale=-0.5)
                    s.activation(rmsk[par][:, 0:2], rmsk[par][:, 0:2], AF.Exp, scale=-0.5)
                    inc(s.activation(tanh_sb[par][:], imp, AF.Tanh), "act_tc")

            @block.vector
            def _(v):
                for tci in range(4):
                    tc = wdw * 4 + tci
                    par = tc % 2
                    qp0, qp1, kvp = pbank[4 * par], pbank[4 * par + 1], pbank[4 * par + 2]
                    v.wait_ge(sem("act_tc"), tc + 1)
                    for h in range(8):
                        qp = qp0 if h < 4 else qp1
                        v.tensor_scalar(qhat[tc][:, h * 128:(h + 1) * 128],
                                        qp[:, (h % 4) * 128:(h % 4 + 1) * 128],
                                        rmsq[par][:, h:h + 1], None, op0=ALU.mult)
                    for g2 in range(2):
                        v.tensor_scalar(khat[tc][:, g2 * 128:(g2 + 1) * 128],
                                        kvp[:, g2 * 128:(g2 + 1) * 128],
                                        rmsk[par][:, g2:g2 + 1], None, op0=ALU.mult)
                    v.tensor_copy(vhat[tc][:], kvp[:, 256:512])
                    inc(v.tensor_tensor_reduce(scr_ttr[:, par * 64:(par + 1) * 64],
                                               tanh_sb[par][:], w2bc[:], 1.0, 0.0,
                                               op0=ALU.mult, op1=ALU.add,
                                               accum_out=logits_cm[:, tc:tc + 1]), "proj_evac")
        proj_done = cnt("proj_evac")

        if dbg and phases < 2:
            @block.sync
            def _(sy):
                sy.wait_ge(sem("proj_evac"), proj_done)
                if "logits" in dbg:
                    inc(sy.dma_start(out=dout("logits", [128, TC]), in_=logits_cm[:]), "dbg_st", 16)
                if "r1" in dbg:
                    inc(sy.dma_start(out=dout("r1", [1, S], BF16), in_=r1b[:]), "dbg_st", 16)
                if "qhat" in dbg:
                    o = dout("qhat", [S, NQ], BF16)
                    for tc in range(TC):
                        inc(sy.dma_start(out=o[tc * 128:(tc + 1) * 128, :], in_=qhat[tc][:]), "dbg_st", 16)
                if "khat" in dbg:
                    o = dout("khat", [S, NKVD], BF16)
                    for tc in range(TC):
                        inc(sy.dma_start(out=o[tc * 128:(tc + 1) * 128, :], in_=khat[tc][:]), "dbg_st", 16)
                if "vhat" in dbg:
                    o = dout("vhat", [S, NKVD], BF16)
                    for tc in range(TC):
                        inc(sy.dma_start(out=o[tc * 128:(tc + 1) * 128, :], in_=vhat[tc][:]), "dbg_st", 16)
                sy.wait_ge(sem("dbg_st"), cnt("dbg_st"))

        if phases < 2:
            lower_extended_insts(nc)
            return nc, dbg_outs

        ar.reset_to("persist")
        qf = [ar.alloc(f"qf{h}", [128, CT], BF16) for h in range(NH)]
        kf = [ar.alloc(f"kf{g}", [128, CT], BF16) for g in range(NKV)]
        vtm = [[ar.alloc(f"vtm{g}_{j}", [128, 132], BF16) for j in range(NCT)] for g in range(NKV)]
        ar.mark("regA_end")
        assert ar.off <= ar.marks["p1end"], "phase2 regionA overflow"
        ar.reset_to("qkv_end")
        kaT_sb = ar.alloc("kaT_sb", [1, 320], BF16)
        cs_tile = [ar.alloc(f"cs_tile{i}", [128, 2 * D], BF16) for i in range(2)]
        logitsT = ar.alloc("logitsT", [32, 128], F32)
        tk_lo = ar.alloc("tk_lo", [32, 1], F32)
        tk_hi = ar.alloc("tk_hi", [32, 1], F32)
        tk_mid = ar.alloc("tk_mid", [32, 1], F32)
        tk_gt = ar.alloc("tk_gt", [32, 128], F32)
        tk_gt2 = ar.alloc("tk_gt2", [32, 128], F32)
        tk_pc = ar.alloc("tk_pc", [32, 1], F32)
        tk_sc = ar.alloc("tk_sc", [1, 1], F32)
        tk_cond = ar.alloc("tk_cond", [32, 1], U32)
        tk_condl = ar.alloc("tk_condl", [32, 1], U32)
        tk_z = ar.alloc("tk_z", [32, 128], F32)
        tk_scan = ar.alloc("tk_scan", [32, 128], F32)
        tk_row = ar.alloc("tk_row", [1, 64], F32)
        tk_off = ar.alloc("tk_off", [32, 1], F32)
        tk_d32 = ar.alloc("tk_d32", [32, 1], F32)
        tk_selu = ar.alloc("tk_selu", [32, 128], U32)
        tk_le = ar.alloc("tk_le", [32, 128], F32)
        tk_big = ar.alloc("tk_big", [32, 128], F32)
        tk_pos = ar.alloc("tk_pos", [32, 128], F32)
        skw_b = [ar.alloc(f"skw{i}", [128, 320], BF16) for i in range(2)]
        vsf = [ar.alloc(f"vsf{g}", [128, 640], BF16) for g in range(NKV)]
        ar.mark("p2end")

        @block.gpsimd
        def _(g):
            inc(g.dma_start(out=kaT_sb[:], in_=kaT_row[:]), "ld_g", 16)
        ld_g_p2 = cnt("ld_g")

        @block.tensor
        def _(t):
            t.wait_ge(sem("ld_g"), ld_g_p2)
            t.wait_ge(sem("proj_evac"), proj_done)
            wchain(t, "tks")
            inc(t.matmul(pbank[0][:, 0:320], ones_row_b[:], kaT_sb[:], start=True, stop=True),
                "tks")

        @block.vector
        def _(v):
            wchain(v, "tks")
            inc(v.tensor_copy(kabc[:], pbank[0][:, 0:320]), "tks")

        def skw_build(v, par, tc, shalf, wvec):
            a_sl = kbT_sb[:, (tc % 2) * 32 + shalf * 16:(tc % 2) * 32 + shalf * 16 + 16]
            in0 = a_sl.rearrange("p (a one) -> p a one", one=1).to_broadcast([128, 16, 20])
            jb = tc // 2
            in1 = kabc[0:128, jb * 20:(jb + 1) * 20].rearrange("p (one c) -> p one c", one=1)\
                .to_broadcast([128, 16, 20])
            o = skw_b[par].rearrange("p (a c) -> p a c", a=16)
            return v.scalar_tensor_tensor(o, in0, wvec[:, tc:tc + 1], in1,
                                          op0=ALU.mult, op1=ALU.mult)

        @block.scalar
        def _(s):
            s.wait_ge(sem("proj_evac"), proj_done)
            inc(s.activation(weights_cm[:], logits_cm[:], AF.Sigmoid, bias=b2bias[:]), "w_rdy")

        @block.tensor
        def _(t):
            t.wait_ge(sem("proj_evac"), proj_done)
            inc(t.transpose(pbank[1][0:32, 0:128], logits_cm[:], ident_f[:]), "tks")

        @block.vector
        def _(v):
            wchain(v, "tks")
            v.tensor_copy(logitsT[:], pbank[1][0:32, 0:128])
            fence(v, v.tensor_reduce(tk_pc[:, 0:1], w2bc[0:32, :], axis=AX,
                                     op=ALU.add, apply_absolute_value=True), "fl_v")
            fence(v, v.tensor_scalar(tk_hi[:], tk_pc[:, 0:1], 1.0, 1.0, op0=ALU.mult, op1=ALU.add), "fl_v")
            v.memset(tk_z[:], 0.0)
            inc(v.scalar_tensor_tensor(tk_lo[:], tk_hi[:], -1.0, tk_hi[:], op0=ALU.mult,
                                       op1=ALU.min), "tks")

        for it in range(BISECT):
            @block.vector
            def _(v):
                wchain(v, "tks")
                if it > 0:
                    v.tensor_scalar(tk_cond[:], pbank[3][0:32, 0:1], 2048.0, None, op0=ALU.is_gt)
                    fence(v, v.tensor_scalar(tk_condl[:], pbank[3][0:32, 0:1], 2048.0, None,
                                             op0=ALU.is_le), "fl_v")
                    v.copy_predicated(tk_lo[:], tk_cond[:], tk_mid[:])
                    fence(v, v.copy_predicated(tk_hi[:], tk_condl[:], tk_mid[:]), "fl_v")
                fence(v, v.tensor_add(tk_mid[:], tk_lo[:], tk_hi[:]), "fl_v")
                fence(v, v.tensor_scalar(tk_mid[:], tk_mid[:], 0.5, None, op0=ALU.mult), "fl_v")
                fence(v, v.tensor_scalar(tk_gt[:], logitsT[:], tk_mid[:, 0:1], None, op0=ALU.is_gt), "fl_v")
                inc(v.reduce_sum(tk_pc[:], tk_gt[:], axis=AX), "tks")

            @block.tensor
            def _(t):
                wchain(t, "tks")
                inc(t.matmul(pbank[2][0:1, 0:1], ones_col_f[0:32, :], tk_pc[:], start=True, stop=True),
                    "tks")

            @block.vector
            def _(v):
                wchain(v, "tks")
                inc(v.tensor_copy(tk_sc[:], pbank[2][0:1, 0:1]), "tks")

            @block.tensor
            def _(t):
                wchain(t, "tks")
                inc(t.matmul(pbank[3][0:32, 0:1], ones_row_f[0:1, 0:32], tk_sc[:], start=True, stop=True),
                    "tks")

        @block.vector
        def _(v):
            wchain(v, "tks")
            v.tensor_scalar(tk_cond[:], pbank[3][0:32, 0:1], 2048.0, None, op0=ALU.is_gt)
            fence(v, v.tensor_scalar(tk_condl[:], pbank[3][0:32, 0:1], 2048.0, None, op0=ALU.is_le), "fl_v")
            v.copy_predicated(tk_lo[:], tk_cond[:], tk_mid[:])
            fence(v, v.copy_predicated(tk_hi[:], tk_condl[:], tk_mid[:]), "fl_v")
            v.tensor_scalar(tk_gt[:], logitsT[:], tk_hi[:, 0:1], None, op0=ALU.is_gt)
            fence(v, v.tensor_scalar(tk_gt2[:], logitsT[:], tk_lo[:, 0:1], None, op0=ALU.is_gt), "fl_v")
            fence(v, v.tensor_sub(tk_gt2[:], tk_gt2[:], tk_gt[:]), "fl_v")
            inc(v.reduce_sum(tk_pc[:], tk_gt[:], axis=AX), "tks")

        @block.tensor
        def _(t):
            wchain(t, "tks")
            inc(t.matmul(pbank[2][0:1, 0:1], ones_col_f[0:32, :], tk_pc[:], start=True, stop=True),
                "tks")

        @block.vector
        def _(v):
            wchain(v, "tks")
            inc(v.tensor_scalar(tk_sc[:], pbank[2][0:1, 0:1], -1.0, 2048.0, op0=ALU.mult,
                                op1=ALU.add), "tks")

        @block.tensor
        def _(t):
            wchain(t, "tks")
            inc(t.matmul(pbank[3][0:32, 0:1], ones_row_f[0:1, 0:32], tk_sc[:], start=True, stop=True),
                "tks")

        def prefix_pass(v, t_sems, src, dst, tag):
            fence(v, v.tensor_tensor_scan(dst, src, tk_z[:], 0.0, op0=ALU.add, op1=ALU.add), "fl_v")
            inc(v.tensor_copy(tk_pc[:], dst[0:32, 127:128]), "tks")

        @block.vector
        def _(v):
            wchain(v, "tks")
            fence(v, v.tensor_copy(tk_d32[:], pbank[3][0:32, 0:1]), "fl_v")
            prefix_pass(v, None, tk_gt2[:], tk_scan[:], "pfb")

        @block.tensor
        def _(t):
            wchain(t, "tks")
            inc(t.transpose(pbank[4][0:1, 0:32], tk_pc[0:32, 0:1], ident_f[0:32, 0:32]), "tks")

        @block.vector
        def _(v):
            wchain(v, "tks")
            fence(v, v.tensor_copy(tk_row[0:1, 0:32], pbank[4][0:1, 0:32]), "fl_v")
            fence(v, v.memset(tk_row[0:1, 32:64], 0.0), "fl_v")
            fence(v, v.tensor_tensor_scan(tk_row[0:1, 32:64], tk_row[0:1, 0:32], tk_row[0:1, 32:64],
                                          0.0, op0=ALU.add, op1=ALU.add), "fl_v")
            v.memset(tk_row[0:1, 0:1], 0.0)
            inc(v.tensor_copy(tk_row[0:1, 1:32], tk_row[0:1, 32:63]), "tks")

        @block.tensor
        def _(t):
            wchain(t, "tks")
            inc(t.transpose(pbank[4][0:32, 0:1], tk_row[0:1, 0:32], ident_f[0:1, 0:1]), "tks")

        @block.vector
        def _(v):
            wchain(v, "tks")
            fence(v, v.tensor_copy(tk_off[:], pbank[4][0:32, 0:1]), "fl_v")
            fence(v, v.tensor_scalar(tk_scan[:], tk_scan[:], tk_off[:, 0:1], None, op0=ALU.add), "fl_v")
            fence(v, v.tensor_scalar(tk_le[:], tk_scan[:], tk_d32[:, 0:1], None, op0=ALU.is_le), "fl_v")
            fence(v, v.tensor_mul(tk_gt2[:], tk_gt2[:], tk_le[:]), "fl_v")
            fence(v, v.tensor_add(tk_gt[:], tk_gt[:], tk_gt2[:]), "fl_v")
            prefix_pass(v, None, tk_gt[:], tk_scan[:], "pfs")

        @block.tensor
        def _(t):
            wchain(t, "tks")
            inc(t.transpose(pbank[4][0:1, 0:32], tk_pc[0:32, 0:1], ident_f[0:32, 0:32]), "tks")

        @block.vector
        def _(v):
            wchain(v, "tks")
            fence(v, v.tensor_copy(tk_row[0:1, 0:32], pbank[4][0:1, 0:32]), "fl_v")
            fence(v, v.memset(tk_row[0:1, 32:64], 0.0), "fl_v")
            fence(v, v.tensor_tensor_scan(tk_row[0:1, 32:64], tk_row[0:1, 0:32], tk_row[0:1, 32:64],
                                          0.0, op0=ALU.add, op1=ALU.add), "fl_v")
            v.memset(tk_row[0:1, 0:1], 0.0)
            inc(v.tensor_copy(tk_row[0:1, 1:32], tk_row[0:1, 32:63]), "tks")

        @block.tensor
        def _(t):
            wchain(t, "tks")
            inc(t.transpose(pbank[4][0:32, 0:1], tk_row[0:1, 0:32], ident_f[0:1, 0:1]), "tks")

        @block.vector
        def _(v):
            wchain(v, "tks")
            fence(v, v.tensor_copy(tk_off[:], pbank[4][0:32, 0:1]), "fl_v")
            fence(v, v.tensor_scalar(tk_scan[:], tk_scan[:], tk_off[:, 0:1], None, op0=ALU.add), "fl_v")
            v.tensor_scalar(tk_pos[:], tk_scan[:], -1.0, None, op0=ALU.add)
            v.tensor_scalar(tk_selu[:], tk_gt[:], 0.5, None, op0=ALU.is_lt)
            fence(v, v.memset(tk_big[:], BIGPOS), "fl_v")
            inc(v.copy_predicated(tk_pos[:], tk_selu[:], tk_big[:]), "tks")

        @block.tensor
        def _(t):
            wchain(t, "tks")
            inc(t.transpose(pbank[5][0:128, 0:32], tk_pos[0:32, 0:128], ident_f[0:32, 0:32]), "tks")
            inc(t.transpose(pbank[6][0:128, 0:32], tk_gt[0:32, 0:128], ident_f[0:32, 0:32]), "tks")

        @block.vector
        def _(v):
            wchain(v, "tks")
            inc(v.tensor_copy(sel_cm[:], pbank[6][0:128, 0:32]), "tks")
            v.tensor_copy(selu_cm[:], pbank[6][0:128, 0:32])
            fence(v, v.tensor_copy(pos_cm[:], pbank[5][0:128, 0:32]), "fl_v")
            v.wait_ge(sem("w_rdy"), 1)
            fence(v, v.tensor_scalar(restw_cm[:], sel_cm[:], -1.0, 1.0, op0=ALU.mult, op1=ALU.add), "fl_v")
            inc(v.tensor_mul(restw_cm[:], weights_cm[:], restw_cm[:]), "rw_rdy")

        pos_fl = cnt("fl_v")

        @block.gpsimd
        def _(g):
            g.wait_ge(sem("fl_v"), pos_fl)
            btk = g.to_reg(TK - 1)
            for tc in range(TC):
                inc(g.indirect_dma_start(
                    out=qd_dram[:], out_offset=IndirectOffsetOnAxis(pos_cm[:, tc:tc + 1], 0),
                    in_=qhat[tc][:], in_offset=None,
                    bounds_check=btk, oob_is_err=False), "sc_q", 16)
                inc(g.indirect_dma_start(
                    out=kd_dram[:], out_offset=IndirectOffsetOnAxis(pos_cm[:, tc:tc + 1], 0),
                    in_=khat[tc][:], in_offset=None,
                    bounds_check=btk, oob_is_err=False), "sc_k", 16)
                inc(g.indirect_dma_start(
                    out=vd_dram[:], out_offset=IndirectOffsetOnAxis(pos_cm[:, tc:tc + 1], 0),
                    in_=vhat[tc][:], in_offset=None,
                    bounds_check=btk, oob_is_err=False), "sc_v", 16)
        sc_q_all, sc_k_all = cnt("sc_q"), cnt("sc_k")
        sc_v_all = cnt("sc_v")

        @block.gpsimd
        def _(g):
            g.wait_ge(sem("fl_v"), pos_fl)
            btk2 = g.to_reg(TK - 1)
            for tc in range(TC):
                i = tc % 2
                if tc >= 2:
                    g.wait_ge(sem(f"sc_i{i}"), 32 * (tc // 2))
                inc(g.dma_start(out=cs_tile[i][:, 0:D],
                                in_=cos_tm[tc * 128:(tc + 1) * 128, :]), f"cs_ld{i}", 16)
                inc(g.dma_start(out=cs_tile[i][:, D:2 * D],
                                in_=sin_tm[tc * 128:(tc + 1) * 128, :]), f"cs_ld{i}", 16)
                g.wait_ge(sem(f"cs_ld{i}"), 32 * (tc // 2 + 1))
                inc(g.indirect_dma_start(
                    out=cosd_dram[:], out_offset=IndirectOffsetOnAxis(pos_cm[:, tc:tc + 1], 0),
                    in_=cs_tile[i][:, 0:D], in_offset=None,
                    bounds_check=btk2, oob_is_err=False), f"sc_i{i}", 16)
                inc(g.indirect_dma_start(
                    out=sind_dram[:], out_offset=IndirectOffsetOnAxis(pos_cm[:, tc:tc + 1], 0),
                    in_=cs_tile[i][:, D:2 * D], in_offset=None,
                    bounds_check=btk2, oob_is_err=False), f"sc_i{i}", 16)
        sc_i_all = (cnt("sc_i0"), cnt("sc_i1"))

        for shalf in range(2):
            @block.vector
            def _(v):
                for tc in range(TC):
                    if shalf == 0 and tc == 0:
                        wchain(v, "tks")
                        v.wait_ge(sem("w_rdy"), 1)
                    if shalf * TC + tc >= 2:
                        v.wait_ge(sem("skw_used"), shalf * TC + tc - 1)
                    inc(skw_build(v, tc % 2, tc, shalf, weights_cm), "skw_rdy")

            @block.tensor
            def _(t):
                for tc in range(TC):
                    t.wait_ge(sem("skw_rdy"), shalf * TC + tc + 1)
                    if shalf == 1 and tc == 0:
                        t.wait_ge(sem("qs_evac"), 8)
                    fi, la = tc == 0, tc == TC - 1
                    for h in range(NH):
                        mmi = t.matmul(pbank[h][:, 0:320], qhat[tc][:, h * 128:(h + 1) * 128],
                                       skw_b[tc % 2][:], start=fi, stop=la)
                    inc(mmi, "skw_used")

            @block.vector
            def _(v):
                v.wait_ge(sem("skw_used"), (shalf + 1) * TC)
                for h in range(NH):
                    inc(v.tensor_scalar(qf[h][:, TK + shalf * 320:TK + shalf * 320 + 320],
                                        pbank[h][:, 0:320], qwss[:, 0:1], None, op0=ALU.mult),
                        "qs_evac")

        for shalf in range(2):
            @block.vector
            def _(v):
                for tc in range(TC):
                    kk = (2 + shalf) * TC + tc
                    if shalf == 0 and tc == 0:
                        v.wait_ge(sem("rw_rdy"), 1)
                    if kk >= 2:
                        v.wait_ge(sem("skw_used"), kk - 1)
                    inc(skw_build(v, tc % 2, tc, shalf, restw_cm), "skw_rdy")

            @block.tensor
            def _(t):
                for tc in range(TC):
                    kk = (2 + shalf) * TC + tc
                    t.wait_ge(sem("skw_rdy"), kk + 1)
                    if tc == 0 and shalf == 0:
                        t.wait_ge(sem("qs_evac"), 16)
                    if tc == 0 and shalf == 1:
                        t.wait_ge(sem("kvs_evac"), 4)
                    fi, la = tc == 0, tc == TC - 1
                    for g2 in range(NKV):
                        t.matmul(pbank[g2][:, 0:320], khat[tc][:, g2 * 128:(g2 + 1) * 128],
                                 skw_b[tc % 2][:], start=fi, stop=la)
                        mmi = t.matmul(pbank[2 + g2][:, 0:320], vhat[tc][:, g2 * 128:(g2 + 1) * 128],
                                       skw_b[tc % 2][:], start=fi, stop=la)
                    inc(mmi, "skw_used")

            @block.vector
            def _(v):
                v.wait_ge(sem("skw_used"), (3 + shalf) * TC)
                for g2 in range(NKV):
                    inc(v.tensor_scalar(kf[g2][:, TK + shalf * 320:TK + shalf * 320 + 320],
                                        pbank[g2][:, 0:320], kwss[:, 0:1], None, op0=ALU.mult),
                        "kvs_evac")
                    inc(v.tensor_scalar(vsf[g2][:, shalf * 320:shalf * 320 + 320],
                                        pbank[2 + g2][:, 0:320], ssbc[:, 0:1], None, op0=ALU.mult),
                        "kvs_evac")
        kvs_all = cnt("kvs_evac")

        @block.vector
        def _(v):
            v.wait_ge(sem("proj_evac"), proj_done)
            for g2 in range(NKV):
                for j in range(NCT):
                    v.memset(vtm[g2][j][:, 129:132], 0.0)
                    ii = v.memset(vtm[g2][j][:, 128:129], 1.0)
            inc(ii, "vtm_ones")

        for g2 in range(NKV):
            for j in range(5):
                jj = g2 * 5 + j

                @block.tensor
                def _(t):
                    if jj == 0:
                        t.wait_ge(sem("kvs_evac"), kvs_all)
                    if jj >= 2:
                        t.wait_ge(sem("vs_ev"), jj - 1)
                    inc(t.transpose(pbank[4 + jj % 2].bitcast(BF16)[:, 0:128],
                                    vsf[g2][:, j * 128:(j + 1) * 128], ident_b[:]), "vs_tr")

                @block.vector
                def _(v):
                    v.wait_ge(sem("vs_tr"), jj + 1)
                    inc(v.tensor_copy(vtm[g2][16 + j][:, 0:128],
                                      pbank[4 + jj % 2].bitcast(BF16)[:, 0:128]), "vs_ev")
        vs_all = cnt("vs_ev")

        if dbg and phases < 3:
            @block.sync
            def _(sy):
                sy.wait_ge(sem("vs_ev"), vs_all)
                wchain(sy, "tks")
                sy.wait_ge(sem("rw_rdy"), 1)
                sy.wait_ge(sem("qs_evac"), 16)
                sy.wait_ge(sem("sc_q"), sc_q_all)
                sy.wait_ge(sem("sc_k"), sc_k_all)
                sy.wait_ge(sem("sc_v"), sc_v_all)
                sy.wait_ge(sem("sc_i0"), sc_i_all[0])
                sy.wait_ge(sem("sc_i1"), sc_i_all[1])
                if "pos" in dbg:
                    inc(sy.dma_start(out=dout("pos", [128, TC], U32), in_=pos_cm[:]), "dbg_st", 16)
                if "sel" in dbg:
                    inc(sy.dma_start(out=dout("sel", [128, TC]), in_=sel_cm[:]), "dbg_st", 16)
                if "weights" in dbg:
                    inc(sy.dma_start(out=dout("weights", [128, TC]), in_=weights_cm[:]), "dbg_st", 16)
                if "restw" in dbg:
                    inc(sy.dma_start(out=dout("restw", [128, TC]), in_=restw_cm[:]), "dbg_st", 16)
                if "qs" in dbg:
                    o = dout("qs", [NH * 128, 640], BF16)
                    for h in range(NH):
                        inc(sy.dma_start(out=o[h * 128:(h + 1) * 128, :], in_=qf[h][:, TK:CT]), "dbg_st", 16)
                if "ks" in dbg:
                    o = dout("ks", [NKV * 128, 640], BF16)
                    for g2 in range(NKV):
                        inc(sy.dma_start(out=o[g2 * 128:(g2 + 1) * 128, :], in_=kf[g2][:, TK:CT]), "dbg_st", 16)
                if "vs" in dbg:
                    o = dout("vs", [NKV * 128, 640], BF16)
                    for g2 in range(NKV):
                        inc(sy.dma_start(out=o[g2 * 128:(g2 + 1) * 128, :], in_=vsf[g2][:]), "dbg_st", 16)
                if "cosd" in dbg:
                    inc(sy.dma_start(out=dout("cosd", [TK, D], BF16), in_=cosd_dram[:]), "dbg_st", 16)
                if "qd" in dbg:
                    inc(sy.dma_start(out=dout("qd", [TK, NQ], BF16), in_=qd_dram[:]), "dbg_st", 16)
                sy.wait_ge(sem("dbg_st"), cnt("dbg_st"))

        if phases < 3:
            lower_extended_insts(nc)
            return nc, dbg_outs

        ar.reset_to("p1end")
        qwb1024 = ar.alloc("qwb1024", [128, NQ], BF16)
        qwsw1024 = ar.alloc("qwsw1024", [128, NQ], BF16)
        kwb256 = ar.alloc("kwb256", [128, NKVD], BF16)
        kwsw256 = ar.alloc("kwsw256", [128, NKVD], BF16)
        qd_r = [ar.alloc(f"qd_r{i}", [128, NQ], BF16) for i in range(4)]
        kcs_r = [ar.alloc(f"kcs_r{i}", [128, NKVD + 2 * D], BF16) for i in range(4)]
        ropq = [ar.alloc(f"ropq{i}", [128, NQ], BF16) for i in range(2)]
        ropk = [ar.alloc(f"ropk{i}", [128, NKVD], BF16) for i in range(2)]
        cosq = [ar.alloc(f"cosq{i}", [128, NQ], BF16) for i in range(2)]
        sinq = [ar.alloc(f"sinq{i}", [128, NQ], BF16) for i in range(2)]
        cosk = [ar.alloc(f"cosk{i}", [128, NKVD], BF16) for i in range(2)]
        sink = [ar.alloc(f"sink{i}", [128, NKVD], BF16) for i in range(2)]
        rotq = [ar.alloc(f"rotq{i}", [128, NQ], BF16) for i in range(2)]
        rotk = [ar.alloc(f"rotk{i}", [128, NKVD], BF16) for i in range(2)]
        sgn = [ar.alloc(f"sgn{i}", [128, 2 * D], BF16) for i in range(2)]
        pt_sb = ar.alloc("pt_sb", [128, NCT * 512], BF16)
        att_sc = [ar.alloc(f"attsc{i}", [128, 132], BF16) for i in range(4)]
        rcp_sb = ar.alloc("rcp_sb", [128, 8], F32)
        ar.mark("p5top")
        out_sk = [ar.alloc(f"outsk{j}", [128, NQ], BF16) for j in range(5)]
        ar.mark("p4end")

        def p3_gate(e):
            e.wait_ge(sem("skw_used"), 128)
            e.wait_ge(sem("sc_q"), sc_q_all)
            e.wait_ge(sem("sc_k"), sc_k_all)
            e.wait_ge(sem("sc_v"), sc_v_all)
            e.wait_ge(sem("sc_i0"), sc_i_all[0])
            e.wait_ge(sem("sc_i1"), sc_i_all[1])
            e.wait_ge(sem("vs_ev"), vs_all)
            e.wait_ge(sem("qs_evac"), 16)
            e.wait_ge(sem("kvs_evac"), kvs_all)
            e.wait_ge(sem("vtm_ones"), 1)

        @block.tensor
        def _(t):
            p3_gate(t)
            rep8 = qw_rows[0:1, 0:128].rearrange("a (b d) -> a b d", b=1).to_broadcast([1, 8, 128])
            rep8s = qw_rows[0:1, 128:256].rearrange("a (b d) -> a b d", b=1).to_broadcast([1, 8, 128])
            rep2 = kw_rows[0:1, 0:128].rearrange("a (b d) -> a b d", b=1).to_broadcast([1, 2, 128])
            rep2s = kw_rows[0:1, 128:256].rearrange("a (b d) -> a b d", b=1).to_broadcast([1, 2, 128])
            t.matmul(pbank[6][:, 0:512], ones_row_f[:], rep8[0:1, 0:4, :], start=True, stop=True)
            t.matmul(pbank[7][:, 0:512], ones_row_f[:], rep8[0:1, 4:8, :], start=True, stop=True)
            inc(t.matmul(pbank[4][:, 0:256], ones_row_f[:], rep2[:], start=True, stop=True), "tks")

        @block.vector
        def _(v):
            p3_gate(v)
            wchain(v, "tks")
            v.tensor_copy(qwb1024[:, 0:512], pbank[6][:, 0:512])
            v.tensor_copy(qwb1024[:, 512:1024], pbank[7][:, 0:512])
            inc(v.tensor_copy(kwb256[:], pbank[4][:, 0:256]), "tks")

        @block.tensor
        def _(t):
            wchain(t, "tks")
            rep8s = qw_rows[0:1, 128:256].rearrange("a (b d) -> a b d", b=1).to_broadcast([1, 8, 128])
            rep2s = kw_rows[0:1, 128:256].rearrange("a (b d) -> a b d", b=1).to_broadcast([1, 2, 128])
            t.matmul(pbank[6][:, 0:512], ones_row_f[:], rep8s[0:1, 0:4, :], start=True, stop=True)
            t.matmul(pbank[7][:, 0:512], ones_row_f[:], rep8s[0:1, 4:8, :], start=True, stop=True)
            inc(t.matmul(pbank[4][:, 0:256], ones_row_f[:], rep2s[:], start=True, stop=True), "tks")

        @block.vector
        def _(v):
            wchain(v, "tks")
            v.tensor_copy(qwsw1024[:, 0:512], pbank[6][:, 0:512])
            v.tensor_copy(qwsw1024[:, 512:1024], pbank[7][:, 0:512])
            inc(v.tensor_copy(kwsw256[:], pbank[4][:, 0:256]), "tks")

        @block.sync
        def _(sy):
            p3_gate(sy)
            for j in range(NTK):
                i = j % 4
                if j >= 4:
                    sy.wait_ge(sem("rope_done"), j - 3)
                inc(sy.dma_start(out=qd_r[i][:], in_=qd_dram[j * 128:(j + 1) * 128, :]), f"g3_{i}", 16)
                inc(sy.dma_start(out=kcs_r[i][:, 0:NKVD], in_=kd_dram[j * 128:(j + 1) * 128, :]), f"g3_{i}", 16)
                inc(sy.dma_start(out=kcs_r[i][:, NKVD:NKVD + D], in_=cosd_dram[j * 128:(j + 1) * 128, :]), f"g3_{i}", 16)
                inc(sy.dma_start(out=kcs_r[i][:, NKVD + D:NKVD + 2 * D], in_=sind_dram[j * 128:(j + 1) * 128, :]), f"g3_{i}", 16)
                inc(sy.dma_start(out=vtm[0][j][:, 0:128], in_=vd_dram[j * 128:(j + 1) * 128, 0:128]), f"g3_{i}", 16)
                inc(sy.dma_start(out=vtm[1][j][:, 0:128], in_=vd_dram[j * 128:(j + 1) * 128, 128:256]), f"g3_{i}", 16)

        for j in range(NTK):
            i = j % 4
            par = j % 2

            @block.vector
            def _(v):
                v.wait_ge(sem(f"g3_{i}"), 96 * (j // 4 + 1))
                if j >= 2:
                    v.wait_ge(sem("q_trev"), 10 * (j - 1))
                cosd_t = kcs_r[i][:, NKVD:NKVD + D]
                sind_t = kcs_r[i][:, NKVD + D:NKVD + 2 * D]
                v.tensor_scalar(sgn[par][:, 0:64], sind_t[:, 0:64], -1.0, None, op0=ALU.mult)
                fence(v, v.tensor_copy(sgn[par][:, 64:128], sind_t[:, 64:128]), "fl_v")
                for hh in range(NH):
                    v.tensor_copy(rotq[par][:, hh * 128:hh * 128 + 64], qd_r[i][:, hh * 128 + 64:hh * 128 + 128])
                    v.tensor_copy(rotq[par][:, hh * 128 + 64:hh * 128 + 128], qd_r[i][:, hh * 128:hh * 128 + 64])
                for g2 in range(NKV):
                    v.tensor_copy(rotk[par][:, g2 * 128:g2 * 128 + 64], kcs_r[i][:, g2 * 128 + 64:g2 * 128 + 128])
                    v.tensor_copy(rotk[par][:, g2 * 128 + 64:g2 * 128 + 128], kcs_r[i][:, g2 * 128:g2 * 128 + 64])
                ce = cosd_t.rearrange("p (one d) -> p one d", one=1).to_broadcast([128, 8, 128])
                se = sgn[par][:, 0:128].rearrange("p (one d) -> p one d", one=1).to_broadcast([128, 8, 128])
                v.tensor_tensor(cosq[par].rearrange("p (r d) -> p r d", r=8), ce,
                                qwb1024.rearrange("p (r d) -> p r d", r=8), op=ALU.mult)
                v.tensor_tensor(sinq[par].rearrange("p (r d) -> p r d", r=8), se,
                                qwsw1024.rearrange("p (r d) -> p r d", r=8), op=ALU.mult)
                ce2 = cosd_t.rearrange("p (one d) -> p one d", one=1).to_broadcast([128, 2, 128])
                se2 = sgn[par][:, 0:128].rearrange("p (one d) -> p one d", one=1).to_broadcast([128, 2, 128])
                v.tensor_tensor(cosk[par].rearrange("p (r d) -> p r d", r=2), ce2,
                                kwb256.rearrange("p (r d) -> p r d", r=2), op=ALU.mult)
                fence(v, v.tensor_tensor(sink[par].rearrange("p (r d) -> p r d", r=2), se2,
                                         kwsw256.rearrange("p (r d) -> p r d", r=2), op=ALU.mult), "fl_v")
                v.tensor_mul(ropq[par][:], qd_r[i][:], cosq[par][:])
                v.tensor_mul(rotq[par][:], rotq[par][:], sinq[par][:])
                v.tensor_mul(ropk[par][:], kcs_r[i][:, 0:NKVD], cosk[par][:])
                fence(v, v.tensor_mul(rotk[par][:], rotk[par][:], sink[par][:]), "fl_v")
                v.tensor_add(ropq[par][:], ropq[par][:], rotq[par][:])
                inc(v.tensor_add(ropk[par][:], ropk[par][:], rotk[par][:]), "rope_done")

            @block.tensor
            def _(t):
                t.wait_ge(sem("rope_done"), j + 1)
                for hh in range(NH):
                    idx = j * 10 + hh
                    if idx >= 4:
                        t.wait_ge(sem("q_trev"), idx - 3)
                    inc(t.transpose(pbank[idx % 4].bitcast(BF16)[:, 0:128],
                                    ropq[par][:, hh * 128:(hh + 1) * 128], ident_b[:]), "q_tr")
                for g2 in range(NKV):
                    idx = j * 10 + 8 + g2
                    if idx >= 4:
                        t.wait_ge(sem("q_trev"), idx - 3)
                    inc(t.transpose(pbank[idx % 4].bitcast(BF16)[:, 0:128],
                                    ropk[par][:, g2 * 128:(g2 + 1) * 128], ident_b[:]), "q_tr")

            @block.vector
            def _(v):
                for hh in range(NH):
                    idx = j * 10 + hh
                    v.wait_ge(sem("q_tr"), idx + 1)
                    inc(v.tensor_copy(qf[hh][:, j * 128:(j + 1) * 128],
                                      pbank[idx % 4].bitcast(BF16)[:, 0:128]), "q_trev")
                for g2 in range(NKV):
                    idx = j * 10 + 8 + g2
                    v.wait_ge(sem("q_tr"), idx + 1)
                    inc(v.tensor_copy(kf[g2][:, j * 128:(j + 1) * 128],
                                      pbank[idx % 4].bitcast(BF16)[:, 0:128]), "q_trev")
        g3_tot = [cnt(f"g3_{i}") for i in range(4)]

        if dbg and phases < 4:
            @block.sync
            def _(sy):
                sy.wait_ge(sem("q_trev"), 10 * NTK)
                if "pos" in dbg:
                    inc(sy.dma_start(out=dout("pos", [128, TC], U32), in_=pos_cm[:]), "dbg_st", 16)
                if "sel" in dbg:
                    inc(sy.dma_start(out=dout("sel", [128, TC]), in_=sel_cm[:]), "dbg_st", 16)
                if "qfd" in dbg:
                    o = dout("qfd", [NH * 128, CT], BF16)
                    for hh in range(NH):
                        inc(sy.dma_start(out=o[hh * 128:(hh + 1) * 128, :], in_=qf[hh][:]), "dbg_st", 16)
                if "kfd" in dbg:
                    o = dout("kfd", [NKV * 128, CT], BF16)
                    for g2 in range(NKV):
                        inc(sy.dma_start(out=o[g2 * 128:(g2 + 1) * 128, :], in_=kf[g2][:]), "dbg_st", 16)
                if "vtmd" in dbg:
                    o = dout("vtmd", [NKV * NCT * 128, 132], BF16)
                    for g2 in range(NKV):
                        for jj in range(NCT):
                            inc(sy.dma_start(out=o[(g2 * NCT + jj) * 128:(g2 * NCT + jj + 1) * 128, :],
                                             in_=vtm[g2][jj][:]), "dbg_st", 16)
                sy.wait_ge(sem("dbg_st"), cnt("dbg_st"))

        if phases < 4:
            lower_extended_insts(nc)
            return nc, dbg_outs


        QCW = [512, 512, 512, 512, 512, 128]
        det_uses = [0, 0, 0, 0]
        n_sc = 0
        n_pv = 0
        n_ev = 0

        for h in range(NH):
            g2 = h // 4
            for qc in range(6):
                w = QCW[qc]
                q0 = qc * 512

                @block.tensor
                def _(t):
                    if h == 0 and qc == 0:
                        t.wait_ge(sem("q_trev"), 10 * NTK)
                    for kt in range(NCT):
                        idx = n_sc + kt
                        if idx >= 2:
                            t.wait_ge(sem("s_exp"), idx - 1)
                        inc(t.matmul(pbank[idx % 2][:, 0:w], kf[g2][:, kt * 128:(kt + 1) * 128],
                                     qf[h][:, q0:q0 + w], start=True, stop=True), "s_mm")

                @block.scalar
                def _(s):
                    s.wait_ge(sem("pv_mm"), n_pv)
                    for kt in range(NCT):
                        idx = n_sc + kt
                        s.wait_ge(sem("s_mm"), idx + 1)
                        inc(s.activation(pt_sb[:, kt * 512:kt * 512 + w], pbank[idx % 2][:, 0:w],
                                         AF.Exp, scale=ISQD), "s_exp")
                n_sc += NCT

                @block.tensor
                def _(t):
                    t.wait_ge(sem("s_exp"), n_sc)
                    for qsub in range(w // 128):
                        if n_pv + qsub >= 4:
                            t.wait_ge(sem("att_ev"), n_pv + qsub - 3)
                        for kt in range(NCT):
                            mmi = t.matmul(pbank[2 + (n_pv + qsub) % 4][:, 0:129],
                                           pt_sb[:, kt * 512 + qsub * 128:kt * 512 + (qsub + 1) * 128],
                                           vtm[g2][kt][:, 0:129],
                                           start=(kt == 0), stop=(kt == NCT - 1))
                        inc(mmi, "pv_mm")

                @block.vector
                def _(v):
                    v.wait_ge(sem("pv_mm"), n_pv + w // 128)
                    for qsub in range(w // 128):
                        v.reciprocal(rcp_sb[:, qsub:qsub + 1],
                                     pbank[2 + (n_pv + qsub) % 4][:, 128:129])
                    fence(v, v.memset(rcp_sb[:, 7:8], 0.0), "fl_v")
                    for qsub in range(w // 128):
                        i4 = (n_pv + qsub) % 4
                        r0 = q0 + qsub * 128
                        if r0 < TK:
                            if det_uses[i4] > 0:
                                v.wait_ge(sem(f"det_st{i4}"), 16 * det_uses[i4])
                            dst = att_sc[i4][:, 0:128]
                        else:
                            sch = (r0 - TK) // 128
                            dst = out_sk[sch][:, h * 128:(h + 1) * 128]
                        inc(v.tensor_scalar(dst, pbank[2 + i4][:, 0:128],
                                            rcp_sb[:, qsub:qsub + 1], None, op0=ALU.mult), "att_ev")

                @block.sync
                def _(sy):
                    for qsub in range(w // 128):
                        i4 = (n_pv + qsub) % 4
                        r0 = q0 + qsub * 128
                        if r0 < TK:
                            sy.wait_ge(sem("att_ev"), n_pv + qsub + 1)
                            inc(sy.dma_start(out=outdet_dram[r0:r0 + 128, h * 128:(h + 1) * 128],
                                             in_=att_sc[i4][:, 0:128]), f"det_st{i4}", 16)
                            det_uses[i4] += 1
                n_pv += w // 128
        att_all = cnt("att_ev")
        det_all = [cnt(f"det_st{i}") for i in range(4)]

        if dbg and phases < 5:
            @block.sync
            def _(sy):
                sy.wait_ge(sem("att_ev"), att_all)
                for i4 in range(4):
                    sy.wait_ge(sem(f"det_st{i4}"), det_all[i4])
                if "pos" in dbg:
                    inc(sy.dma_start(out=dout("pos", [128, TC], U32), in_=pos_cm[:]), "dbg_st", 16)
                if "sel" in dbg:
                    inc(sy.dma_start(out=dout("sel", [128, TC]), in_=sel_cm[:]), "dbg_st", 16)
                if "outdet" in dbg:
                    inc(sy.dma_start(out=dout("outdet", [TK, NQ], BF16), in_=outdet_dram[:]), "dbg_st", 16)
                if "outsk" in dbg:
                    o = dout("outsk", [5 * 128, NQ], BF16)
                    for j in range(5):
                        inc(sy.dma_start(out=o[j * 128:(j + 1) * 128, :], in_=out_sk[j][:]), "dbg_st", 16)
                sy.wait_ge(sem("dbg_st"), cnt("dbg_st"))

        if phases < 5:
            lower_extended_insts(nc)
            return nc, dbg_outs

        p5top = ar.marks["p5top"]
        ar.reset_to("p4end")
        wo_sb = [ar.alloc(f"wo_sb{hd}", [128, DM], BF16) for hd in range(6)]
        kb_sb = ar.alloc("kb_sb", [32, 256], BF16)
        ka_sb = ar.alloc("ka_sb", [20, 128], BF16)
        assert ar.off <= ar.marks["qkv_end"], f"phase5 D2 overflow: {ar.off}"
        ar.reset_to("persist")
        kexp = [ar.alloc(f"kexp{i}", [128, 384], BF16) for i in range(2)]
        skfm = [ar.alloc(f"skfm{j}", [128, S], BF16) for j in range(5)]
        full_sb = [ar.alloc(f"full_sb{i}", [128, NQ], BF16) for i in range(2)]
        fullT = [ar.alloc(f"fullT{hd}", [128, S], BF16) for hd in range(8)]
        cc_sb = [ar.alloc(f"cc_sb{i}", [128, 512], BF16) for i in range(2)]
        wo_sb += [ar.alloc(f"wo_sb{hd}", [128, DM], BF16) for hd in range(6, 8)]
        sela_sb = ar.alloc("sela_sb", [32, 640], BF16)
        selc_sb = ar.alloc("selc_sb", [20, 640], BF16)
        det_t = [ar.alloc(f"det_t{i}", [128, NQ], BF16) for i in range(2)]
        assert ar.off <= p5top, f"phase5 overflow: {ar.off} > {p5top}"

        @block.gpsimd
        def _(g):
            g.wait_ge(sem("skw_used"), 128)
            g.wait_ge(sem("sc_q"), sc_q_all)
            g.wait_ge(sem("sc_k"), sc_k_all)
            g.wait_ge(sem("sc_v"), sc_v_all)
            for i in range(4):
                g.wait_ge(sem(f"g3_{i}"), g3_tot[i])
            inc(g.dma_start(out=kb_sb[:], in_=kb_n[:]), "ld_g5", 16)
            inc(g.dma_start(out=ka_sb[:], in_=ka_n[:]), "ld_g5", 16)
            for hd in range(6):
                inc(g.dma_start(out=wo_sb[hd][:], in_=wo_c[hd * 128:(hd + 1) * 128, :]), "ld_g5", 16)
            g.wait_ge(sem("pv_mm"), n_pv)
            g.wait_ge(sem("att_ev"), att_all)
            inc(g.dma_start(out=sela_sb[:], in_=sela[:]), "ld_g5", 16)
            inc(g.dma_start(out=selc_sb[:], in_=selc[:]), "ld_g5", 16)
            for hd in range(6, 8):
                inc(g.dma_start(out=wo_sb[hd][:], in_=wo_c[hd * 128:(hd + 1) * 128, :]), "ld_g5", 16)
        ld_g5_all = cnt("ld_g5")

        for j in range(5):
            par5 = j % 2

            @block.tensor
            def _(t):
                if j == 0:
                    t.wait_ge(sem("ld_g5"), ld_g5_all)
                    t.wait_ge(sem("att_ev"), att_all)
                if j >= 2:
                    t.wait_ge(sem("skfm_rdy"), j - 1)
                t.matmul(pbank[6 + par5][:, 0:256],
                         sela_sb[:, j * 128:(j + 1) * 128], kb_sb[:], start=True, stop=True)
                inc(t.matmul(pbank[6 + par5][:, 256:384],
                             selc_sb[:, j * 128:(j + 1) * 128], ka_sb[:], start=True, stop=True),
                    "kexp_mm")

            @block.vector
            def _(v):
                v.wait_ge(sem("kexp_mm"), j + 1)
                v.tensor_copy(kexp[par5][:, 0:256], pbank[6 + par5][:, 0:256])
                fence(v, v.tensor_copy(kexp[par5][:, 256:384],
                                       pbank[6 + par5][:, 256:384]), "fl_v")
                for ts in range(TC):
                    ii = v.tensor_tensor(
                        skfm[j][:, ts * 128:(ts + 1) * 128],
                        kexp[par5][:, (ts % 2) * 128:(ts % 2) * 128 + 128],
                        kexp[par5][:, 256 + ts // 2:256 + ts // 2 + 1].to_broadcast([128, 128]),
                        op=ALU.mult)
                inc(ii, "skfm_rdy")

        @block.gpsimd
        def _(g):
            for i4 in range(4):
                g.wait_ge(sem(f"det_st{i4}"), det_all[i4])
        btk5 = [None]

        @block.gpsimd
        def _(g):
            btk5[0] = g.to_reg(TK - 1)

        for tc in range(TC):
            par5 = tc % 2

            @block.tensor
            def _(t):
                if tc == 0:
                    t.wait_ge(sem("skfm_rdy"), 5)
                if tc >= 2:
                    t.wait_ge(sem("full_ev"), tc - 1)
                for j in range(5):
                    t.matmul(pbank[par5 * 2][:, 0:512], skfm[j][:, tc * 128:(tc + 1) * 128],
                             out_sk[j][:, 0:512], start=(j == 0), stop=(j == 4))
                    mmi = t.matmul(pbank[par5 * 2 + 1][:, 0:512], skfm[j][:, tc * 128:(tc + 1) * 128],
                                   out_sk[j][:, 512:1024], start=(j == 0), stop=(j == 4))
                inc(mmi, "exp_mm")

            @block.vector
            def _(v):
                v.wait_ge(sem("exp_mm"), tc + 1)
                if tc >= 2:
                    v.wait_ge(sem(f"full_st{par5}"), 16 * (tc // 2))
                v.tensor_copy(full_sb[par5][:, 0:512], pbank[par5 * 2][:, 0:512])
                v.tensor_copy(full_sb[par5][:, 512:1024], pbank[par5 * 2 + 1][:, 0:512])
                inc(v.memset(det_t[par5][:], 0.0), "full_ev")

            @block.gpsimd
            def _(g):
                g.wait_ge(sem("full_ev"), tc + 1)
                inc(g.indirect_dma_start(
                    out=det_t[par5][:], out_offset=None,
                    in_=outdet_dram[:],
                    in_offset=IndirectOffsetOnAxis(pos_cm[:, tc:tc + 1], 0),
                    bounds_check=btk5[0], oob_is_err=False), f"patch{par5}", 16)

            @block.vector
            def _(v):
                v.wait_ge(sem(f"patch{par5}"), 16 * (tc // 2 + 1))
                inc(v.copy_predicated(full_sb[par5][:],
                                      selu_cm[:, tc:tc + 1].to_broadcast([128, NQ]),
                                      det_t[par5][:]), "blend_ev")

            @block.sync
            def _(sy):
                sy.wait_ge(sem("blend_ev"), tc + 1)
                inc(sy.dma_start(out=full_dram[tc * 128:(tc + 1) * 128, :], in_=full_sb[par5][:]),
                    f"full_st{par5}", 16)
        full_st_all = (cnt("full_st0"), cnt("full_st1"))

        @block.sync
        def _(sy):
            sy.wait_ge(sem("full_st0"), full_st_all[0])
            sy.wait_ge(sem("full_st1"), full_st_all[1])
            for hd in range(8):
                inc(sy.dma_start_transpose(fullT[hd][:], full_dram[:, hd * 128:(hd + 1) * 128]),
                    "ftr_ld", 16)
        ftr_all = cnt("ftr_ld")

        n_wo = 0
        for oc in range(NF):
            for tcc in range(8):
                @block.tensor
                def _(t):
                    if n_wo == 0:
                        t.wait_ge(sem("ftr_ld"), ftr_all)
                        t.wait_ge(sem("full_ev"), TC)
                    if n_wo >= 2:
                        t.wait_ge(sem("cc_ev"), n_wo - 1)
                    for hd in range(8):
                        mmi = t.matmul(pbank[4 + n_wo % 2][:, 0:512],
                                       wo_sb[hd][:, oc * 128:(oc + 1) * 128],
                                       fullT[hd][:, tcc * 512:(tcc + 1) * 512],
                                       start=(hd == 0), stop=(hd == 7))
                    inc(mmi, "wo_mm")

                @block.vector
                def _(v):
                    v.wait_ge(sem("wo_mm"), n_wo + 1)
                    if n_wo >= 2:
                        v.wait_ge(sem(f"cc_st{n_wo % 2}"), 16 * (n_wo // 2))
                    inc(v.tensor_copy(cc_sb[n_wo % 2][:], pbank[4 + n_wo % 2][:, 0:512]), "cc_ev")

                @block.sync
                def _(sy):
                    sy.wait_ge(sem("cc_ev"), n_wo + 1)
                    inc(sy.dma_start(out=ccin[oc * 128:(oc + 1) * 128, tcc * 512:(tcc + 1) * 512],
                                     in_=cc_sb[n_wo % 2][:]), f"cc_st{n_wo % 2}", 16)
                n_wo += 1
        cc_st_all = (cnt("cc_st0"), cnt("cc_st1"))

        if dbg and phases < 6:
            @block.sync
            def _(sy):
                sy.wait_ge(sem("cc_st0"), cc_st_all[0])
                sy.wait_ge(sem("cc_st1"), cc_st_all[1])
                if "pos" in dbg:
                    inc(sy.dma_start(out=dout("pos", [128, TC], U32), in_=pos_cm[:]), "dbg_st", 16)
                if "sel" in dbg:
                    inc(sy.dma_start(out=dout("sel", [128, TC]), in_=sel_cm[:]), "dbg_st", 16)
                if "full" in dbg:
                    inc(sy.dma_start(out=dout("full", [S, NQ], BF16), in_=full_dram[:]), "dbg_st", 16)
                if "ccin" in dbg:
                    inc(sy.dma_start(out=dout("ccin", [DM, S], BF16), in_=ccin[:]), "dbg_st", 16)
                sy.wait_ge(sem("dbg_st"), cnt("dbg_st"))

        if phases < 6:
            lower_extended_insts(nc)
            return nc, dbg_outs

        @block.gpsimd
        def _(g):
            g.wait_ge(sem("cc_st0"), cc_st_all[0])
            g.wait_ge(sem("cc_st1"), cc_st_all[1])
            inc(g.collective_compute(
                "AllReduce", ALU.add,
                replica_groups=[[0, 1], [2, 3], [4, 5], [6, 7]],
                ins=[ccin.opt()], outs=[ccout.opt()]), "cc_done")
        cc_done_n = cnt("cc_done")

        ar.reset_to("persist")
        ar.top = nc.sbuf_top
        wbT_sb = [ar.alloc(f"wbT_sb{f}", [128, DM], BF16) for f in range(NF)]
        x1full = [ar.alloc(f"x1f{f}", [128, HTOK], BF16) for f in range(NF)]
        aT_sb = [ar.alloc(f"aT_sb{f}", [128, 128], BF16) for f in range(NF)]
        rwT_sb = [ar.alloc(f"rwT_sb{f}", [128, 8], BF16) for f in range(NF)]
        bw_sb = ar.alloc("bw_sb", [128, DM], BF16)
        bb_sb = ar.alloc("bb_sb", [128, NF], F32)
        sq2_off = ar.off
        sq2 = [ar.alloc(f"sq2{i}", [128, HTOK], BF16) for i in range(2)]
        x1r = [ar.alloc(f"x1r{f}", [128, 512], BF16) for f in range(NF)]
        r2bc = ar.alloc("r2bc", [128, 512], BF16)
        vm2 = ar.alloc("vm2", [1, HTOK], F32)
        vv2 = ar.alloc("vv2", [1, HTOK], F32)
        vt2 = nc.alloc_sbuf_tensor_at("vt2", [1, HTOK], F32, offset=(sq2_off + 63) // 64 * 64).ap()
        r2b = ar.alloc("r2b", [1, HTOK], BF16)
        nmr2b = ar.alloc("nmr2b", [1, HTOK], BF16)
        wbar2 = ar.alloc("wbar2", [1, DM + 136], BF16)
        rl_tm = ar.alloc("rl_tm", [128, 8], F32)
        rl_s1 = ar.alloc("rl_s1", [128, 8], F32)
        rl_s2 = ar.alloc("rl_s2", [128, 8], F32)
        z8 = ar.alloc("z8", [128, 8], F32)
        sel8f = ar.alloc("sel8f", [128, 8], F32)
        sel8fm = ar.alloc("sel8fm", [8, 512], BF16)
        esel_sb = ar.alloc("esel_sb", [8, 128], BF16)
        h_sb = ar.alloc("h_sb", [128, 512], BF16)
        hm_sb = ar.alloc("hm_sb", [128, 512], BF16)
        out_f = [ar.alloc(f"out_f{i}", [128, 512], F32) for i in range(2)]
        assert ar.off <= ar.top, f"phase6 overflow: {ar.off}"

        @block.gpsimd
        def _(g):
            g.wait_ge(sem("cc_ev"), n_wo)
            for f in range(NF):
                inc(g.dma_start(out=wbT_sb[f][:], in_=wbT[f * 128:(f + 1) * 128, :]), "ld_g6", 16)
                inc(g.dma_start(out=aT_sb[f][:], in_=aT[f * 128:(f + 1) * 128, :]), "ld_g6", 16)
                inc(g.dma_start(out=rwT_sb[f][:], in_=rwT[f * 128:(f + 1) * 128, :]), "ld_g6", 16)
            inc(g.dma_start(out=bw_sb[:], in_=bw_er[:]), "ld_g6", 16)
            inc(g.dma_start(out=esel_sb[:], in_=esel[:]), "ld_g6", 16)
        ld_g6_all = cnt("ld_g6")

        @block.sync
        def _(sy):
            sy.wait_ge(sem("cc_ev"), n_wo)
            inc(sy.dma_start(out=bb_sb[:], in_=b_base_cm[:]), "ld_s6", 16)
        ld_s6_all = cnt("ld_s6")

        @block.vector
        def _(v):
            v.wait_ge(sem("ld_g6"), ld_g6_all)
            v.memset(z8[:], 0.0)
            inc(v.tensor_scalar(bw_sb[:], bw_sb[:], 2.0, None, op0=ALU.mult), "bw_sc")

        @block.sync
        def _(sy):
            sy.wait_ge(sem("cc_done"), cc_done_n)
            pid = nc.partition_id(engines=[mybir.EngineType.SP])
            half_off = pid % 2 * HTOK
            for f in range(NF):
                inc(sy.dma_start(out=x1full[f][:],
                                 in_=ccout[f * 128:(f + 1) * 128, bass.ds(half_off, HTOK)]),
                    "ld_cc", 16)
        ld_cc_all = cnt("ld_cc")

        @block.gpsimd
        def _(g):
            g.wait_ge(sem("cc_done"), cc_done_n)
            for f in range(NF):
                i = f % 2
                if f >= 2:
                    g.wait_ge(sem("x1_used"), f - 1)
                inc(g.dma_start(out=sq2[i][:], in_=xTh[f * 128:(f + 1) * 128, :]), f"ld_xh{i}", 16)

        @block.vector
        def _(v):
            v.wait_ge(sem("ld_cc"), ld_cc_all)
            for f in range(NF):
                i = f % 2
                v.wait_ge(sem(f"ld_xh{i}"), 16 * (f // 2 + 1))
                inc(v.tensor_add(x1full[f][:], x1full[f][:], sq2[i][:]), "x1_rdy")

        @block.scalar
        def _(s):
            for f in range(NF):
                i = f % 2
                s.wait_ge(sem("x1_rdy"), f + 1)
                inc(s.activation(sq2[i][:], x1full[f][:], AF.Square), "sq6")

        psum_sum2 = [pbank[i][0:1, :] for i in range(4)]
        psum_sq2 = [pbank[4 + i][0:1, :] for i in range(4)]

        @block.tensor
        def _(t):
            t.wait_ge(sem("cc_ev"), n_wo)
            for f in range(NF):
                i = f % 2
                t.wait_ge(sem("sq6"), f + 1)
                fi, la = f == 0, f == NF - 1
                for w in range(4):
                    t.matmul(psum_sum2[w], ones_col_b[:], x1full[f][:, w * 512:(w + 1) * 512],
                             start=fi, stop=la)
                    mmi = t.matmul(psum_sq2[w], ones_col_b[:], sq2[i][:, w * 512:(w + 1) * 512],
                                   start=fi, stop=la)
                inc(mmi, "x1_used")

        @block.vector
        def _(v):
            v.wait_ge(sem("x1_used"), NF)
            for w in range(4):
                v.tensor_scalar(vm2[0:1, w * 512:(w + 1) * 512], psum_sum2[w], 1.0 / DM,
                                None, op0=ALU.mult)
                ii = v.tensor_scalar(vv2[0:1, w * 512:(w + 1) * 512], psum_sq2[w], 1.0 / DM,
                                     None, op0=ALU.mult)
            fence(v, ii, "fl_v")
            fence(v, v.tensor_mul(vt2[:], vm2[:], vm2[:]), "fl_v")
            inc(v.tensor_sub(vv2[:], vv2[:], vt2[:]), "ln2_var")
            v.wait_ge(sem("rstd2"), 1)
            inc(v.scalar_tensor_tensor(nmr2b[:], vm2[:], -1.0, vv2[:],
                                       op0=ALU.mult, op1=ALU.mult), "ln2_rdy")
            fence(v, v.tensor_copy(r2b[:], vv2[:]), "fl_v")
        ln2_fl = cnt("fl_v")

        @block.scalar
        def _(s):
            s.wait_ge(sem("ln2_var"), 1)
            fence(s, s.activation(vv2[:], vv2[:], AF.Ln, bias=eps1_row[:]), "fl_s")
            inc(s.activation(vv2[:], vv2[:], AF.Exp, scale=-0.5), "rstd2")

        @block.tensor
        def _(t):
            t.wait_ge(sem("ln2_var"), 1)
            for f in range(NF):
                fi, la = f == 0, f == NF - 1
                for w in range(4):
                    t.matmul(pbank[w][0:1, 0:512], ones_col_b[:], wbT_sb[f][:, w * 512:(w + 1) * 512],
                             start=fi, stop=la)
                t.matmul(pbank[4][0:1, 0:128], ones_col_b[:], aT_sb[f][:], start=fi, stop=la)
                mmi = t.matmul(pbank[5][0:1, 0:8], ones_col_b[:], rwT_sb[f][:], start=fi, stop=la)
            inc(mmi, "wbar2_mm")

        @block.vector
        def _(v):
            v.wait_ge(sem("wbar2_mm"), 1)
            for w in range(4):
                v.tensor_copy(wbar2[0:1, w * 512:(w + 1) * 512], pbank[w][0:1, 0:512])
            v.tensor_copy(wbar2[0:1, DM:DM + 128], pbank[4][0:1, 0:128])
            inc(v.tensor_copy(wbar2[0:1, DM + 128:DM + 136], pbank[5][0:1, 0:8]), "wbar2_ev")

        n_fo = 0
        n_rt = 0
        for tw in range(4):
            t0 = tw * 512

            @block.tensor
            def _(t):
                if tw == 0:
                    t.wait_ge(sem("ln2_rdy"), 1)
                    t.wait_ge(sem("wbar2_ev"), 1)
                    t.wait_ge(sem("fl_v"), ln2_fl)
                else:
                    t.wait_ge(sem("r2cp"), tw)
                inc(t.matmul(pbank[5][:, 0:512], ones_row_b[:], r2b[0:1, t0:t0 + 512],
                             start=True, stop=True), "r2mm")

            @block.vector
            def _(v):
                v.wait_ge(sem("r2mm"), tw + 1)
                inc(v.tensor_copy(r2bc[:], pbank[5][:, 0:512]), "r2cp")
                fence(v, v.memset(rl_s2[0:1, 0:1], 0.0), "fl_v")

            @block.vector
            def _(v):
                if tw >= 1:
                    v.wait_ge(sem("out_st0"), 16 * (n_fo // 2))
                    v.wait_ge(sem("out_st1"), 16 * (n_fo // 2))
                for f in range(NF):
                    ii = v.tensor_mul(x1r[f][:], x1full[f][:, t0:t0 + 512], r2bc[:])
                inc(ii, "x1r_rdy")
            x1r_n = cnt("x1r_rdy")

            for tt in range(4):
                @block.tensor
                def _(t):
                    t.wait_ge(sem("x1r_rdy"), x1r_n)
                    if n_rt >= 1:
                        t.wait_ge(sem("rt_ev"), n_rt)
                    for f in range(NF):
                        t.matmul(pbank[6][:, 0:8], x1r[f][:, tt * 128:(tt + 1) * 128],
                                 rwT_sb[f][:], start=(f == 0), stop=False)
                    inc(t.matmul(pbank[6][:, 0:8], nmr2b[0:1, t0 + tt * 128:t0 + (tt + 1) * 128],
                                 wbar2[0:1, DM + 128:DM + 136], start=False, stop=True), "rt_mm")

                @block.vector
                def _(v):
                    v.wait_ge(sem("rt_mm"), n_rt + 1)
                    fence(v, v.tensor_copy(rl_tm[:], pbank[6][:, 0:8]), "fl_v")
                    fence(v, v.reduce_max(rl_s1[:, 0:1], rl_tm[:], axis=AX), "fl_v")
                    fence(v, v.tensor_scalar(rl_s2[:], rl_tm[:], rl_s1[:, 0:1], None,
                                             op0=ALU.is_ge), "fl_v")
                    fence(v, v.tensor_tensor_scan(rl_s1[:], rl_s2[:], z8[:], 0.0,
                                                  op0=ALU.add, op1=ALU.add), "fl_v")
                    fence(v, v.tensor_scalar(rl_s1[:], rl_s1[:], 1.0, None, op0=ALU.is_le), "fl_v")
                    inc(v.tensor_mul(sel8f[:], rl_s2[:], rl_s1[:]), "rt_sel")

                @block.tensor
                def _(t):
                    t.wait_ge(sem("rt_sel"), n_rt + 1)
                    inc(t.transpose(pbank[7][0:8, 0:128], sel8f[:], ident_f[:]), "rt_tr")

                @block.vector
                def _(v):
                    v.wait_ge(sem("rt_tr"), n_rt + 1)
                    inc(v.tensor_copy(sel8fm[0:8, tt * 128:(tt + 1) * 128],
                                      pbank[7][0:8, 0:128]), "rt_ev")
                n_rt += 1

            @block.tensor
            def _(t):
                t.wait_ge(sem("rt_ev"), n_rt)
                for f in range(NF):
                    t.matmul(pbank[6][:, 0:512], aT_sb[f][:], x1r[f][:], start=(f == 0), stop=False)
                inc(t.matmul(pbank[6][:, 0:512], wbar2[0:1, DM:DM + 128],
                             nmr2b[0:1, t0:t0 + 512], start=False, stop=True), "h_mm")
                inc(t.matmul(pbank[7][:, 0:512], esel_sb[:], sel8fm[:], start=True, stop=True),
                    "mask_mm")

            @block.vector
            def _(v):
                v.wait_ge(sem("h_mm"), tw + 1)
                fence(v, v.tensor_copy(h_sb[:], pbank[6][:, 0:512]), "fl_v")
                v.wait_ge(sem("mask_mm"), tw + 1)
                inc(v.tensor_mul(hm_sb[:], h_sb[:], pbank[7][:, 0:512]), "hm_rdy")

            for oc in range(NF):
                @block.tensor
                def _(t):
                    t.wait_ge(sem("hm_rdy"), tw + 1)
                    if tw == 0 and oc == 0:
                        t.wait_ge(sem("bw_sc"), 1)
                    if n_fo >= 2:
                        t.wait_ge(sem("fo_ev"), n_fo - 1)
                    for f in range(NF):
                        t.matmul(pbank[n_fo % 2][:, 0:512], wbT_sb[f][:, oc * 128:(oc + 1) * 128],
                                 x1r[f][:], start=(f == 0), stop=False)
                    t.matmul(pbank[n_fo % 2][:, 0:512], wbar2[0:1, oc * 128:(oc + 1) * 128],
                             nmr2b[0:1, t0:t0 + 512], start=False, stop=False)
                    inc(t.matmul(pbank[n_fo % 2][:, 0:512], bw_sb[:, oc * 128:(oc + 1) * 128],
                                 hm_sb[:], start=False, stop=True), "fo_mm")

                @block.vector
                def _(v):
                    v.wait_ge(sem("fo_mm"), n_fo + 1)
                    if n_fo == 0:
                        v.wait_ge(sem("ld_s6"), ld_s6_all)
                    if n_fo >= 2:
                        v.wait_ge(sem(f"out_st{n_fo % 2}"), 16 * (n_fo // 2))
                    inc(v.scalar_tensor_tensor(out_f[n_fo % 2][:], pbank[n_fo % 2][:, 0:512],
                                               bb_sb[:, oc:oc + 1], x1full[oc][:, t0:t0 + 512],
                                               op0=ALU.add, op1=ALU.add), "fo_ev")

                @block.sync
                def _(sy):
                    sy.wait_ge(sem("fo_ev"), n_fo + 1)
                    inc(sy.dma_start(out=out_fm[oc * 128:(oc + 1) * 128, t0:t0 + 512],
                                     in_=out_f[n_fo % 2][:]), f"out_st{n_fo % 2}", 16)
                n_fo += 1

        @block.sync
        def _(sy):
            sy.wait_ge(sem("out_st0"), cnt("out_st0"))
            sy.wait_ge(sem("out_st1"), cnt("out_st1"))

    lower_extended_insts(nc)
    return nc, dbg_outs


def make_in_maps(inputs):
    x = np.ascontiguousarray(inputs["L0_attn_hidden_states"], dtype=np.float32)
    cos = np.ascontiguousarray(np.asarray(inputs["L0_attn_cos"], dtype=np.float32)[0, :S, 0, :])
    sin = np.ascontiguousarray(np.asarray(inputs["L0_attn_sin"], dtype=np.float32)[0, :S, 0, :])
    wq = np.asarray(inputs["L0_attn_wq"], dtype=np.float32)
    wk = np.asarray(inputs["L0_attn_wk"], dtype=np.float32)
    wv = np.asarray(inputs["L0_attn_wv"], dtype=np.float32)
    wo = np.asarray(inputs["L0_attn_wo"], dtype=np.float32)
    qnw = np.asarray(inputs["L0_attn_q_norm_w"], dtype=np.float32).reshape(-1)
    knw = np.asarray(inputs["L0_attn_k_norm_w"], dtype=np.float32).reshape(-1)
    w1 = np.asarray(inputs["L0_attn_imp_w1"], dtype=np.float32)
    b1 = np.asarray(inputs["L0_attn_imp_b1"], dtype=np.float32).reshape(1, 64)
    w2 = np.asarray(inputs["L0_attn_imp_w2"], dtype=np.float32).reshape(1, 64)
    b2 = np.asarray(inputs["L0_attn_imp_b2"], dtype=np.float32).reshape(1, 1)
    ka = np.asarray(inputs["L0_attn_kron_a"], dtype=np.float32)
    kb = np.asarray(inputs["L0_attn_kron_b"], dtype=np.float32)
    ss = np.asarray(inputs["L0_attn_sketch_scale"], dtype=np.float32).reshape(1, 1)
    Wb = np.asarray(inputs["L0_ffn_W_base"], dtype=np.float32)
    bb = np.asarray(inputs["L0_ffn_b_base"], dtype=np.float32).reshape(-1)
    rw = np.asarray(inputs["L0_ffn_router_W"], dtype=np.float32)
    A = np.asarray(inputs["L0_ffn_A"], dtype=np.float32)
    Bw = np.asarray(inputs["L0_ffn_Bw"], dtype=np.float32)

    ident = np.eye(128, dtype=np.float32)
    iota = np.arange(S, dtype=np.uint32).reshape(TC, 128).T.copy()
    sela_m = np.zeros((32, 5 * 128), np.float32)
    selc_m = np.zeros((20, 5 * 128), np.float32)
    for j in range(5):
        for p in range(128):
            sv = 128 * j + p
            sela_m[sv // 20, j * 128 + p] = 1.0
            selc_m[sv % 20, j * 128 + p] = 1.0
    kaT_row = np.zeros((1, 320), np.float32)
    for jp in range(16):
        kaT_row[0, jp * 20:(jp + 1) * 20] = ka[:, jp]
    es = np.zeros((8, 128), np.float32)
    for e in range(8):
        es[e, e * 16:(e + 1) * 16] = 1.0

    maps = []
    for core in range(8):
        b, hf = core // 2, core % 2
        m = {
            "xT": np.ascontiguousarray(x[b].T),
            "xTh": np.ascontiguousarray(x[b].T[:, hf * HTOK:(hf + 1) * HTOK]),
            "cos_tm": cos, "sin_tm": sin,
            "wq_c": np.ascontiguousarray(wq[:, hf * NQ:(hf + 1) * NQ]),
            "wk_c": np.ascontiguousarray(wk[:, hf * NKVD:(hf + 1) * NKVD]),
            "wv_c": np.ascontiguousarray(wv[:, hf * NKVD:(hf + 1) * NKVD]),
            "imp_w1": w1, "imp_w2r": w2, "imp_b1r": b1, "imp_b2": b2,
            "qnw_col": qnw.reshape(128, 1), "knw_col": knw.reshape(128, 1),
            "qnw_row": qnw.reshape(1, 128), "knw_row": knw.reshape(1, 128),
            "kbT": np.ascontiguousarray(kb.T), "kb_n": kb, "ka_n": ka,
            "kaT_row": kaT_row, "sela": sela_m, "selc": selc_m,
            "wo_c": np.ascontiguousarray(wo[hf * NQ:(hf + 1) * NQ, :]),
            "wbT": np.ascontiguousarray(Wb.T),
            "b_base_cm": np.ascontiguousarray(bb.reshape(NF, 128).T),
            "rwT": np.ascontiguousarray(rw.T),
            "aT": np.ascontiguousarray(A.reshape(128, DM).T),
            "bw_er": np.ascontiguousarray(Bw.transpose(0, 2, 1).reshape(128, DM)),
            "esel": es, "ident": ident, "iota_cm": iota, "ss_in": ss,
        }
        maps.append({k: np.ascontiguousarray(v) for k, v in m.items()})
    return maps


_CACHE = {}


def kernel(**inputs):
    if "nc" not in _CACHE:
        _CACHE["nc"] = build_program(phases=7)[0]
    nc = _CACHE["nc"]
    maps = make_in_maps(inputs)
    res = run_bass_kernel_spmd(nc, maps, core_ids=list(range(8)))
    out = np.empty((B, S, DM), np.float32)
    for core in range(8):
        b, hf = core // 2, core % 2
        out[b, hf * HTOK:(hf + 1) * HTOK, :] = np.asarray(res.results[core]["out_fm"]).T
    return out


# version 07
# speedup vs baseline: 1.0448x; 1.0267x over previous
import numpy as np
import concourse.bass as bass
import concourse.mybir as mybir
from concourse.bass_utils import run_bass_kernel_spmd
from concourse.bass_core_rust import IndirectOffsetOnAxis
from concourse.library_overlay import lower_extended_insts

F32 = mybir.dt.float32
BF16 = mybir.dt.bfloat16
U32 = mybir.dt.uint32
AF = mybir.ActivationFunctionType
ALU = mybir.AluOpType
AX = mybir.AxisListType.X

B, S, DM = 4, 4096, 2048
H, KVH, D = 16, 4, 128
SKN, TK = 640, 2048
NH, NKV = 8, 2
TC = S // 128
NF = DM // 128
NQ = NH * D
NKVD = NKV * D
CT = TK + SKN
NCT = CT // 128
NTK = TK // 128
HTOK = S // 2
HTC = HTOK // 128
LOGSHIFT = float(np.log(S / SKN))
ISQD = float(1.0 / np.sqrt(D))
EPS1, EPS2 = 1e-5, 1e-6
BISECT = 34
BIGPOS = 60000.0

_SZ = {F32: 4, BF16: 2, U32: 4}


class Arena:
    def __init__(self, nc):
        self.nc = nc
        self.off = (nc.sbuf_base + 63) // 64 * 64
        self.top = nc.sbuf_top
        self.marks = {}

    def alloc(self, name, shape, dtype, align=64):
        nbytes = int(np.prod(shape[1:])) * _SZ[dtype]
        self.off = (self.off + align - 1) // align * align
        t = self.nc.alloc_sbuf_tensor_at(name, list(shape), dtype, offset=self.off)
        self.off += nbytes
        assert self.off <= self.top, f"SBUF overflow at {name}: {self.off}"
        return t.ap()

    def mark(self, name):
        self.marks[name] = self.off

    def reset_to(self, name):
        self.off = self.marks[name]


def build_program(dbg=(), phases=99):
    dbg = set(dbg)
    nc = bass.Bass("TRN2", target_bir_lowering=False)

    def din(name, shape, dtype=F32):
        return nc.dram_tensor(name, list(shape), dtype, kind="ExternalInput").ap()

    xT = din("xT", [DM, S])
    xTh = din("xTh", [DM, HTOK])
    cos_tm = din("cos_tm", [S, D])
    sin_tm = din("sin_tm", [S, D])
    wq_c = din("wq_c", [DM, NQ])
    wk_c = din("wk_c", [DM, NKVD])
    wv_c = din("wv_c", [DM, NKVD])
    imp_w1 = din("imp_w1", [DM, 64])
    imp_w2r = din("imp_w2r", [1, 64])
    imp_b1r = din("imp_b1r", [1, 64])
    imp_b2 = din("imp_b2", [1, 1])
    qnw_col = din("qnw_col", [128, 1])
    knw_col = din("knw_col", [128, 1])
    qnw_row = din("qnw_row", [1, 128])
    knw_row = din("knw_row", [1, 128])
    kbT = din("kbT", [256, 32])
    kb_n = din("kb_n", [32, 256])
    ka_n = din("ka_n", [20, 128])
    kaT_row = din("kaT_row", [1, 320])
    sela = din("sela", [32, 5 * 128])
    selc = din("selc", [20, 5 * 128])
    wo_c = din("wo_c", [NQ, DM])
    wbT = din("wbT", [DM, DM])
    b_base_cm = din("b_base_cm", [128, NF])
    rwT = din("rwT", [DM, 8])
    aT = din("aT", [DM, 128])
    bw_er = din("bw_er", [128, DM])
    esel = din("esel", [8, 128])
    ident = din("ident", [128, 128])
    iota_cm = din("iota_cm", [128, TC], U32)
    ss_in = din("ss_in", [1, 1])

    out_fm = nc.dram_tensor("out_fm", [DM, HTOK], F32, kind="ExternalOutput").ap()
    qd_dram = nc.dram_tensor("qd_dram", [TK, NQ], BF16).ap()
    kd_dram = nc.dram_tensor("kd_dram", [TK, NKVD], BF16).ap()
    vd_dram = nc.dram_tensor("vd_dram", [TK, NKVD], BF16).ap()
    cosd_dram = nc.dram_tensor("cosd_dram", [TK, D], BF16).ap()
    sind_dram = nc.dram_tensor("sind_dram", [TK, D], BF16).ap()
    outdet_dram = nc.dram_tensor("outdet_dram", [TK, NQ], BF16).ap()
    full_dram = nc.dram_tensor("full_dram", [S, NQ], BF16).ap()
    ccin = nc.dram_tensor("ccin", [DM, S], BF16).ap()
    ccout = nc.dram_tensor("ccout", [DM, S], BF16).ap()

    dbg_outs = {}

    def dout(name, shape, dtype=F32):
        ap = nc.dram_tensor("dbg_" + name, list(shape), dtype, kind="ExternalOutput").ap()
        dbg_outs[name] = "dbg_" + name
        return ap

    ar = Arena(nc)
    ident_f = ar.alloc("ident_f", [128, 128], F32)
    ident_b = ar.alloc("ident_b", [128, 128], BF16)
    ones_col_b = ar.alloc("ones_col_b", [128, 1], BF16)
    ones_col_f = ar.alloc("ones_col_f", [128, 1], F32)
    ones_row_b = ar.alloc("ones_row_b", [1, 128], BF16)
    ones_row_f = ar.alloc("ones_row_f", [1, 128], F32)
    small_f = ar.alloc("small_f", [1, 128], F32)
    w2bc = ar.alloc("w2bc", [128, 64], F32)
    b2bias = ar.alloc("b2bias", [128, 1], F32)
    b1row_b = ar.alloc("b1row_b", [1, 64], BF16)
    qw_col = ar.alloc("qw_col", [128, 1], F32)
    kw_col = ar.alloc("kw_col", [128, 1], F32)
    qw_rows = ar.alloc("qw_rows", [1, 256], F32)
    kw_rows = ar.alloc("kw_rows", [1, 256], F32)
    ssbc = ar.alloc("ssbc", [128, 1], F32)
    eps2_col = ar.alloc("eps2_col", [128, 1], F32)
    eps1_row = ar.alloc("eps1_row", [1, 1], F32)
    qwss = ar.alloc("qwss", [128, 1], F32)
    kwss = ar.alloc("kwss", [128, 1], F32)
    kabc = ar.alloc("kabc", [128, 320], BF16)
    kbT_sb = ar.alloc("kbT_sb", [128, 64], BF16)
    logits_cm = ar.alloc("logits_cm", [128, TC], F32)
    weights_cm = ar.alloc("weights_cm", [128, TC], F32)
    restw_cm = ar.alloc("restw_cm", [128, TC], F32)
    sel_cm = ar.alloc("sel_cm", [128, TC], F32)
    pos_cm = ar.alloc("pos_cm", [128, TC], U32)
    selu_cm = ar.alloc("selu_cm", [128, TC], U32)
    ar.mark("persist")

    r1b = ar.alloc("r1b", [1, S], BF16)
    nmr1b = ar.alloc("nmr1b", [1, S], BF16)
    wbar = ar.alloc("wbar", [1, 1600], BF16)
    wrhs = [ar.alloc(f"wrhs{f}", [128, 1600], BF16) for f in range(NF)]
    rmsq = [ar.alloc(f"rmsq{i}", [128, 8], F32) for i in range(2)]
    rmsk = [ar.alloc(f"rmsk{i}", [128, 8], F32) for i in range(2)]
    tanh_sb = [ar.alloc(f"tanh{i}", [128, 64], F32) for i in range(2)]
    scr_act = ar.alloc("scr_act", [128, 20 * 128], BF16)
    scr_ttr = ar.alloc("scr_ttr", [128, 2 * 64], F32)
    ar.mark("p1ab")
    vecmu = ar.alloc("vecmu", [1, 1024], F32)
    vec1 = ar.alloc("vec1", [1, 1024], F32)
    vec2 = ar.alloc("vec2", [1, 1024], F32)
    xstat = [ar.alloc(f"xstat{i}", [128, 1024], BF16) for i in range(2)]
    sq_b = [ar.alloc(f"sq_b{i}", [128, 1024], BF16) for i in range(2)]
    ar.reset_to("p1ab")
    xw = [ar.alloc(f"xw{f}", [128, 512], BF16) for f in range(NF)]
    rbc_b = ar.alloc("rbc_b", [128, 512], BF16)
    ar.mark("p1end")
    qhat = [ar.alloc(f"qhat{t}", [128, NQ], BF16) for t in range(TC)]
    khat = [ar.alloc(f"khat{t}", [128, NKVD], BF16) for t in range(TC)]
    vhat = [ar.alloc(f"vhat{t}", [128, NKVD], BF16) for t in range(TC)]
    ar.mark("qkv_end")

    pbank = [nc.alloc_psum_tensor(f"pb{i}", [128, 512], F32).ap() for i in range(8)]

    sem_cnt, sem_h = {}, {}

    def sem(name):
        if name not in sem_h:
            sem_h[name] = nc.alloc_semaphore(name)
            sem_cnt[name] = 0
        return sem_h[name]

    def inc(instr, name, k=1):
        instr.then_inc(sem(name), k)
        sem_cnt[name] += k
        return instr

    def cnt(name):
        sem(name)
        return sem_cnt[name]

    def wchain(e, name):
        e.wait_ge(sem(name), cnt(name))



    def fence(eng, instr, tag):
        inc(instr, tag)
        eng.wait_ge(sem(tag), cnt(tag))

    with nc.Block() as block:
        @block.gpsimd
        def _(g):
            inc(g.dma_start(out=ident_b[:], in_=ident[:]), "ld_g", 16)
            inc(g.dma_start(out=kbT_sb[:, 0:32], in_=kbT[0:128, :]), "ld_g", 16)
            inc(g.dma_start(out=kbT_sb[:, 32:64], in_=kbT[128:256, :]), "ld_g", 16)
            inc(g.dma_start(out=b1row_b[:], in_=imp_b1r[:]), "ld_g", 16)
            for f in range(NF):
                inc(g.dma_start(out=wrhs[f][:, 0:1024], in_=wq_c[f * 128:(f + 1) * 128, :]), "ld_g", 16)
                inc(g.dma_start(out=wrhs[f][:, 1024:1280], in_=wk_c[f * 128:(f + 1) * 128, :]), "ld_g", 16)
                inc(g.dma_start(out=wrhs[f][:, 1280:1536], in_=wv_c[f * 128:(f + 1) * 128, :]), "ld_g", 16)
                inc(g.dma_start(out=wrhs[f][:, 1536:1600], in_=imp_w1[f * 128:(f + 1) * 128, :]), "ld_g", 16)
        ld_g_p0 = cnt("ld_g")

        @block.sync
        def _(sy):
            inc(sy.dma_start(out=ident_f[:], in_=ident[:]), "ld_s", 16)
            inc(sy.dma_start(out=qw_col[:], in_=qnw_col[:]), "ld_s", 16)
            inc(sy.dma_start(out=kw_col[:], in_=knw_col[:]), "ld_s", 16)
            inc(sy.dma_start(out=qw_rows[0:1, 0:128], in_=qnw_row[:]), "ld_s", 16)
            inc(sy.dma_start(out=kw_rows[0:1, 0:128], in_=knw_row[:]), "ld_s", 16)
            inc(sy.dma_start(out=small_f[0:1, 0:64], in_=imp_w2r[:]), "ld_s", 16)
            inc(sy.dma_start(out=small_f[0:1, 64:65], in_=imp_b2[:]), "ld_s", 16)
            inc(sy.dma_start(out=small_f[0:1, 65:66], in_=ss_in[:]), "ld_s", 16)
        ld_s_p0 = cnt("ld_s")

        @block.vector
        def _(v):
            v.memset(eps2_col[:], EPS2)
            v.memset(eps1_row[:], EPS1)
            v.memset(ones_col_b[:], 1.0)
            v.memset(ones_col_f[:], 1.0)
            v.memset(ones_row_b[:], 1.0)
            v.memset(ones_row_f[:], 1.0)
            v.wait_ge(sem("ld_s"), ld_s_p0)
            v.tensor_copy(qw_rows[0:1, 128:192], qw_rows[0:1, 64:128])
            v.tensor_copy(qw_rows[0:1, 192:256], qw_rows[0:1, 0:64])
            v.tensor_copy(kw_rows[0:1, 128:192], kw_rows[0:1, 64:128])
            inc(v.tensor_copy(kw_rows[0:1, 192:256], kw_rows[0:1, 0:64]), "c0v")
        c0v1 = cnt("c0v")

        @block.tensor
        def _(t):
            t.wait_ge(sem("c0v"), c0v1)
            t.matmul(pbank[5][:, 0:64], ones_row_f[:], small_f[0:1, 0:64], start=True, stop=True)
            inc(t.matmul(pbank[5][:, 64:66], ones_row_f[:], small_f[0:1, 64:66], start=True, stop=True),
                "c0t")
        c0t1 = cnt("c0t")

        @block.vector
        def _(v):
            v.wait_ge(sem("c0t"), c0t1)
            v.tensor_copy(w2bc[:], pbank[5][:, 0:64])
            v.tensor_scalar(b2bias[:], pbank[5][:, 64:65], -LOGSHIFT, None, op0=ALU.add)
            fence(v, v.tensor_copy(ssbc[:], pbank[5][:, 65:66]), "fl_v")
            v.tensor_mul(qwss[:], qw_col[:], ssbc[:])
            inc(v.tensor_mul(kwss[:], kw_col[:], ssbc[:]), "c0v")
        c0v2 = cnt("c0v")

        psum_sum = [pbank[i][0:1, :] for i in range(2)]
        psum_sq = [pbank[2 + i][0:1, :] for i in range(2)]

        @block.gpsimd
        def _(g):
            for k in range(4 * NF):
                qtr, f = divmod(k, NF)
                i = k % 2
                if k >= 2:
                    g.wait_ge(sem("stat_used"), k - 1)
                inc(g.dma_start(out=xstat[i][:],
                                in_=xT[f * 128:(f + 1) * 128, qtr * 1024:(qtr + 1) * 1024]),
                    f"ld_x{i}", 16)

        @block.scalar
        def _(s):
            for k in range(4 * NF):
                i = k % 2
                s.wait_ge(sem(f"ld_x{i}"), (k // 2 + 1) * 16)
                if k >= 2:
                    s.wait_ge(sem("stat_used"), k - 1)
                inc(s.activation(sq_b[i][:], xstat[i][:], AF.Square), "sq1")
                if (k + 1) % NF == 0:
                    qtr = (k + 1) // NF - 1
                    s.wait_ge(sem("stats_evac"), qtr + 1)
                    fence(s, s.activation(vec2[:], vec2[:], AF.Ln, bias=eps2_col[0:1, 0:1] if False else eps1_row[:]), "fl_s")
                    inc(s.activation(vec2[:], vec2[:], AF.Exp, scale=-0.5), "rstd1")

        @block.tensor
        def _(t):
            t.wait_ge(sem("c0v"), c0v2)
            for k in range(4 * NF):
                qtr, f = divmod(k, NF)
                i = k % 2
                t.wait_ge(sem("sq1"), k + 1)
                t.wait_ge(sem(f"ld_x{i}"), (k // 2 + 1) * 16)
                if f == 0 and qtr >= 1:
                    t.wait_ge(sem("stats_evac"), qtr)
                fi, la = f == 0, f == NF - 1
                for w in range(2):
                    t.matmul(psum_sum[w], ones_col_b[:], xstat[i][:, w * 512:(w + 1) * 512],
                             start=fi, stop=la)
                    mmi = t.matmul(psum_sq[w], ones_col_b[:], sq_b[i][:, w * 512:(w + 1) * 512],
                                   start=fi, stop=la)
                inc(mmi, "stat_used")

        @block.vector
        def _(v):
            for qtr in range(4):
                v.wait_ge(sem("stat_used"), NF * (qtr + 1))
                sl = slice(qtr * 1024, (qtr + 1) * 1024)
                for w in range(2):
                    v.tensor_scalar(vecmu[0:1, w * 512:(w + 1) * 512],
                                    psum_sum[w], 1.0 / DM, None, op0=ALU.mult)
                    ii = v.tensor_scalar(vec1[0:1, w * 512:(w + 1) * 512],
                                         psum_sq[w], 1.0 / DM, None, op0=ALU.mult)
                fence(v, ii, "fl_v")
                fence(v, v.tensor_mul(vec2[:], vecmu[:], vecmu[:]), "fl_v")
                inc(v.scalar_tensor_tensor(vec2[:], vec2[:], -1.0, vec1[:],
                                            op0=ALU.mult, op1=ALU.add), "stats_evac")
                v.wait_ge(sem("rstd1"), qtr + 1)
                inc(v.scalar_tensor_tensor(nmr1b[0:1, sl], vecmu[:], -1.0, vec2[:],
                                           op0=ALU.mult, op1=ALU.mult), "ln1_rdy")
                fence(v, v.tensor_copy(r1b[0:1, sl], vec2[:]), "fl_v")

        fl_v_after_stats = cnt("fl_v")

        @block.tensor
        def _(t):
            t.wait_ge(sem("ld_g"), ld_g_p0)
            t.wait_ge(sem("stats_evac"), 4)
            for f in range(NF):
                fi, la = f == 0, f == NF - 1
                for w in range(3):
                    t.matmul(pbank[w][0:1, 0:512], ones_col_b[:], wrhs[f][:, w * 512:(w + 1) * 512],
                             start=fi, stop=la)
                mmi = t.matmul(pbank[3][0:1, 0:64], ones_col_b[:], wrhs[f][:, 1536:1600],
                               start=fi, stop=la)
            inc(mmi, "tks")

        @block.vector
        def _(v):
            wchain(v, "tks")
            for w in range(3):
                v.tensor_copy(wbar[0:1, w * 512:(w + 1) * 512], pbank[w][0:1, 0:512])
            inc(v.tensor_copy(wbar[0:1, 1536:1600], pbank[3][0:1, 0:64]), "tks")

        for wdw in range(8):
            tok0 = wdw * 512

            @block.gpsimd
            def _(g):
                if wdw == 0:
                    g.wait_ge(sem("stat_used"), 4 * NF)
                    g.wait_ge(sem("fl_v"), fl_v_after_stats)
                else:
                    g.wait_ge(sem("proj_mm"), 4 * wdw)
                for f in range(NF):
                    inc(g.dma_start(out=xw[f][:],
                                    in_=xT[f * 128:(f + 1) * 128, tok0:tok0 + 512]),
                        "ld_xw", 16)
            ld_xw_w = cnt("ld_xw")

            @block.tensor
            def _(t):
                if wdw == 0:
                    wchain(t, "tks")
                else:
                    t.wait_ge(sem("proj_evac"), 4 * wdw - 1)
                t.wait_ge(sem("ln1_rdy"), wdw // 2 + 1)
                inc(t.matmul(pbank[0][:, 0:512], ones_row_b[:],
                             r1b[0:1, tok0:tok0 + 512], start=True, stop=True), "rbc_mm")

            @block.vector
            def _(v):
                v.wait_ge(sem("rbc_mm"), wdw + 1)
                fence(v, v.tensor_copy(rbc_b[:], pbank[0][:, 0:512]), "fl_v")
                v.wait_ge(sem("ld_xw"), ld_xw_w)
                for f in range(NF):
                    ii = v.tensor_mul(xw[f][:], xw[f][:], rbc_b[:])
                inc(ii, "xr_rdy")
            xr_w = cnt("xr_rdy")

            @block.tensor
            def _(t):
                t.wait_ge(sem("xr_rdy"), xr_w)
                for tci in range(4):
                    tc = wdw * 4 + tci
                    c = slice(tci * 128, (tci + 1) * 128)
                    par = tc % 2
                    qp0, qp1, kvp = pbank[4 * par], pbank[4 * par + 1], pbank[4 * par + 2]
                    imp = pbank[4 * par + 3][:, 0:64]
                    if tc >= 2:
                        t.wait_ge(sem("proj_evac"), tc - 1)
                    for f in range(NF):
                        fi = f == 0
                        t.matmul(qp0[:, 0:512], xw[f][:, c], wrhs[f][:, 0:512], start=fi, stop=False)
                        t.matmul(qp1[:, 0:512], xw[f][:, c], wrhs[f][:, 512:1024], start=fi, stop=False)
                        t.matmul(kvp[:, 0:512], xw[f][:, c], wrhs[f][:, 1024:1536], start=fi, stop=False)
                        t.matmul(imp, xw[f][:, c], wrhs[f][:, 1536:1600], start=fi, stop=False)
                    nm = nmr1b[0:1, tok0 + tci * 128:tok0 + (tci + 1) * 128]
                    t.matmul(qp0[:, 0:512], nm, wbar[0:1, 0:512], start=False, stop=True)
                    t.matmul(qp1[:, 0:512], nm, wbar[0:1, 512:1024], start=False, stop=True)
                    t.matmul(kvp[:, 0:512], nm, wbar[0:1, 1024:1536], start=False, stop=True)
                    t.matmul(imp, nm, wbar[0:1, 1536:1600], start=False, stop=False)
                    inc(t.matmul(imp, ones_row_b[:], b1row_b[:], start=False, stop=True), "proj_mm")

            @block.scalar
            def _(s):
                for tci in range(4):
                    tc = wdw * 4 + tci
                    par = tc % 2
                    qp0, qp1, kvp = pbank[4 * par], pbank[4 * par + 1], pbank[4 * par + 2]
                    imp = pbank[4 * par + 3][:, 0:64]
                    s.wait_ge(sem("proj_mm"), tc + 1)
                    if tc >= 2:
                        s.wait_ge(sem("proj_evac"), tc - 1)
                    for h in range(4):
                        sa = scr_act[:, (par * 10 + h) * 128:(par * 10 + h + 1) * 128]
                        s.activation(sa, qp0[:, h * 128:(h + 1) * 128], AF.Square,
                                     accum_out=rmsq[par][:, h:h + 1])
                    for h in range(4):
                        sa = scr_act[:, (par * 10 + 4 + h) * 128:(par * 10 + 5 + h) * 128]
                        s.activation(sa, qp1[:, h * 128:(h + 1) * 128], AF.Square,
                                     accum_out=rmsq[par][:, 4 + h:5 + h])
                    for g2 in range(2):
                        sa = scr_act[:, (par * 10 + 8 + g2) * 128:(par * 10 + 9 + g2) * 128]
                        ii = s.activation(sa, kvp[:, g2 * 128:(g2 + 1) * 128], AF.Square,
                                          accum_out=rmsk[par][:, g2:g2 + 1])
                    fence(s, ii, "fl_s")
                    s.activation(rmsq[par][:], rmsq[par][:], AF.Ln, scale=1.0 / D, bias=eps2_col[:])
                    fence(s, s.activation(rmsk[par][:, 0:2], rmsk[par][:, 0:2], AF.Ln,
                                          scale=1.0 / D, bias=eps2_col[:]), "fl_s")
                    s.activation(rmsq[par][:], rmsq[par][:], AF.Exp, scale=-0.5)
                    s.activation(rmsk[par][:, 0:2], rmsk[par][:, 0:2], AF.Exp, scale=-0.5)
                    inc(s.activation(tanh_sb[par][:], imp, AF.Tanh), "act_tc")

            @block.vector
            def _(v):
                for tci in range(4):
                    tc = wdw * 4 + tci
                    par = tc % 2
                    qp0, qp1, kvp = pbank[4 * par], pbank[4 * par + 1], pbank[4 * par + 2]
                    v.wait_ge(sem("act_tc"), tc + 1)
                    for h in range(8):
                        qp = qp0 if h < 4 else qp1
                        v.tensor_scalar(qhat[tc][:, h * 128:(h + 1) * 128],
                                        qp[:, (h % 4) * 128:(h % 4 + 1) * 128],
                                        rmsq[par][:, h:h + 1], None, op0=ALU.mult)
                    for g2 in range(2):
                        v.tensor_scalar(khat[tc][:, g2 * 128:(g2 + 1) * 128],
                                        kvp[:, g2 * 128:(g2 + 1) * 128],
                                        rmsk[par][:, g2:g2 + 1], None, op0=ALU.mult)
                    v.tensor_copy(vhat[tc][:], kvp[:, 256:512])
                    inc(v.tensor_tensor_reduce(scr_ttr[:, par * 64:(par + 1) * 64],
                                               tanh_sb[par][:], w2bc[:], 1.0, 0.0,
                                               op0=ALU.mult, op1=ALU.add,
                                               accum_out=logits_cm[:, tc:tc + 1]), "proj_evac")
        proj_done = cnt("proj_evac")

        if dbg and phases < 2:
            @block.sync
            def _(sy):
                sy.wait_ge(sem("proj_evac"), proj_done)
                if "logits" in dbg:
                    inc(sy.dma_start(out=dout("logits", [128, TC]), in_=logits_cm[:]), "dbg_st", 16)
                if "r1" in dbg:
                    inc(sy.dma_start(out=dout("r1", [1, S], BF16), in_=r1b[:]), "dbg_st", 16)
                if "qhat" in dbg:
                    o = dout("qhat", [S, NQ], BF16)
                    for tc in range(TC):
                        inc(sy.dma_start(out=o[tc * 128:(tc + 1) * 128, :], in_=qhat[tc][:]), "dbg_st", 16)
                if "khat" in dbg:
                    o = dout("khat", [S, NKVD], BF16)
                    for tc in range(TC):
                        inc(sy.dma_start(out=o[tc * 128:(tc + 1) * 128, :], in_=khat[tc][:]), "dbg_st", 16)
                if "vhat" in dbg:
                    o = dout("vhat", [S, NKVD], BF16)
                    for tc in range(TC):
                        inc(sy.dma_start(out=o[tc * 128:(tc + 1) * 128, :], in_=vhat[tc][:]), "dbg_st", 16)
                sy.wait_ge(sem("dbg_st"), cnt("dbg_st"))

        if phases < 2:
            lower_extended_insts(nc)
            return nc, dbg_outs

        ar.reset_to("persist")
        qf = [ar.alloc(f"qf{h}", [128, CT], BF16) for h in range(NH)]
        kf = [ar.alloc(f"kf{g}", [128, CT], BF16) for g in range(NKV)]
        vtm = [[ar.alloc(f"vtm{g}_{j}", [128, 132], BF16) for j in range(NCT)] for g in range(NKV)]
        ar.mark("regA_end")
        assert ar.off <= ar.marks["p1end"], "phase2 regionA overflow"
        ar.reset_to("qkv_end")
        kaT_sb = ar.alloc("kaT_sb", [1, 320], BF16)
        cs_tile = [ar.alloc(f"cs_tile{i}", [128, 2 * D], BF16) for i in range(2)]
        logitsT = ar.alloc("logitsT", [32, 128], F32)
        tk_lo = ar.alloc("tk_lo", [32, 1], F32)
        tk_hi = ar.alloc("tk_hi", [32, 1], F32)
        tk_mid = ar.alloc("tk_mid", [32, 1], F32)
        tk_gt = ar.alloc("tk_gt", [32, 128], F32)
        tk_gt2 = ar.alloc("tk_gt2", [32, 128], F32)
        tk_pc = ar.alloc("tk_pc", [32, 1], F32)
        tk_sc = ar.alloc("tk_sc", [1, 1], F32)
        tk_cond = ar.alloc("tk_cond", [32, 1], U32)
        tk_condl = ar.alloc("tk_condl", [32, 1], U32)
        tk_z = ar.alloc("tk_z", [32, 128], F32)
        tk_scan = ar.alloc("tk_scan", [32, 128], F32)
        tk_row = ar.alloc("tk_row", [1, 64], F32)
        tk_off = ar.alloc("tk_off", [32, 1], F32)
        tk_d32 = ar.alloc("tk_d32", [32, 1], F32)
        tk_selu = ar.alloc("tk_selu", [32, 128], U32)
        tk_le = ar.alloc("tk_le", [32, 128], F32)
        tk_big = ar.alloc("tk_big", [32, 128], F32)
        tk_pos = ar.alloc("tk_pos", [32, 128], F32)
        skw_b = [ar.alloc(f"skw{i}", [128, 320], BF16) for i in range(2)]
        vsf = [ar.alloc(f"vsf{g}", [128, 640], BF16) for g in range(NKV)]
        ar.mark("p2end")

        @block.gpsimd
        def _(g):
            inc(g.dma_start(out=kaT_sb[:], in_=kaT_row[:]), "ld_g", 16)
        ld_g_p2 = cnt("ld_g")

        @block.tensor
        def _(t):
            t.wait_ge(sem("ld_g"), ld_g_p2)
            t.wait_ge(sem("proj_evac"), proj_done)
            wchain(t, "tks")
            inc(t.matmul(pbank[0][:, 0:320], ones_row_b[:], kaT_sb[:], start=True, stop=True),
                "tks")

        @block.vector
        def _(v):
            wchain(v, "tks")
            inc(v.tensor_copy(kabc[:], pbank[0][:, 0:320]), "tks")

        def skw_build(v, par, tc, shalf, wvec):
            a_sl = kbT_sb[:, (tc % 2) * 32 + shalf * 16:(tc % 2) * 32 + shalf * 16 + 16]
            in0 = a_sl.rearrange("p (a one) -> p a one", one=1).to_broadcast([128, 16, 20])
            jb = tc // 2
            in1 = kabc[0:128, jb * 20:(jb + 1) * 20].rearrange("p (one c) -> p one c", one=1)\
                .to_broadcast([128, 16, 20])
            o = skw_b[par].rearrange("p (a c) -> p a c", a=16)
            return v.scalar_tensor_tensor(o, in0, wvec[:, tc:tc + 1], in1,
                                          op0=ALU.mult, op1=ALU.mult)

        @block.scalar
        def _(s):
            s.wait_ge(sem("proj_evac"), proj_done)
            inc(s.activation(weights_cm[:], logits_cm[:], AF.Sigmoid, bias=b2bias[:]), "w_rdy")

        @block.tensor
        def _(t):
            t.wait_ge(sem("proj_evac"), proj_done)
            inc(t.transpose(pbank[1][0:32, 0:128], logits_cm[:], ident_f[:]), "tks")

        @block.vector
        def _(v):
            wchain(v, "tks")
            v.tensor_copy(logitsT[:], pbank[1][0:32, 0:128])
            fence(v, v.tensor_reduce(tk_pc[:, 0:1], w2bc[0:32, :], axis=AX,
                                     op=ALU.add, apply_absolute_value=True), "fl_v")
            fence(v, v.tensor_scalar(tk_hi[:], tk_pc[:, 0:1], 1.0, 1.0, op0=ALU.mult, op1=ALU.add), "fl_v")
            v.memset(tk_z[:], 0.0)
            inc(v.scalar_tensor_tensor(tk_lo[:], tk_hi[:], -1.0, tk_hi[:], op0=ALU.mult,
                                       op1=ALU.min), "tks")

        for it in range(BISECT):
            @block.vector
            def _(v):
                wchain(v, "tks")
                if it > 0:
                    v.tensor_scalar(tk_cond[:], pbank[3][0:32, 0:1], 2048.0, None, op0=ALU.is_gt)
                    fence(v, v.tensor_scalar(tk_condl[:], pbank[3][0:32, 0:1], 2048.0, None,
                                             op0=ALU.is_le), "fl_v")
                    v.copy_predicated(tk_lo[:], tk_cond[:], tk_mid[:])
                    fence(v, v.copy_predicated(tk_hi[:], tk_condl[:], tk_mid[:]), "fl_v")
                fence(v, v.tensor_add(tk_mid[:], tk_lo[:], tk_hi[:]), "fl_v")
                fence(v, v.tensor_scalar(tk_mid[:], tk_mid[:], 0.5, None, op0=ALU.mult), "fl_v")
                fence(v, v.tensor_scalar(tk_gt[:], logitsT[:], tk_mid[:, 0:1], None, op0=ALU.is_gt), "fl_v")
                inc(v.reduce_sum(tk_pc[:], tk_gt[:], axis=AX), "tks")

            @block.tensor
            def _(t):
                wchain(t, "tks")
                inc(t.matmul(pbank[2][0:1, 0:1], ones_col_f[0:32, :], tk_pc[:], start=True, stop=True),
                    "tks")

            @block.vector
            def _(v):
                wchain(v, "tks")
                inc(v.tensor_copy(tk_sc[:], pbank[2][0:1, 0:1]), "tks")

            @block.tensor
            def _(t):
                wchain(t, "tks")
                inc(t.matmul(pbank[3][0:32, 0:1], ones_row_f[0:1, 0:32], tk_sc[:], start=True, stop=True),
                    "tks")

        @block.vector
        def _(v):
            wchain(v, "tks")
            v.tensor_scalar(tk_cond[:], pbank[3][0:32, 0:1], 2048.0, None, op0=ALU.is_gt)
            fence(v, v.tensor_scalar(tk_condl[:], pbank[3][0:32, 0:1], 2048.0, None, op0=ALU.is_le), "fl_v")
            v.copy_predicated(tk_lo[:], tk_cond[:], tk_mid[:])
            fence(v, v.copy_predicated(tk_hi[:], tk_condl[:], tk_mid[:]), "fl_v")
            v.tensor_scalar(tk_gt[:], logitsT[:], tk_hi[:, 0:1], None, op0=ALU.is_gt)
            fence(v, v.tensor_scalar(tk_gt2[:], logitsT[:], tk_lo[:, 0:1], None, op0=ALU.is_gt), "fl_v")
            fence(v, v.tensor_sub(tk_gt2[:], tk_gt2[:], tk_gt[:]), "fl_v")
            inc(v.reduce_sum(tk_pc[:], tk_gt[:], axis=AX), "tks")

        @block.tensor
        def _(t):
            wchain(t, "tks")
            inc(t.matmul(pbank[2][0:1, 0:1], ones_col_f[0:32, :], tk_pc[:], start=True, stop=True),
                "tks")

        @block.vector
        def _(v):
            wchain(v, "tks")
            inc(v.tensor_scalar(tk_sc[:], pbank[2][0:1, 0:1], -1.0, 2048.0, op0=ALU.mult,
                                op1=ALU.add), "tks")

        @block.tensor
        def _(t):
            wchain(t, "tks")
            inc(t.matmul(pbank[3][0:32, 0:1], ones_row_f[0:1, 0:32], tk_sc[:], start=True, stop=True),
                "tks")

        def prefix_pass(v, t_sems, src, dst, tag):
            fence(v, v.tensor_tensor_scan(dst, src, tk_z[:], 0.0, op0=ALU.add, op1=ALU.add), "fl_v")
            inc(v.tensor_copy(tk_pc[:], dst[0:32, 127:128]), "tks")

        @block.vector
        def _(v):
            wchain(v, "tks")
            fence(v, v.tensor_copy(tk_d32[:], pbank[3][0:32, 0:1]), "fl_v")
            prefix_pass(v, None, tk_gt2[:], tk_scan[:], "pfb")

        @block.tensor
        def _(t):
            wchain(t, "tks")
            inc(t.transpose(pbank[4][0:1, 0:32], tk_pc[0:32, 0:1], ident_f[0:32, 0:32]), "tks")

        @block.vector
        def _(v):
            wchain(v, "tks")
            fence(v, v.tensor_copy(tk_row[0:1, 0:32], pbank[4][0:1, 0:32]), "fl_v")
            fence(v, v.memset(tk_row[0:1, 32:64], 0.0), "fl_v")
            fence(v, v.tensor_tensor_scan(tk_row[0:1, 32:64], tk_row[0:1, 0:32], tk_row[0:1, 32:64],
                                          0.0, op0=ALU.add, op1=ALU.add), "fl_v")
            v.memset(tk_row[0:1, 0:1], 0.0)
            inc(v.tensor_copy(tk_row[0:1, 1:32], tk_row[0:1, 32:63]), "tks")

        @block.tensor
        def _(t):
            wchain(t, "tks")
            inc(t.transpose(pbank[4][0:32, 0:1], tk_row[0:1, 0:32], ident_f[0:1, 0:1]), "tks")

        @block.vector
        def _(v):
            wchain(v, "tks")
            fence(v, v.tensor_copy(tk_off[:], pbank[4][0:32, 0:1]), "fl_v")
            fence(v, v.tensor_scalar(tk_scan[:], tk_scan[:], tk_off[:, 0:1], None, op0=ALU.add), "fl_v")
            fence(v, v.tensor_scalar(tk_le[:], tk_scan[:], tk_d32[:, 0:1], None, op0=ALU.is_le), "fl_v")
            fence(v, v.tensor_mul(tk_gt2[:], tk_gt2[:], tk_le[:]), "fl_v")
            fence(v, v.tensor_add(tk_gt[:], tk_gt[:], tk_gt2[:]), "fl_v")
            prefix_pass(v, None, tk_gt[:], tk_scan[:], "pfs")

        @block.tensor
        def _(t):
            wchain(t, "tks")
            inc(t.transpose(pbank[4][0:1, 0:32], tk_pc[0:32, 0:1], ident_f[0:32, 0:32]), "tks")

        @block.vector
        def _(v):
            wchain(v, "tks")
            fence(v, v.tensor_copy(tk_row[0:1, 0:32], pbank[4][0:1, 0:32]), "fl_v")
            fence(v, v.memset(tk_row[0:1, 32:64], 0.0), "fl_v")
            fence(v, v.tensor_tensor_scan(tk_row[0:1, 32:64], tk_row[0:1, 0:32], tk_row[0:1, 32:64],
                                          0.0, op0=ALU.add, op1=ALU.add), "fl_v")
            v.memset(tk_row[0:1, 0:1], 0.0)
            inc(v.tensor_copy(tk_row[0:1, 1:32], tk_row[0:1, 32:63]), "tks")

        @block.tensor
        def _(t):
            wchain(t, "tks")
            inc(t.transpose(pbank[4][0:32, 0:1], tk_row[0:1, 0:32], ident_f[0:1, 0:1]), "tks")

        @block.vector
        def _(v):
            wchain(v, "tks")
            fence(v, v.tensor_copy(tk_off[:], pbank[4][0:32, 0:1]), "fl_v")
            fence(v, v.tensor_scalar(tk_scan[:], tk_scan[:], tk_off[:, 0:1], None, op0=ALU.add), "fl_v")
            v.tensor_scalar(tk_pos[:], tk_scan[:], -1.0, None, op0=ALU.add)
            v.tensor_scalar(tk_selu[:], tk_gt[:], 0.5, None, op0=ALU.is_lt)
            fence(v, v.memset(tk_big[:], BIGPOS), "fl_v")
            inc(v.copy_predicated(tk_pos[:], tk_selu[:], tk_big[:]), "tks")

        @block.tensor
        def _(t):
            wchain(t, "tks")
            inc(t.transpose(pbank[5][0:128, 0:32], tk_pos[0:32, 0:128], ident_f[0:32, 0:32]), "tks")
            inc(t.transpose(pbank[6][0:128, 0:32], tk_gt[0:32, 0:128], ident_f[0:32, 0:32]), "tks")

        @block.vector
        def _(v):
            wchain(v, "tks")
            inc(v.tensor_copy(sel_cm[:], pbank[6][0:128, 0:32]), "tks")
            v.tensor_copy(selu_cm[:], pbank[6][0:128, 0:32])
            fence(v, v.tensor_copy(pos_cm[:], pbank[5][0:128, 0:32]), "fl_v")
            v.wait_ge(sem("w_rdy"), 1)
            fence(v, v.tensor_scalar(restw_cm[:], sel_cm[:], -1.0, 1.0, op0=ALU.mult, op1=ALU.add), "fl_v")
            inc(v.tensor_mul(restw_cm[:], weights_cm[:], restw_cm[:]), "rw_rdy")

        pos_fl = cnt("fl_v")

        @block.gpsimd
        def _(g):
            g.wait_ge(sem("fl_v"), pos_fl)
            btk = g.to_reg(TK - 1)
            for tc in range(TC):
                inc(g.indirect_dma_start(
                    out=qd_dram[:], out_offset=IndirectOffsetOnAxis(pos_cm[:, tc:tc + 1], 0),
                    in_=qhat[tc][:], in_offset=None,
                    bounds_check=btk, oob_is_err=False), "sc_q", 16)
                inc(g.indirect_dma_start(
                    out=kd_dram[:], out_offset=IndirectOffsetOnAxis(pos_cm[:, tc:tc + 1], 0),
                    in_=khat[tc][:], in_offset=None,
                    bounds_check=btk, oob_is_err=False), "sc_k", 16)
                inc(g.indirect_dma_start(
                    out=vd_dram[:], out_offset=IndirectOffsetOnAxis(pos_cm[:, tc:tc + 1], 0),
                    in_=vhat[tc][:], in_offset=None,
                    bounds_check=btk, oob_is_err=False), "sc_v", 16)
        sc_q_all, sc_k_all = cnt("sc_q"), cnt("sc_k")
        sc_v_all = cnt("sc_v")

        @block.gpsimd
        def _(g):
            g.wait_ge(sem("fl_v"), pos_fl)
            btk2 = g.to_reg(TK - 1)
            for tc in range(TC):
                i = tc % 2
                if tc >= 2:
                    g.wait_ge(sem(f"sc_i{i}"), 32 * (tc // 2))
                inc(g.dma_start(out=cs_tile[i][:, 0:D],
                                in_=cos_tm[tc * 128:(tc + 1) * 128, :]), f"cs_ld{i}", 16)
                inc(g.dma_start(out=cs_tile[i][:, D:2 * D],
                                in_=sin_tm[tc * 128:(tc + 1) * 128, :]), f"cs_ld{i}", 16)
                g.wait_ge(sem(f"cs_ld{i}"), 32 * (tc // 2 + 1))
                inc(g.indirect_dma_start(
                    out=cosd_dram[:], out_offset=IndirectOffsetOnAxis(pos_cm[:, tc:tc + 1], 0),
                    in_=cs_tile[i][:, 0:D], in_offset=None,
                    bounds_check=btk2, oob_is_err=False), f"sc_i{i}", 16)
                inc(g.indirect_dma_start(
                    out=sind_dram[:], out_offset=IndirectOffsetOnAxis(pos_cm[:, tc:tc + 1], 0),
                    in_=cs_tile[i][:, D:2 * D], in_offset=None,
                    bounds_check=btk2, oob_is_err=False), f"sc_i{i}", 16)
        sc_i_all = (cnt("sc_i0"), cnt("sc_i1"))

        for shalf in range(2):
            @block.vector
            def _(v):
                for tc in range(TC):
                    if shalf == 0 and tc == 0:
                        wchain(v, "tks")
                        v.wait_ge(sem("w_rdy"), 1)
                    if shalf * TC + tc >= 2:
                        v.wait_ge(sem("skw_used"), shalf * TC + tc - 1)
                    inc(skw_build(v, tc % 2, tc, shalf, weights_cm), "skw_rdy")

            @block.tensor
            def _(t):
                for tc in range(TC):
                    t.wait_ge(sem("skw_rdy"), shalf * TC + tc + 1)
                    if shalf == 1 and tc == 0:
                        t.wait_ge(sem("qs_evac"), 8)
                    fi, la = tc == 0, tc == TC - 1
                    for h in range(NH):
                        mmi = t.matmul(pbank[h][:, 0:320], qhat[tc][:, h * 128:(h + 1) * 128],
                                       skw_b[tc % 2][:], start=fi, stop=la)
                    inc(mmi, "skw_used")

            @block.vector
            def _(v):
                v.wait_ge(sem("skw_used"), (shalf + 1) * TC)
                for h in range(NH):
                    inc(v.tensor_scalar(qf[h][:, TK + shalf * 320:TK + shalf * 320 + 320],
                                        pbank[h][:, 0:320], qwss[:, 0:1], None, op0=ALU.mult),
                        "qs_evac")

        for shalf in range(2):
            @block.vector
            def _(v):
                for tc in range(TC):
                    kk = (2 + shalf) * TC + tc
                    if shalf == 0 and tc == 0:
                        v.wait_ge(sem("rw_rdy"), 1)
                    if kk >= 2:
                        v.wait_ge(sem("skw_used"), kk - 1)
                    inc(skw_build(v, tc % 2, tc, shalf, restw_cm), "skw_rdy")

            @block.tensor
            def _(t):
                for tc in range(TC):
                    kk = (2 + shalf) * TC + tc
                    t.wait_ge(sem("skw_rdy"), kk + 1)
                    if tc == 0 and shalf == 0:
                        t.wait_ge(sem("qs_evac"), 16)
                    if tc == 0 and shalf == 1:
                        t.wait_ge(sem("kvs_evac"), 4)
                    fi, la = tc == 0, tc == TC - 1
                    for g2 in range(NKV):
                        t.matmul(pbank[g2][:, 0:320], khat[tc][:, g2 * 128:(g2 + 1) * 128],
                                 skw_b[tc % 2][:], start=fi, stop=la)
                        mmi = t.matmul(pbank[2 + g2][:, 0:320], vhat[tc][:, g2 * 128:(g2 + 1) * 128],
                                       skw_b[tc % 2][:], start=fi, stop=la)
                    inc(mmi, "skw_used")

            @block.vector
            def _(v):
                v.wait_ge(sem("skw_used"), (3 + shalf) * TC)
                for g2 in range(NKV):
                    inc(v.tensor_scalar(kf[g2][:, TK + shalf * 320:TK + shalf * 320 + 320],
                                        pbank[g2][:, 0:320], kwss[:, 0:1], None, op0=ALU.mult),
                        "kvs_evac")
                    inc(v.tensor_scalar(vsf[g2][:, shalf * 320:shalf * 320 + 320],
                                        pbank[2 + g2][:, 0:320], ssbc[:, 0:1], None, op0=ALU.mult),
                        "kvs_evac")
        kvs_all = cnt("kvs_evac")

        @block.vector
        def _(v):
            v.wait_ge(sem("proj_evac"), proj_done)
            for g2 in range(NKV):
                for j in range(NCT):
                    v.memset(vtm[g2][j][:, 129:132], 0.0)
                    ii = v.memset(vtm[g2][j][:, 128:129], 1.0)
            inc(ii, "vtm_ones")

        for g2 in range(NKV):
            for j in range(5):
                jj = g2 * 5 + j

                @block.tensor
                def _(t):
                    if jj == 0:
                        t.wait_ge(sem("kvs_evac"), kvs_all)
                    if jj >= 2:
                        t.wait_ge(sem("vs_ev"), jj - 1)
                    inc(t.transpose(pbank[4 + jj % 2].bitcast(BF16)[:, 0:128],
                                    vsf[g2][:, j * 128:(j + 1) * 128], ident_b[:]), "vs_tr")

                @block.vector
                def _(v):
                    v.wait_ge(sem("vs_tr"), jj + 1)
                    inc(v.tensor_copy(vtm[g2][16 + j][:, 0:128],
                                      pbank[4 + jj % 2].bitcast(BF16)[:, 0:128]), "vs_ev")
        vs_all = cnt("vs_ev")

        if dbg and phases < 3:
            @block.sync
            def _(sy):
                sy.wait_ge(sem("vs_ev"), vs_all)
                wchain(sy, "tks")
                sy.wait_ge(sem("rw_rdy"), 1)
                sy.wait_ge(sem("qs_evac"), 16)
                sy.wait_ge(sem("sc_q"), sc_q_all)
                sy.wait_ge(sem("sc_k"), sc_k_all)
                sy.wait_ge(sem("sc_v"), sc_v_all)
                sy.wait_ge(sem("sc_i0"), sc_i_all[0])
                sy.wait_ge(sem("sc_i1"), sc_i_all[1])
                if "pos" in dbg:
                    inc(sy.dma_start(out=dout("pos", [128, TC], U32), in_=pos_cm[:]), "dbg_st", 16)
                if "sel" in dbg:
                    inc(sy.dma_start(out=dout("sel", [128, TC]), in_=sel_cm[:]), "dbg_st", 16)
                if "weights" in dbg:
                    inc(sy.dma_start(out=dout("weights", [128, TC]), in_=weights_cm[:]), "dbg_st", 16)
                if "restw" in dbg:
                    inc(sy.dma_start(out=dout("restw", [128, TC]), in_=restw_cm[:]), "dbg_st", 16)
                if "qs" in dbg:
                    o = dout("qs", [NH * 128, 640], BF16)
                    for h in range(NH):
                        inc(sy.dma_start(out=o[h * 128:(h + 1) * 128, :], in_=qf[h][:, TK:CT]), "dbg_st", 16)
                if "ks" in dbg:
                    o = dout("ks", [NKV * 128, 640], BF16)
                    for g2 in range(NKV):
                        inc(sy.dma_start(out=o[g2 * 128:(g2 + 1) * 128, :], in_=kf[g2][:, TK:CT]), "dbg_st", 16)
                if "vs" in dbg:
                    o = dout("vs", [NKV * 128, 640], BF16)
                    for g2 in range(NKV):
                        inc(sy.dma_start(out=o[g2 * 128:(g2 + 1) * 128, :], in_=vsf[g2][:]), "dbg_st", 16)
                if "cosd" in dbg:
                    inc(sy.dma_start(out=dout("cosd", [TK, D], BF16), in_=cosd_dram[:]), "dbg_st", 16)
                if "qd" in dbg:
                    inc(sy.dma_start(out=dout("qd", [TK, NQ], BF16), in_=qd_dram[:]), "dbg_st", 16)
                sy.wait_ge(sem("dbg_st"), cnt("dbg_st"))

        if phases < 3:
            lower_extended_insts(nc)
            return nc, dbg_outs

        ar.reset_to("p1end")
        qwb1024 = ar.alloc("qwb1024", [128, NQ], BF16)
        qwsw1024 = ar.alloc("qwsw1024", [128, NQ], BF16)
        kwb256 = ar.alloc("kwb256", [128, NKVD], BF16)
        kwsw256 = ar.alloc("kwsw256", [128, NKVD], BF16)
        qd_r = [ar.alloc(f"qd_r{i}", [128, NQ], BF16) for i in range(4)]
        kcs_r = [ar.alloc(f"kcs_r{i}", [128, NKVD + 2 * D], BF16) for i in range(4)]
        ropq = [ar.alloc(f"ropq{i}", [128, NQ], BF16) for i in range(2)]
        ropk = [ar.alloc(f"ropk{i}", [128, NKVD], BF16) for i in range(2)]
        cosq = [ar.alloc(f"cosq{i}", [128, NQ], BF16) for i in range(2)]
        sinq = [ar.alloc(f"sinq{i}", [128, NQ], BF16) for i in range(2)]
        cosk = [ar.alloc(f"cosk{i}", [128, NKVD], BF16) for i in range(2)]
        sink = [ar.alloc(f"sink{i}", [128, NKVD], BF16) for i in range(2)]
        rotq = [ar.alloc(f"rotq{i}", [128, NQ], BF16) for i in range(2)]
        rotk = [ar.alloc(f"rotk{i}", [128, NKVD], BF16) for i in range(2)]
        sgn = [ar.alloc(f"sgn{i}", [128, 2 * D], BF16) for i in range(2)]
        pt_sb = ar.alloc("pt_sb", [128, NCT * 512], BF16)
        att_sc = [ar.alloc(f"attsc{i}", [128, 132], BF16) for i in range(4)]
        rcp_sb = ar.alloc("rcp_sb", [128, 8], F32)
        pt_sb2 = nc.alloc_sbuf_tensor_at("pt_sb2", [128, NCT * 512], BF16, offset=94720).ap()
        assert 94720 + NCT * 512 * 2 <= ar.marks["p1end"]
        ar.mark("p5top")
        out_sk = [ar.alloc(f"outsk{j}", [128, NQ], BF16) for j in range(5)]
        ar.mark("p4end")

        def p3_gate(e):
            e.wait_ge(sem("skw_used"), 128)
            e.wait_ge(sem("sc_q"), sc_q_all)
            e.wait_ge(sem("sc_k"), sc_k_all)
            e.wait_ge(sem("sc_v"), sc_v_all)
            e.wait_ge(sem("sc_i0"), sc_i_all[0])
            e.wait_ge(sem("sc_i1"), sc_i_all[1])
            e.wait_ge(sem("vs_ev"), vs_all)
            e.wait_ge(sem("qs_evac"), 16)
            e.wait_ge(sem("kvs_evac"), kvs_all)
            e.wait_ge(sem("vtm_ones"), 1)

        @block.tensor
        def _(t):
            p3_gate(t)
            rep8 = qw_rows[0:1, 0:128].rearrange("a (b d) -> a b d", b=1).to_broadcast([1, 8, 128])
            rep8s = qw_rows[0:1, 128:256].rearrange("a (b d) -> a b d", b=1).to_broadcast([1, 8, 128])
            rep2 = kw_rows[0:1, 0:128].rearrange("a (b d) -> a b d", b=1).to_broadcast([1, 2, 128])
            rep2s = kw_rows[0:1, 128:256].rearrange("a (b d) -> a b d", b=1).to_broadcast([1, 2, 128])
            t.matmul(pbank[6][:, 0:512], ones_row_f[:], rep8[0:1, 0:4, :], start=True, stop=True)
            t.matmul(pbank[7][:, 0:512], ones_row_f[:], rep8[0:1, 4:8, :], start=True, stop=True)
            inc(t.matmul(pbank[4][:, 0:256], ones_row_f[:], rep2[:], start=True, stop=True), "tks")

        @block.vector
        def _(v):
            p3_gate(v)
            wchain(v, "tks")
            v.tensor_copy(qwb1024[:, 0:512], pbank[6][:, 0:512])
            v.tensor_copy(qwb1024[:, 512:1024], pbank[7][:, 0:512])
            inc(v.tensor_copy(kwb256[:], pbank[4][:, 0:256]), "tks")

        @block.tensor
        def _(t):
            wchain(t, "tks")
            rep8s = qw_rows[0:1, 128:256].rearrange("a (b d) -> a b d", b=1).to_broadcast([1, 8, 128])
            rep2s = kw_rows[0:1, 128:256].rearrange("a (b d) -> a b d", b=1).to_broadcast([1, 2, 128])
            t.matmul(pbank[6][:, 0:512], ones_row_f[:], rep8s[0:1, 0:4, :], start=True, stop=True)
            t.matmul(pbank[7][:, 0:512], ones_row_f[:], rep8s[0:1, 4:8, :], start=True, stop=True)
            inc(t.matmul(pbank[4][:, 0:256], ones_row_f[:], rep2s[:], start=True, stop=True), "tks")

        @block.vector
        def _(v):
            wchain(v, "tks")
            v.tensor_copy(qwsw1024[:, 0:512], pbank[6][:, 0:512])
            v.tensor_copy(qwsw1024[:, 512:1024], pbank[7][:, 0:512])
            inc(v.tensor_copy(kwsw256[:], pbank[4][:, 0:256]), "tks")

        @block.sync
        def _(sy):
            p3_gate(sy)
            for j in range(NTK):
                i = j % 4
                if j >= 4:
                    sy.wait_ge(sem("rope_done"), j - 3)
                inc(sy.dma_start(out=qd_r[i][:], in_=qd_dram[j * 128:(j + 1) * 128, :]), f"g3_{i}", 16)
                inc(sy.dma_start(out=kcs_r[i][:, 0:NKVD], in_=kd_dram[j * 128:(j + 1) * 128, :]), f"g3_{i}", 16)
                inc(sy.dma_start(out=kcs_r[i][:, NKVD:NKVD + D], in_=cosd_dram[j * 128:(j + 1) * 128, :]), f"g3_{i}", 16)
                inc(sy.dma_start(out=kcs_r[i][:, NKVD + D:NKVD + 2 * D], in_=sind_dram[j * 128:(j + 1) * 128, :]), f"g3_{i}", 16)
                inc(sy.dma_start(out=vtm[0][j][:, 0:128], in_=vd_dram[j * 128:(j + 1) * 128, 0:128]), f"g3_{i}", 16)
                inc(sy.dma_start(out=vtm[1][j][:, 0:128], in_=vd_dram[j * 128:(j + 1) * 128, 128:256]), f"g3_{i}", 16)

        for j in range(NTK):
            i = j % 4
            par = j % 2

            @block.vector
            def _(v):
                v.wait_ge(sem(f"g3_{i}"), 96 * (j // 4 + 1))
                if j >= 2:
                    v.wait_ge(sem("q_trev"), 10 * (j - 1))
                cosd_t = kcs_r[i][:, NKVD:NKVD + D]
                sind_t = kcs_r[i][:, NKVD + D:NKVD + 2 * D]
                v.tensor_scalar(sgn[par][:, 0:64], sind_t[:, 0:64], -1.0, None, op0=ALU.mult)
                fence(v, v.tensor_copy(sgn[par][:, 64:128], sind_t[:, 64:128]), "fl_v")
                for hh in range(NH):
                    v.tensor_copy(rotq[par][:, hh * 128:hh * 128 + 64], qd_r[i][:, hh * 128 + 64:hh * 128 + 128])
                    v.tensor_copy(rotq[par][:, hh * 128 + 64:hh * 128 + 128], qd_r[i][:, hh * 128:hh * 128 + 64])
                for g2 in range(NKV):
                    v.tensor_copy(rotk[par][:, g2 * 128:g2 * 128 + 64], kcs_r[i][:, g2 * 128 + 64:g2 * 128 + 128])
                    v.tensor_copy(rotk[par][:, g2 * 128 + 64:g2 * 128 + 128], kcs_r[i][:, g2 * 128:g2 * 128 + 64])
                ce = cosd_t.rearrange("p (one d) -> p one d", one=1).to_broadcast([128, 8, 128])
                se = sgn[par][:, 0:128].rearrange("p (one d) -> p one d", one=1).to_broadcast([128, 8, 128])
                v.tensor_tensor(cosq[par].rearrange("p (r d) -> p r d", r=8), ce,
                                qwb1024.rearrange("p (r d) -> p r d", r=8), op=ALU.mult)
                v.tensor_tensor(sinq[par].rearrange("p (r d) -> p r d", r=8), se,
                                qwsw1024.rearrange("p (r d) -> p r d", r=8), op=ALU.mult)
                ce2 = cosd_t.rearrange("p (one d) -> p one d", one=1).to_broadcast([128, 2, 128])
                se2 = sgn[par][:, 0:128].rearrange("p (one d) -> p one d", one=1).to_broadcast([128, 2, 128])
                v.tensor_tensor(cosk[par].rearrange("p (r d) -> p r d", r=2), ce2,
                                kwb256.rearrange("p (r d) -> p r d", r=2), op=ALU.mult)
                fence(v, v.tensor_tensor(sink[par].rearrange("p (r d) -> p r d", r=2), se2,
                                         kwsw256.rearrange("p (r d) -> p r d", r=2), op=ALU.mult), "fl_v")
                v.tensor_mul(ropq[par][:], qd_r[i][:], cosq[par][:])
                v.tensor_mul(rotq[par][:], rotq[par][:], sinq[par][:])
                v.tensor_mul(ropk[par][:], kcs_r[i][:, 0:NKVD], cosk[par][:])
                fence(v, v.tensor_mul(rotk[par][:], rotk[par][:], sink[par][:]), "fl_v")
                v.tensor_add(ropq[par][:], ropq[par][:], rotq[par][:])
                inc(v.tensor_add(ropk[par][:], ropk[par][:], rotk[par][:]), "rope_done")

            @block.tensor
            def _(t):
                t.wait_ge(sem("rope_done"), j + 1)
                for hh in range(NH):
                    idx = j * 10 + hh
                    if idx >= 4:
                        t.wait_ge(sem("q_trev"), idx - 3)
                    inc(t.transpose(pbank[idx % 4].bitcast(BF16)[:, 0:128],
                                    ropq[par][:, hh * 128:(hh + 1) * 128], ident_b[:]), "q_tr")
                for g2 in range(NKV):
                    idx = j * 10 + 8 + g2
                    if idx >= 4:
                        t.wait_ge(sem("q_trev"), idx - 3)
                    inc(t.transpose(pbank[idx % 4].bitcast(BF16)[:, 0:128],
                                    ropk[par][:, g2 * 128:(g2 + 1) * 128], ident_b[:]), "q_tr")

            @block.vector
            def _(v):
                for hh in range(NH):
                    idx = j * 10 + hh
                    v.wait_ge(sem("q_tr"), idx + 1)
                    inc(v.tensor_copy(qf[hh][:, j * 128:(j + 1) * 128],
                                      pbank[idx % 4].bitcast(BF16)[:, 0:128]), "q_trev")
                for g2 in range(NKV):
                    idx = j * 10 + 8 + g2
                    v.wait_ge(sem("q_tr"), idx + 1)
                    inc(v.tensor_copy(kf[g2][:, j * 128:(j + 1) * 128],
                                      pbank[idx % 4].bitcast(BF16)[:, 0:128]), "q_trev")
        g3_tot = [cnt(f"g3_{i}") for i in range(4)]

        if dbg and phases < 4:
            @block.sync
            def _(sy):
                sy.wait_ge(sem("q_trev"), 10 * NTK)
                if "pos" in dbg:
                    inc(sy.dma_start(out=dout("pos", [128, TC], U32), in_=pos_cm[:]), "dbg_st", 16)
                if "sel" in dbg:
                    inc(sy.dma_start(out=dout("sel", [128, TC]), in_=sel_cm[:]), "dbg_st", 16)
                if "qfd" in dbg:
                    o = dout("qfd", [NH * 128, CT], BF16)
                    for hh in range(NH):
                        inc(sy.dma_start(out=o[hh * 128:(hh + 1) * 128, :], in_=qf[hh][:]), "dbg_st", 16)
                if "kfd" in dbg:
                    o = dout("kfd", [NKV * 128, CT], BF16)
                    for g2 in range(NKV):
                        inc(sy.dma_start(out=o[g2 * 128:(g2 + 1) * 128, :], in_=kf[g2][:]), "dbg_st", 16)
                if "vtmd" in dbg:
                    o = dout("vtmd", [NKV * NCT * 128, 132], BF16)
                    for g2 in range(NKV):
                        for jj in range(NCT):
                            inc(sy.dma_start(out=o[(g2 * NCT + jj) * 128:(g2 * NCT + jj + 1) * 128, :],
                                             in_=vtm[g2][jj][:]), "dbg_st", 16)
                sy.wait_ge(sem("dbg_st"), cnt("dbg_st"))

        if phases < 4:
            lower_extended_insts(nc)
            return nc, dbg_outs


        QCW = [512, 512, 512, 512, 512, 128]
        det_uses = [0, 0, 0, 0]
        chunks = [(h, qc) for h in range(NH) for qc in range(6)]
        NCH = len(chunks)
        ptb = [pt_sb, pt_sb2]
        n_pv = 0
        pv_after = []

        for ci in range(NCH + 1):
            cur = chunks[ci] if ci < NCH else None
            prev = chunks[ci - 1] if ci >= 1 else None
            prev_pv0 = n_pv

            @block.tensor
            def _(t, ci=ci, cur=cur, prev=prev, prev_pv0=prev_pv0):
                sc_left = list(range(NCT)) if cur is not None else []

                def sc_emit(n):
                    h, qc = cur
                    g2 = h // 4
                    w = QCW[qc]
                    q0 = qc * 512
                    for kt in sc_left[:n]:
                        idx = ci * NCT + kt
                        if ci == 0 and kt == 0:
                            t.wait_ge(sem("q_trev"), 10 * NTK)
                        if idx >= 2:
                            t.wait_ge(sem("s_exp"), idx - 1)
                        inc(t.matmul(pbank[idx % 2][:, 0:w], kf[g2][:, kt * 128:(kt + 1) * 128],
                                     qf[h][:, q0:q0 + w], start=True, stop=True), "s_mm")
                    del sc_left[:n]

                if prev is None:
                    sc_emit(NCT)
                    return
                hp, qcp = prev
                g2p = hp // 4
                wp = QCW[qcp]
                nq = wp // 128
                per = -(-NCT // nq)
                for qsub in range(nq):
                    if cur is not None:
                        sc_emit(per)
                    t.wait_ge(sem("s_exp"), ci * NCT)
                    if prev_pv0 + qsub >= 4:
                        t.wait_ge(sem("att_ev"), prev_pv0 + qsub - 3)
                    for kt in range(NCT):
                        mmi = t.matmul(pbank[2 + (prev_pv0 + qsub) % 4][:, 0:129],
                                       ptb[(ci - 1) % 2][:, kt * 512 + qsub * 128:kt * 512 + (qsub + 1) * 128],
                                       vtm[g2p][kt][:, 0:129],
                                       start=(kt == 0), stop=(kt == NCT - 1))
                    inc(mmi, "pv_mm")
                if cur is not None:
                    sc_emit(NCT)

            if cur is not None:
                @block.scalar
                def _(s, ci=ci, cur=cur):
                    if ci >= 2:
                        s.wait_ge(sem("pv_mm"), pv_after[ci - 2])
                    h, qc = cur
                    w = QCW[qc]
                    for kt in range(NCT):
                        idx = ci * NCT + kt
                        s.wait_ge(sem("s_mm"), idx + 1)
                        inc(s.activation(ptb[ci % 2][:, kt * 512:kt * 512 + w],
                                         pbank[idx % 2][:, 0:w],
                                         AF.Exp, scale=ISQD), "s_exp")

            if prev is not None:
                hp, qcp = prev
                wp = QCW[qcp]
                nq = wp // 128
                q0p = qcp * 512

                @block.vector
                def _(v, prev_pv0=prev_pv0, hp=hp, q0p=q0p, nq=nq):
                    v.wait_ge(sem("pv_mm"), prev_pv0 + nq)
                    for qsub in range(nq):
                        v.reciprocal(rcp_sb[:, qsub:qsub + 1],
                                     pbank[2 + (prev_pv0 + qsub) % 4][:, 128:129])
                    fence(v, v.memset(rcp_sb[:, 7:8], 0.0), "fl_v")
                    for qsub in range(nq):
                        i4 = (prev_pv0 + qsub) % 4
                        r0 = q0p + qsub * 128
                        if r0 < TK:
                            if det_uses[i4] > 0:
                                v.wait_ge(sem(f"det_st{i4}"), 16 * det_uses[i4])
                            dst = att_sc[i4][:, 0:128]
                        else:
                            sch = (r0 - TK) // 128
                            dst = out_sk[sch][:, hp * 128:(hp + 1) * 128]
                        inc(v.tensor_scalar(dst, pbank[2 + i4][:, 0:128],
                                            rcp_sb[:, qsub:qsub + 1], None, op0=ALU.mult), "att_ev")

                @block.sync
                def _(sy, prev_pv0=prev_pv0, hp=hp, q0p=q0p, nq=nq):
                    for qsub in range(nq):
                        i4 = (prev_pv0 + qsub) % 4
                        r0 = q0p + qsub * 128
                        if r0 < TK:
                            sy.wait_ge(sem("att_ev"), prev_pv0 + qsub + 1)
                            inc(sy.dma_start(out=outdet_dram[r0:r0 + 128, hp * 128:(hp + 1) * 128],
                                             in_=att_sc[i4][:, 0:128]), f"det_st{i4}", 16)
                            det_uses[i4] += 1
                n_pv += nq
                pv_after.append(n_pv)
        att_all = cnt("att_ev")
        det_all = [cnt(f"det_st{i}") for i in range(4)]

        if dbg and phases < 5:
            @block.sync
            def _(sy):
                sy.wait_ge(sem("att_ev"), att_all)
                for i4 in range(4):
                    sy.wait_ge(sem(f"det_st{i4}"), det_all[i4])
                if "pos" in dbg:
                    inc(sy.dma_start(out=dout("pos", [128, TC], U32), in_=pos_cm[:]), "dbg_st", 16)
                if "sel" in dbg:
                    inc(sy.dma_start(out=dout("sel", [128, TC]), in_=sel_cm[:]), "dbg_st", 16)
                if "outdet" in dbg:
                    inc(sy.dma_start(out=dout("outdet", [TK, NQ], BF16), in_=outdet_dram[:]), "dbg_st", 16)
                if "outsk" in dbg:
                    o = dout("outsk", [5 * 128, NQ], BF16)
                    for j in range(5):
                        inc(sy.dma_start(out=o[j * 128:(j + 1) * 128, :], in_=out_sk[j][:]), "dbg_st", 16)
                sy.wait_ge(sem("dbg_st"), cnt("dbg_st"))

        if phases < 5:
            lower_extended_insts(nc)
            return nc, dbg_outs

        p5top = ar.marks["p5top"]
        ar.reset_to("p4end")
        wo_sb = [ar.alloc(f"wo_sb{hd}", [128, DM], BF16) for hd in range(6)]
        kb_sb = ar.alloc("kb_sb", [32, 256], BF16)
        ka_sb = ar.alloc("ka_sb", [20, 128], BF16)
        assert ar.off <= ar.marks["qkv_end"], f"phase5 D2 overflow: {ar.off}"
        ar.reset_to("persist")
        kexp = [ar.alloc(f"kexp{i}", [128, 384], BF16) for i in range(2)]
        skfm = [ar.alloc(f"skfm{j}", [128, S], BF16) for j in range(5)]
        full_sb = [ar.alloc(f"full_sb{i}", [128, NQ], BF16) for i in range(2)]
        fullT = [ar.alloc(f"fullT{hd}", [128, S], BF16) for hd in range(8)]
        cc_sb = [ar.alloc(f"cc_sb{i}", [128, 512], BF16) for i in range(2)]
        wo_sb += [ar.alloc(f"wo_sb{hd}", [128, DM], BF16) for hd in range(6, 8)]
        sela_sb = ar.alloc("sela_sb", [32, 640], BF16)
        selc_sb = ar.alloc("selc_sb", [20, 640], BF16)
        det_t = [ar.alloc(f"det_t{i}", [128, NQ], BF16) for i in range(2)]
        assert ar.off <= p5top, f"phase5 overflow: {ar.off} > {p5top}"

        @block.gpsimd
        def _(g):
            g.wait_ge(sem("skw_used"), 128)
            g.wait_ge(sem("sc_q"), sc_q_all)
            g.wait_ge(sem("sc_k"), sc_k_all)
            g.wait_ge(sem("sc_v"), sc_v_all)
            for i in range(4):
                g.wait_ge(sem(f"g3_{i}"), g3_tot[i])
            inc(g.dma_start(out=kb_sb[:], in_=kb_n[:]), "ld_g5", 16)
            inc(g.dma_start(out=ka_sb[:], in_=ka_n[:]), "ld_g5", 16)
            for hd in range(6):
                inc(g.dma_start(out=wo_sb[hd][:], in_=wo_c[hd * 128:(hd + 1) * 128, :]), "ld_g5", 16)
            g.wait_ge(sem("pv_mm"), n_pv)
            g.wait_ge(sem("att_ev"), att_all)
            inc(g.dma_start(out=sela_sb[:], in_=sela[:]), "ld_g5", 16)
            inc(g.dma_start(out=selc_sb[:], in_=selc[:]), "ld_g5", 16)
            for hd in range(6, 8):
                inc(g.dma_start(out=wo_sb[hd][:], in_=wo_c[hd * 128:(hd + 1) * 128, :]), "ld_g5", 16)
        ld_g5_all = cnt("ld_g5")

        for j in range(5):
            par5 = j % 2

            @block.tensor
            def _(t):
                if j == 0:
                    t.wait_ge(sem("ld_g5"), ld_g5_all)
                    t.wait_ge(sem("att_ev"), att_all)
                if j >= 2:
                    t.wait_ge(sem("skfm_rdy"), j - 1)
                t.matmul(pbank[6 + par5][:, 0:256],
                         sela_sb[:, j * 128:(j + 1) * 128], kb_sb[:], start=True, stop=True)
                inc(t.matmul(pbank[6 + par5][:, 256:384],
                             selc_sb[:, j * 128:(j + 1) * 128], ka_sb[:], start=True, stop=True),
                    "kexp_mm")

            @block.vector
            def _(v):
                v.wait_ge(sem("kexp_mm"), j + 1)
                v.tensor_copy(kexp[par5][:, 0:256], pbank[6 + par5][:, 0:256])
                fence(v, v.tensor_copy(kexp[par5][:, 256:384],
                                       pbank[6 + par5][:, 256:384]), "fl_v")
                for ts in range(TC):
                    ii = v.tensor_tensor(
                        skfm[j][:, ts * 128:(ts + 1) * 128],
                        kexp[par5][:, (ts % 2) * 128:(ts % 2) * 128 + 128],
                        kexp[par5][:, 256 + ts // 2:256 + ts // 2 + 1].to_broadcast([128, 128]),
                        op=ALU.mult)
                inc(ii, "skfm_rdy")

        @block.gpsimd
        def _(g):
            for i4 in range(4):
                g.wait_ge(sem(f"det_st{i4}"), det_all[i4])
        btk5 = [None]

        @block.gpsimd
        def _(g):
            btk5[0] = g.to_reg(TK - 1)

        for tc in range(TC):
            par5 = tc % 2

            @block.tensor
            def _(t):
                if tc == 0:
                    t.wait_ge(sem("skfm_rdy"), 5)
                if tc >= 2:
                    t.wait_ge(sem("full_ev"), tc - 1)
                for j in range(5):
                    t.matmul(pbank[par5 * 2][:, 0:512], skfm[j][:, tc * 128:(tc + 1) * 128],
                             out_sk[j][:, 0:512], start=(j == 0), stop=(j == 4))
                    mmi = t.matmul(pbank[par5 * 2 + 1][:, 0:512], skfm[j][:, tc * 128:(tc + 1) * 128],
                                   out_sk[j][:, 512:1024], start=(j == 0), stop=(j == 4))
                inc(mmi, "exp_mm")

            @block.vector
            def _(v):
                v.wait_ge(sem("exp_mm"), tc + 1)
                if tc >= 2:
                    v.wait_ge(sem(f"full_st{par5}"), 16 * (tc // 2))
                v.tensor_copy(full_sb[par5][:, 0:512], pbank[par5 * 2][:, 0:512])
                v.tensor_copy(full_sb[par5][:, 512:1024], pbank[par5 * 2 + 1][:, 0:512])
                inc(v.memset(det_t[par5][:], 0.0), "full_ev")

            @block.gpsimd
            def _(g):
                g.wait_ge(sem("full_ev"), tc + 1)
                inc(g.indirect_dma_start(
                    out=det_t[par5][:], out_offset=None,
                    in_=outdet_dram[:],
                    in_offset=IndirectOffsetOnAxis(pos_cm[:, tc:tc + 1], 0),
                    bounds_check=btk5[0], oob_is_err=False), f"patch{par5}", 16)

            @block.vector
            def _(v):
                v.wait_ge(sem(f"patch{par5}"), 16 * (tc // 2 + 1))
                inc(v.copy_predicated(full_sb[par5][:],
                                      selu_cm[:, tc:tc + 1].to_broadcast([128, NQ]),
                                      det_t[par5][:]), "blend_ev")

            @block.sync
            def _(sy):
                sy.wait_ge(sem("blend_ev"), tc + 1)
                inc(sy.dma_start(out=full_dram[tc * 128:(tc + 1) * 128, :], in_=full_sb[par5][:]),
                    f"full_st{par5}", 16)
        full_st_all = (cnt("full_st0"), cnt("full_st1"))

        @block.sync
        def _(sy):
            sy.wait_ge(sem("full_st0"), full_st_all[0])
            sy.wait_ge(sem("full_st1"), full_st_all[1])
            for hd in range(8):
                inc(sy.dma_start_transpose(fullT[hd][:], full_dram[:, hd * 128:(hd + 1) * 128]),
                    "ftr_ld", 16)
        ftr_all = cnt("ftr_ld")

        n_wo = 0
        for oc in range(NF):
            for tcc in range(8):
                @block.tensor
                def _(t):
                    if n_wo == 0:
                        t.wait_ge(sem("ftr_ld"), ftr_all)
                        t.wait_ge(sem("full_ev"), TC)
                    if n_wo >= 2:
                        t.wait_ge(sem("cc_ev"), n_wo - 1)
                    for hd in range(8):
                        mmi = t.matmul(pbank[4 + n_wo % 2][:, 0:512],
                                       wo_sb[hd][:, oc * 128:(oc + 1) * 128],
                                       fullT[hd][:, tcc * 512:(tcc + 1) * 512],
                                       start=(hd == 0), stop=(hd == 7))
                    inc(mmi, "wo_mm")

                @block.vector
                def _(v):
                    v.wait_ge(sem("wo_mm"), n_wo + 1)
                    if n_wo >= 2:
                        v.wait_ge(sem(f"cc_st{n_wo % 2}"), 16 * (n_wo // 2))
                    inc(v.tensor_copy(cc_sb[n_wo % 2][:], pbank[4 + n_wo % 2][:, 0:512]), "cc_ev")

                @block.sync
                def _(sy):
                    sy.wait_ge(sem("cc_ev"), n_wo + 1)
                    inc(sy.dma_start(out=ccin[oc * 128:(oc + 1) * 128, tcc * 512:(tcc + 1) * 512],
                                     in_=cc_sb[n_wo % 2][:]), f"cc_st{n_wo % 2}", 16)
                n_wo += 1
        cc_st_all = (cnt("cc_st0"), cnt("cc_st1"))

        if dbg and phases < 6:
            @block.sync
            def _(sy):
                sy.wait_ge(sem("cc_st0"), cc_st_all[0])
                sy.wait_ge(sem("cc_st1"), cc_st_all[1])
                if "pos" in dbg:
                    inc(sy.dma_start(out=dout("pos", [128, TC], U32), in_=pos_cm[:]), "dbg_st", 16)
                if "sel" in dbg:
                    inc(sy.dma_start(out=dout("sel", [128, TC]), in_=sel_cm[:]), "dbg_st", 16)
                if "full" in dbg:
                    inc(sy.dma_start(out=dout("full", [S, NQ], BF16), in_=full_dram[:]), "dbg_st", 16)
                if "ccin" in dbg:
                    inc(sy.dma_start(out=dout("ccin", [DM, S], BF16), in_=ccin[:]), "dbg_st", 16)
                sy.wait_ge(sem("dbg_st"), cnt("dbg_st"))

        if phases < 6:
            lower_extended_insts(nc)
            return nc, dbg_outs

        @block.gpsimd
        def _(g):
            g.wait_ge(sem("cc_st0"), cc_st_all[0])
            g.wait_ge(sem("cc_st1"), cc_st_all[1])
            inc(g.collective_compute(
                "AllReduce", ALU.add,
                replica_groups=[[0, 1], [2, 3], [4, 5], [6, 7]],
                ins=[ccin.opt()], outs=[ccout.opt()]), "cc_done")
        cc_done_n = cnt("cc_done")

        ar.reset_to("persist")
        ar.top = nc.sbuf_top
        wbT_sb = [ar.alloc(f"wbT_sb{f}", [128, DM], BF16) for f in range(NF)]
        x1full = [ar.alloc(f"x1f{f}", [128, HTOK], BF16) for f in range(NF)]
        aT_sb = [ar.alloc(f"aT_sb{f}", [128, 128], BF16) for f in range(NF)]
        rwT_sb = [ar.alloc(f"rwT_sb{f}", [128, 8], BF16) for f in range(NF)]
        bw_sb = ar.alloc("bw_sb", [128, DM], BF16)
        bb_sb = ar.alloc("bb_sb", [128, NF], F32)
        sq2_off = ar.off
        sq2 = [ar.alloc(f"sq2{i}", [128, HTOK], BF16) for i in range(2)]
        x1r = [ar.alloc(f"x1r{f}", [128, 512], BF16) for f in range(NF)]
        r2bc = ar.alloc("r2bc", [128, 512], BF16)
        vm2 = ar.alloc("vm2", [1, HTOK], F32)
        vv2 = ar.alloc("vv2", [1, HTOK], F32)
        vt2 = nc.alloc_sbuf_tensor_at("vt2", [1, HTOK], F32, offset=(sq2_off + 63) // 64 * 64).ap()
        r2b = ar.alloc("r2b", [1, HTOK], BF16)
        nmr2b = ar.alloc("nmr2b", [1, HTOK], BF16)
        wbar2 = ar.alloc("wbar2", [1, DM + 136], BF16)
        rl_tm = ar.alloc("rl_tm", [128, 8], F32)
        rl_s1 = ar.alloc("rl_s1", [128, 8], F32)
        rl_s2 = ar.alloc("rl_s2", [128, 8], F32)
        z8 = ar.alloc("z8", [128, 8], F32)
        sel8f = ar.alloc("sel8f", [128, 8], F32)
        sel8fm = ar.alloc("sel8fm", [8, 512], BF16)
        esel_sb = ar.alloc("esel_sb", [8, 128], BF16)
        h_sb = ar.alloc("h_sb", [128, 512], BF16)
        hm_sb = ar.alloc("hm_sb", [128, 512], BF16)
        out_f = [ar.alloc(f"out_f{i}", [128, 512], F32) for i in range(2)]
        assert ar.off <= ar.top, f"phase6 overflow: {ar.off}"

        @block.gpsimd
        def _(g):
            g.wait_ge(sem("cc_ev"), n_wo)
            for f in range(NF):
                inc(g.dma_start(out=wbT_sb[f][:], in_=wbT[f * 128:(f + 1) * 128, :]), "ld_g6", 16)
                inc(g.dma_start(out=aT_sb[f][:], in_=aT[f * 128:(f + 1) * 128, :]), "ld_g6", 16)
                inc(g.dma_start(out=rwT_sb[f][:], in_=rwT[f * 128:(f + 1) * 128, :]), "ld_g6", 16)
            inc(g.dma_start(out=bw_sb[:], in_=bw_er[:]), "ld_g6", 16)
            inc(g.dma_start(out=esel_sb[:], in_=esel[:]), "ld_g6", 16)
        ld_g6_all = cnt("ld_g6")

        @block.sync
        def _(sy):
            sy.wait_ge(sem("cc_ev"), n_wo)
            inc(sy.dma_start(out=bb_sb[:], in_=b_base_cm[:]), "ld_s6", 16)
        ld_s6_all = cnt("ld_s6")

        @block.vector
        def _(v):
            v.wait_ge(sem("ld_g6"), ld_g6_all)
            v.memset(z8[:], 0.0)
            inc(v.tensor_scalar(bw_sb[:], bw_sb[:], 2.0, None, op0=ALU.mult), "bw_sc")

        @block.sync
        def _(sy):
            sy.wait_ge(sem("cc_done"), cc_done_n)
            pid = nc.partition_id(engines=[mybir.EngineType.SP])
            half_off = pid % 2 * HTOK
            for f in range(NF):
                inc(sy.dma_start(out=x1full[f][:],
                                 in_=ccout[f * 128:(f + 1) * 128, bass.ds(half_off, HTOK)]),
                    "ld_cc", 16)
        ld_cc_all = cnt("ld_cc")

        @block.gpsimd
        def _(g):
            g.wait_ge(sem("cc_done"), cc_done_n)
            for f in range(NF):
                i = f % 2
                if f >= 2:
                    g.wait_ge(sem("x1_used"), f - 1)
                inc(g.dma_start(out=sq2[i][:], in_=xTh[f * 128:(f + 1) * 128, :]), f"ld_xh{i}", 16)

        @block.vector
        def _(v):
            v.wait_ge(sem("ld_cc"), ld_cc_all)
            for f in range(NF):
                i = f % 2
                v.wait_ge(sem(f"ld_xh{i}"), 16 * (f // 2 + 1))
                inc(v.tensor_add(x1full[f][:], x1full[f][:], sq2[i][:]), "x1_rdy")

        @block.scalar
        def _(s):
            for f in range(NF):
                i = f % 2
                s.wait_ge(sem("x1_rdy"), f + 1)
                inc(s.activation(sq2[i][:], x1full[f][:], AF.Square), "sq6")

        psum_sum2 = [pbank[i][0:1, :] for i in range(4)]
        psum_sq2 = [pbank[4 + i][0:1, :] for i in range(4)]

        @block.tensor
        def _(t):
            t.wait_ge(sem("cc_ev"), n_wo)
            for f in range(NF):
                i = f % 2
                t.wait_ge(sem("sq6"), f + 1)
                fi, la = f == 0, f == NF - 1
                for w in range(4):
                    t.matmul(psum_sum2[w], ones_col_b[:], x1full[f][:, w * 512:(w + 1) * 512],
                             start=fi, stop=la)
                    mmi = t.matmul(psum_sq2[w], ones_col_b[:], sq2[i][:, w * 512:(w + 1) * 512],
                                   start=fi, stop=la)
                inc(mmi, "x1_used")

        @block.vector
        def _(v):
            v.wait_ge(sem("x1_used"), NF)
            for w in range(4):
                v.tensor_scalar(vm2[0:1, w * 512:(w + 1) * 512], psum_sum2[w], 1.0 / DM,
                                None, op0=ALU.mult)
                ii = v.tensor_scalar(vv2[0:1, w * 512:(w + 1) * 512], psum_sq2[w], 1.0 / DM,
                                     None, op0=ALU.mult)
            fence(v, ii, "fl_v")
            fence(v, v.tensor_mul(vt2[:], vm2[:], vm2[:]), "fl_v")
            inc(v.tensor_sub(vv2[:], vv2[:], vt2[:]), "ln2_var")
            v.wait_ge(sem("rstd2"), 1)
            inc(v.scalar_tensor_tensor(nmr2b[:], vm2[:], -1.0, vv2[:],
                                       op0=ALU.mult, op1=ALU.mult), "ln2_rdy")
            fence(v, v.tensor_copy(r2b[:], vv2[:]), "fl_v")
        ln2_fl = cnt("fl_v")

        @block.scalar
        def _(s):
            s.wait_ge(sem("ln2_var"), 1)
            fence(s, s.activation(vv2[:], vv2[:], AF.Ln, bias=eps1_row[:]), "fl_s")
            inc(s.activation(vv2[:], vv2[:], AF.Exp, scale=-0.5), "rstd2")

        @block.tensor
        def _(t):
            t.wait_ge(sem("ln2_var"), 1)
            for f in range(NF):
                fi, la = f == 0, f == NF - 1
                for w in range(4):
                    t.matmul(pbank[w][0:1, 0:512], ones_col_b[:], wbT_sb[f][:, w * 512:(w + 1) * 512],
                             start=fi, stop=la)
                t.matmul(pbank[4][0:1, 0:128], ones_col_b[:], aT_sb[f][:], start=fi, stop=la)
                mmi = t.matmul(pbank[5][0:1, 0:8], ones_col_b[:], rwT_sb[f][:], start=fi, stop=la)
            inc(mmi, "wbar2_mm")

        @block.vector
        def _(v):
            v.wait_ge(sem("wbar2_mm"), 1)
            for w in range(4):
                v.tensor_copy(wbar2[0:1, w * 512:(w + 1) * 512], pbank[w][0:1, 0:512])
            v.tensor_copy(wbar2[0:1, DM:DM + 128], pbank[4][0:1, 0:128])
            inc(v.tensor_copy(wbar2[0:1, DM + 128:DM + 136], pbank[5][0:1, 0:8]), "wbar2_ev")

        n_fo = 0
        n_rt = 0
        for tw in range(4):
            t0 = tw * 512

            @block.tensor
            def _(t):
                if tw == 0:
                    t.wait_ge(sem("ln2_rdy"), 1)
                    t.wait_ge(sem("wbar2_ev"), 1)
                    t.wait_ge(sem("fl_v"), ln2_fl)
                else:
                    t.wait_ge(sem("r2cp"), tw)
                inc(t.matmul(pbank[5][:, 0:512], ones_row_b[:], r2b[0:1, t0:t0 + 512],
                             start=True, stop=True), "r2mm")

            @block.vector
            def _(v):
                v.wait_ge(sem("r2mm"), tw + 1)
                inc(v.tensor_copy(r2bc[:], pbank[5][:, 0:512]), "r2cp")
                fence(v, v.memset(rl_s2[0:1, 0:1], 0.0), "fl_v")

            @block.vector
            def _(v):
                if tw >= 1:
                    v.wait_ge(sem("out_st0"), 16 * (n_fo // 2))
                    v.wait_ge(sem("out_st1"), 16 * (n_fo // 2))
                for f in range(NF):
                    ii = v.tensor_mul(x1r[f][:], x1full[f][:, t0:t0 + 512], r2bc[:])
                inc(ii, "x1r_rdy")
            x1r_n = cnt("x1r_rdy")

            for tt in range(4):
                @block.tensor
                def _(t):
                    t.wait_ge(sem("x1r_rdy"), x1r_n)
                    if n_rt >= 1:
                        t.wait_ge(sem("rt_ev"), n_rt)
                    for f in range(NF):
                        t.matmul(pbank[6][:, 0:8], x1r[f][:, tt * 128:(tt + 1) * 128],
                                 rwT_sb[f][:], start=(f == 0), stop=False)
                    inc(t.matmul(pbank[6][:, 0:8], nmr2b[0:1, t0 + tt * 128:t0 + (tt + 1) * 128],
                                 wbar2[0:1, DM + 128:DM + 136], start=False, stop=True), "rt_mm")

                @block.vector
                def _(v):
                    v.wait_ge(sem("rt_mm"), n_rt + 1)
                    fence(v, v.tensor_copy(rl_tm[:], pbank[6][:, 0:8]), "fl_v")
                    fence(v, v.reduce_max(rl_s1[:, 0:1], rl_tm[:], axis=AX), "fl_v")
                    fence(v, v.tensor_scalar(rl_s2[:], rl_tm[:], rl_s1[:, 0:1], None,
                                             op0=ALU.is_ge), "fl_v")
                    fence(v, v.tensor_tensor_scan(rl_s1[:], rl_s2[:], z8[:], 0.0,
                                                  op0=ALU.add, op1=ALU.add), "fl_v")
                    fence(v, v.tensor_scalar(rl_s1[:], rl_s1[:], 1.0, None, op0=ALU.is_le), "fl_v")
                    inc(v.tensor_mul(sel8f[:], rl_s2[:], rl_s1[:]), "rt_sel")

                @block.tensor
                def _(t):
                    t.wait_ge(sem("rt_sel"), n_rt + 1)
                    inc(t.transpose(pbank[7][0:8, 0:128], sel8f[:], ident_f[:]), "rt_tr")

                @block.vector
                def _(v):
                    v.wait_ge(sem("rt_tr"), n_rt + 1)
                    inc(v.tensor_copy(sel8fm[0:8, tt * 128:(tt + 1) * 128],
                                      pbank[7][0:8, 0:128]), "rt_ev")
                n_rt += 1

            @block.tensor
            def _(t):
                t.wait_ge(sem("rt_ev"), n_rt)
                for f in range(NF):
                    t.matmul(pbank[6][:, 0:512], aT_sb[f][:], x1r[f][:], start=(f == 0), stop=False)
                inc(t.matmul(pbank[6][:, 0:512], wbar2[0:1, DM:DM + 128],
                             nmr2b[0:1, t0:t0 + 512], start=False, stop=True), "h_mm")
                inc(t.matmul(pbank[7][:, 0:512], esel_sb[:], sel8fm[:], start=True, stop=True),
                    "mask_mm")

            @block.vector
            def _(v):
                v.wait_ge(sem("h_mm"), tw + 1)
                fence(v, v.tensor_copy(h_sb[:], pbank[6][:, 0:512]), "fl_v")
                v.wait_ge(sem("mask_mm"), tw + 1)
                inc(v.tensor_mul(hm_sb[:], h_sb[:], pbank[7][:, 0:512]), "hm_rdy")

            for oc in range(NF):
                @block.tensor
                def _(t):
                    t.wait_ge(sem("hm_rdy"), tw + 1)
                    if tw == 0 and oc == 0:
                        t.wait_ge(sem("bw_sc"), 1)
                    if n_fo >= 2:
                        t.wait_ge(sem("fo_ev"), n_fo - 1)
                    for f in range(NF):
                        t.matmul(pbank[n_fo % 2][:, 0:512], wbT_sb[f][:, oc * 128:(oc + 1) * 128],
                                 x1r[f][:], start=(f == 0), stop=False)
                    t.matmul(pbank[n_fo % 2][:, 0:512], wbar2[0:1, oc * 128:(oc + 1) * 128],
                             nmr2b[0:1, t0:t0 + 512], start=False, stop=False)
                    inc(t.matmul(pbank[n_fo % 2][:, 0:512], bw_sb[:, oc * 128:(oc + 1) * 128],
                                 hm_sb[:], start=False, stop=True), "fo_mm")

                @block.vector
                def _(v):
                    v.wait_ge(sem("fo_mm"), n_fo + 1)
                    if n_fo == 0:
                        v.wait_ge(sem("ld_s6"), ld_s6_all)
                    if n_fo >= 2:
                        v.wait_ge(sem(f"out_st{n_fo % 2}"), 16 * (n_fo // 2))
                    inc(v.scalar_tensor_tensor(out_f[n_fo % 2][:], pbank[n_fo % 2][:, 0:512],
                                               bb_sb[:, oc:oc + 1], x1full[oc][:, t0:t0 + 512],
                                               op0=ALU.add, op1=ALU.add), "fo_ev")

                @block.sync
                def _(sy):
                    sy.wait_ge(sem("fo_ev"), n_fo + 1)
                    inc(sy.dma_start(out=out_fm[oc * 128:(oc + 1) * 128, t0:t0 + 512],
                                     in_=out_f[n_fo % 2][:]), f"out_st{n_fo % 2}", 16)
                n_fo += 1

        @block.sync
        def _(sy):
            sy.wait_ge(sem("out_st0"), cnt("out_st0"))
            sy.wait_ge(sem("out_st1"), cnt("out_st1"))

    lower_extended_insts(nc)
    return nc, dbg_outs


def make_in_maps(inputs):
    x = np.ascontiguousarray(inputs["L0_attn_hidden_states"], dtype=np.float32)
    cos = np.ascontiguousarray(np.asarray(inputs["L0_attn_cos"], dtype=np.float32)[0, :S, 0, :])
    sin = np.ascontiguousarray(np.asarray(inputs["L0_attn_sin"], dtype=np.float32)[0, :S, 0, :])
    wq = np.asarray(inputs["L0_attn_wq"], dtype=np.float32)
    wk = np.asarray(inputs["L0_attn_wk"], dtype=np.float32)
    wv = np.asarray(inputs["L0_attn_wv"], dtype=np.float32)
    wo = np.asarray(inputs["L0_attn_wo"], dtype=np.float32)
    qnw = np.asarray(inputs["L0_attn_q_norm_w"], dtype=np.float32).reshape(-1)
    knw = np.asarray(inputs["L0_attn_k_norm_w"], dtype=np.float32).reshape(-1)
    w1 = np.asarray(inputs["L0_attn_imp_w1"], dtype=np.float32)
    b1 = np.asarray(inputs["L0_attn_imp_b1"], dtype=np.float32).reshape(1, 64)
    w2 = np.asarray(inputs["L0_attn_imp_w2"], dtype=np.float32).reshape(1, 64)
    b2 = np.asarray(inputs["L0_attn_imp_b2"], dtype=np.float32).reshape(1, 1)
    ka = np.asarray(inputs["L0_attn_kron_a"], dtype=np.float32)
    kb = np.asarray(inputs["L0_attn_kron_b"], dtype=np.float32)
    ss = np.asarray(inputs["L0_attn_sketch_scale"], dtype=np.float32).reshape(1, 1)
    Wb = np.asarray(inputs["L0_ffn_W_base"], dtype=np.float32)
    bb = np.asarray(inputs["L0_ffn_b_base"], dtype=np.float32).reshape(-1)
    rw = np.asarray(inputs["L0_ffn_router_W"], dtype=np.float32)
    A = np.asarray(inputs["L0_ffn_A"], dtype=np.float32)
    Bw = np.asarray(inputs["L0_ffn_Bw"], dtype=np.float32)

    ident = np.eye(128, dtype=np.float32)
    iota = np.arange(S, dtype=np.uint32).reshape(TC, 128).T.copy()
    sela_m = np.zeros((32, 5 * 128), np.float32)
    selc_m = np.zeros((20, 5 * 128), np.float32)
    for j in range(5):
        for p in range(128):
            sv = 128 * j + p
            sela_m[sv // 20, j * 128 + p] = 1.0
            selc_m[sv % 20, j * 128 + p] = 1.0
    kaT_row = np.zeros((1, 320), np.float32)
    for jp in range(16):
        kaT_row[0, jp * 20:(jp + 1) * 20] = ka[:, jp]
    es = np.zeros((8, 128), np.float32)
    for e in range(8):
        es[e, e * 16:(e + 1) * 16] = 1.0

    maps = []
    for core in range(8):
        b, hf = core // 2, core % 2
        m = {
            "xT": np.ascontiguousarray(x[b].T),
            "xTh": np.ascontiguousarray(x[b].T[:, hf * HTOK:(hf + 1) * HTOK]),
            "cos_tm": cos, "sin_tm": sin,
            "wq_c": np.ascontiguousarray(wq[:, hf * NQ:(hf + 1) * NQ]),
            "wk_c": np.ascontiguousarray(wk[:, hf * NKVD:(hf + 1) * NKVD]),
            "wv_c": np.ascontiguousarray(wv[:, hf * NKVD:(hf + 1) * NKVD]),
            "imp_w1": w1, "imp_w2r": w2, "imp_b1r": b1, "imp_b2": b2,
            "qnw_col": qnw.reshape(128, 1), "knw_col": knw.reshape(128, 1),
            "qnw_row": qnw.reshape(1, 128), "knw_row": knw.reshape(1, 128),
            "kbT": np.ascontiguousarray(kb.T), "kb_n": kb, "ka_n": ka,
            "kaT_row": kaT_row, "sela": sela_m, "selc": selc_m,
            "wo_c": np.ascontiguousarray(wo[hf * NQ:(hf + 1) * NQ, :]),
            "wbT": np.ascontiguousarray(Wb.T),
            "b_base_cm": np.ascontiguousarray(bb.reshape(NF, 128).T),
            "rwT": np.ascontiguousarray(rw.T),
            "aT": np.ascontiguousarray(A.reshape(128, DM).T),
            "bw_er": np.ascontiguousarray(Bw.transpose(0, 2, 1).reshape(128, DM)),
            "esel": es, "ident": ident, "iota_cm": iota, "ss_in": ss,
        }
        maps.append({k: np.ascontiguousarray(v) for k, v in m.items()})
    return maps


_CACHE = {}


def kernel(**inputs):
    if "nc" not in _CACHE:
        _CACHE["nc"] = build_program(phases=7)[0]
    nc = _CACHE["nc"]
    maps = make_in_maps(inputs)
    res = run_bass_kernel_spmd(nc, maps, core_ids=list(range(8)))
    out = np.empty((B, S, DM), np.float32)
    for core in range(8):
        b, hf = core // 2, core % 2
        out[b, hf * HTOK:(hf + 1) * HTOK, :] = np.asarray(res.results[core]["out_fm"]).T
    return out
